# version 07
# baseline (speedup 1.0000x reference)
import contextlib
import numpy as np
import concourse.bass as bass
import concourse.mybir as mybir
from concourse.bass_utils import run_bass_kernel_spmd

F32 = mybir.dt.float32
BF16 = mybir.dt.bfloat16
U8 = mybir.dt.uint8
I32 = mybir.dt.int32
FP8 = mybir.dt.float8e4
AF = mybir.ActivationFunctionType
ALU = mybir.AluOpType

NC = 8
B, L, DM = 2, 8192, 2048
NT = B * L
DI, DS, DR, KC = 4096, 16, 128, 4
IF = 8192
DOWN = DI // NC
TOK = NT // NC
STR = 256
NCHUNK = 8
CT = 2048
TT = 256
NTAU = CT // TT
EPS = 1e-5

S_8TH = 1.5 * 2.0 ** 20
S_HALF = 1.5 * 2.0 ** 22
S_ONE = 1.5 * 2.0 ** 23
S_TWO = 1.5 * 2.0 ** 24

_DEBUG_TAPS = ()
_LAST_TAPS = {}


class Sem:
    def __init__(self, nc, es, name):
        self.h = es.enter_context(nc.semaphore(name))
        self.n = 0

    def inc(self, instr, amt=1):
        instr.then_inc(self.h, amt)
        self.n += amt
        return self.n


def build(taps=()):
    nc = bass.Bass("TRN2", target_bir_lowering=False,
                   detect_race_conditions=False)
    es = contextlib.ExitStack()

    def din(name, shape, dt=F32):
        return nc.dram_tensor(name, shape, dt, kind="ExternalInput").ap()

    def dout(name, shape, dt=F32):
        return nc.dram_tensor(name, shape, dt, kind="ExternalOutput").ap()

    def dram(name, shape, dt=F32):
        return nc.dram_tensor(name, shape, dt)

    x_in = din("x_own", [TOK, DM])
    ident_in = din("ident", [128, 128])
    rmsw_in = din("rmsw", [1, DM])
    w_in_sl = din("w_in_sl", [DM, 1024])
    convw_in = din("convw", [128, 16])
    convb_in = din("convb", [128, 4])
    wx_in = din("wx_sl", [DOWN, 160])
    wdt_in = din("wdt_sl", [DR, DOWN])
    dtb_in = din("dtb", [128, 4])
    alog_in = din("alog", [128, 64])
    dp_in = din("dp", [128, 4])
    wout_in = din("wout_sl", [DOWN, DM])
    wg_in = din("wg_sl", [IF // NC, DM])
    wu_in = din("wu_sl", [IF // NC, DM])
    wd_in = din("wd_sl", [DM // NC, IF])

    out_own = dout("out_own", [TOK, DM])

    hn_in = dram("hn_in", [DM, TOK], BF16)
    hn_all = dram("hn_all", [NC * DM, TOK], BF16)
    ln1_spill = dram("ln1_spill", [TOK, DM], F32)
    x2_spill = dram("x2_spill", [TOK, DM], F32)
    ar_in = [dram(f"ar_in{k}", [160, CT], F32) for k in range(NCHUNK)]
    ar_out = [dram(f"ar_out{k}", [160, CT], F32) for k in range(NCHUNK)]
    rs_in = [dram(f"rs_in{k}", [CT, DM], F32) for k in range(NCHUNK)]
    rs_out = [dram(f"rs_out{k}", [STR, DM], F32) for k in range(NCHUNK)]
    wgq_in = dram("wgq_in", [DM, 1024], BF16)
    wgq_all = dram("wgq_all", [NC * DM, 1024], BF16)
    wuq_in = dram("wuq_in", [DM, 1024], BF16)
    wuq_all = dram("wuq_all", [NC * DM, 1024], BF16)
    wdq_in = dram("wdq_in", [IF, DM // NC], BF16)
    wdq_all = dram("wdq_all", [NC * IF, DM // NC], BF16)
    ikt_spill = dram("ikt_spill", [IF, TOK], BF16)
    uc_spill = dram("uc_spill", [NCHUNK * 512, CT], BF16)
    sz_spill = dram("sz_spill", [NCHUNK * 512, CT], BF16)

    tap_bufs = {}
    for tname, tshape, tdt in taps:
        tap_bufs[tname] = dout("tap_" + tname, list(tshape), tdt)

    L_ = Sem(nc, es, "Lq")
    ST = Sem(nc, es, "STq")
    V = Sem(nc, es, "Vm")
    SC = Sem(nc, es, "SCm")
    PE = Sem(nc, es, "PEm")
    G = Sem(nc, es, "Gm")
    c_aghn = Sem(nc, es, "c_aghn")
    c_agwg = Sem(nc, es, "c_agwg")
    c_agwu = Sem(nc, es, "c_agwu")
    c_agwd = Sem(nc, es, "c_agwd")
    c_ar = [Sem(nc, es, f"c_ar{k}") for k in range(NCHUNK)]
    c_rs = [Sem(nc, es, f"c_rs{k}") for k in range(NCHUNK)]

    def sbuf(name, shape, dt, stack=None):
        return (stack or es).enter_context(nc.sbuf_tensor(name, shape, dt))

    def psum(name, shape, dt=F32):
        return es.enter_context(nc.psum_tensor(name, shape, dt))

    w_x_bf = sbuf("w_x_bf", [128, 4 * 160], BF16)
    w_dt_bf = sbuf("w_dt_bf", [128, DOWN], BF16)
    A_sb = sbuf("A_sb", [128, 64], F32)
    negA_sb = sbuf("negA_sb", [128, 64], F32)
    convw_sb = sbuf("convw_sb", [128, 16], F32)
    convb_sb = sbuf("convb_sb", [128, 4], F32)
    dtb_sb = sbuf("dtb_sb", [128, 4], F32)
    dp_sb = sbuf("dp_sb", [128, 4], F32)
    onesbf = sbuf("onesbf", [1, 128], BF16)
    ones32 = sbuf("ones32", [1, 128], F32)
    ident_bf = sbuf("ident_bf", [128, 128], BF16)
    junkA = sbuf("junkA", [128, 8], F32)
    junkB = sbuf("junkB", [128, 8], F32)
    consts = sbuf("consts", [128, 4], F32)

    pA = [psum("pA0", [128, 512]), psum("pA1", [128, 512])]
    pB = [psum("pB0", [128, 512]), psum("pB1", [128, 512])]
    pT = [psum("pT0", [128, 128], BF16), psum("pT1", [128, 128], BF16)]
    pTB = psum("pTB", [128, 1024], BF16)
    pR = psum("pR", [128, 512])

    block = es.enter_context(nc.Block())
    SY, VE, SE, TE, GE = nc.sync, nc.vector, nc.scalar, nc.tensor, nc.gpsimd

    def spacer():
        return VE.tensor_copy(junkA[:, :], junkB[:, :])

    def barrier():
        vals = [(V.h, V.n), (SC.h, SC.n), (PE.h, PE.n), (G.h, G.n),
                (L_.h, L_.n), (ST.h, ST.n)]
        for e in (SY, VE, SE, TE, GE):
            for h, n in vals:
                if n > 0:
                    e.wait_ge(h, n)

    def ln_stats(src, bns, mv):
        for j in range(4):
            VE.bn_stats(bns[:, 6 * j:6 * j + 6], src[:, 512 * j:512 * (j + 1)])
        spacer()
        VE.bn_aggr(mv[:, 0:2], bns[:, 0:24])
        spacer()
        return V.inc(spacer())

    def ln_coefs(mv, cf, vwait):
        SE.wait_ge(V.h, vwait)
        SE.activation(cf[:, 2:3], mv[:, 1:2], AF.Ln, bias=float(EPS))
        sc = SC.inc(SE.activation(cf[:, 0:1], cf[:, 2:3], AF.Exp, scale=-0.5))
        VE.wait_ge(SC.h, sc)
        VE.scalar_tensor_tensor(cf[:, 1:2], mv[:, 0:1], -1.0, cf[:, 0:1],
                                ALU.mult, ALU.mult)
        spacer()
        return V.inc(spacer())

    def _block_scales(s, src_am_ap, nb):
        VE.tensor_scalar(s["ie"][:, 0:nb], src_am_ap.bitcast(I32), 23, None,
                         ALU.logical_shift_right, ALU.bypass)
        spacer()
        VE.tensor_scalar(s["ie"][:, 0:nb], s["ie"][:, 0:nb], 129, None,
                         ALU.subtract, ALU.bypass)
        spacer()
        VE.tensor_scalar(s["ie"][:, 0:nb], s["ie"][:, 0:nb], -127, 127,
                         ALU.max, ALU.min)
        spacer()
        VE.tensor_scalar(s["ir"][:, 0:nb], s["ie"][:, 0:nb], -1, 127,
                         ALU.mult, ALU.add)
        VE.tensor_scalar(s["is"][:, 0:nb], s["ie"][:, 0:nb], 127, None,
                         ALU.add, ALU.bypass)
        spacer()
        VE.tensor_scalar(s["ir"][:, 0:nb], s["ir"][:, 0:nb], 23, None,
                         ALU.logical_shift_left, ALU.bypass)
        VE.tensor_scalar(s["is"][:, 0:nb], s["is"][:, 0:nb], 23, None,
                         ALU.logical_shift_left, ALU.bypass)
        spacer()

    def quant_w(wt, qbf, s, Fw):
        nb = Fw // 32
        VE.tensor_reduce(s["am"][:, 0:nb],
                         wt[:, 0:Fw].rearrange("p (b e) -> p b e", e=32),
                         axis=mybir.AxisListType.X, op=ALU.max,
                         apply_absolute_value=True)
        spacer()
        _block_scales(s, s["am"][:, 0:nb], nb)
        VE.tensor_tensor(s["v"][:, 0:Fw].rearrange("p (b e) -> p b e", e=32),
                         wt[:, 0:Fw].rearrange("p (b e) -> p b e", e=32),
                         s["ir"][:, 0:nb].bitcast(F32).unsqueeze(2)
                         .broadcast_to([128, nb, 32]), ALU.mult)
        VE.tensor_scalar(s["q"][:, 0:Fw].bitcast(I32), s["v"][:, 0:Fw].bitcast(I32),
                         0x7FFFFFFF, None, ALU.bitwise_and, ALU.bypass)
        VE.tensor_scalar(s["r05"][:, 0:Fw], s["v"][:, 0:Fw], S_HALF, S_HALF,
                         ALU.add, ALU.subtract)
        VE.tensor_scalar(s["r1"][:, 0:Fw], s["v"][:, 0:Fw], S_ONE, S_ONE,
                         ALU.add, ALU.subtract)
        VE.tensor_scalar(s["r2c"][:, 0:Fw], s["v"][:, 0:Fw], S_TWO, S_TWO,
                         ALU.add, ALU.subtract)
        VE.tensor_scalar(s["m2"][:, 0:Fw], s["q"][:, 0:Fw].bitcast(I32), 0x40000000,
                         None, ALU.is_lt, ALU.bypass)
        VE.tensor_scalar(s["m4"][:, 0:Fw], s["q"][:, 0:Fw].bitcast(I32), 0x40800000,
                         None, ALU.is_lt, ALU.bypass)
        VE.tensor_copy(s["q"][:, 0:Fw], s["r2c"][:, 0:Fw])
        VE.copy_predicated(s["q"][:, 0:Fw], s["m4"][:, 0:Fw], s["r1"][:, 0:Fw])
        VE.copy_predicated(s["q"][:, 0:Fw], s["m2"][:, 0:Fw], s["r05"][:, 0:Fw])
        VE.tensor_scalar(s["q"][:, 0:Fw], s["q"][:, 0:Fw], 6.0, -6.0, ALU.min, ALU.max)
        VE.tensor_tensor(qbf[:, 0:Fw].rearrange("p (b e) -> p b e", e=32),
                         s["q"][:, 0:Fw].rearrange("p (b e) -> p b e", e=32),
                         s["is"][:, 0:nb].bitcast(F32).unsqueeze(2)
                         .broadcast_to([128, nb, 32]), ALU.mult)
        return V.inc(spacer())

    def quant_a(at, qbf_ap, s, Fw):
        nb = Fw // 32
        VE.tensor_reduce(s["am"][:, 0:nb],
                         at[:, 0:Fw].rearrange("p (b e) -> p b e", e=32),
                         axis=mybir.AxisListType.X, op=ALU.max,
                         apply_absolute_value=True)
        spacer()
        _block_scales(s, s["am"][:, 0:nb], nb)
        VE.tensor_tensor(s["v"][:, 0:Fw].rearrange("p (b e) -> p b e", e=32),
                         at[:, 0:Fw].rearrange("p (b e) -> p b e", e=32),
                         s["ir"][:, 0:nb].bitcast(F32).unsqueeze(2)
                         .broadcast_to([128, nb, 32]), ALU.mult)
        VE.tensor_copy(s["f8"][:, 0:Fw], s["v"][:, 0:Fw])
        VE.tensor_scalar(s["q"][:, 0:Fw].bitcast(I32), s["v"][:, 0:Fw].bitcast(I32),
                         0x7FFFFFFF, None, ALU.bitwise_and, ALU.bypass)
        VE.tensor_scalar(s["r1"][:, 0:Fw], s["v"][:, 0:Fw], S_8TH, S_8TH,
                         ALU.add, ALU.subtract)
        VE.tensor_scalar(s["m2"][:, 0:Fw], s["q"][:, 0:Fw].bitcast(I32), 0x3F800000,
                         None, ALU.is_lt, ALU.bypass)
        VE.tensor_copy(s["q"][:, 0:Fw], s["f8"][:, 0:Fw])
        VE.copy_predicated(s["q"][:, 0:Fw], s["m2"][:, 0:Fw], s["r1"][:, 0:Fw])
        VE.tensor_scalar(s["q"][:, 0:Fw], s["q"][:, 0:Fw], 7.5, -7.5, ALU.min, ALU.max)
        VE.tensor_tensor(qbf_ap.rearrange("p (b e) -> p b e", e=32),
                         s["q"][:, 0:Fw].rearrange("p (b e) -> p b e", e=32),
                         s["is"][:, 0:nb].bitcast(F32).unsqueeze(2)
                         .broadcast_to([128, nb, 32]), ALU.mult)
        return V.inc(spacer())


    def transpose8(src_ap_fn, n_tiles, evac_fn):
        for b0 in range(0, n_tiles, 8):
            nb_ = min(8, n_tiles - b0)
            TE.wait_ge(SC.h, SC.n)
            TE.wait_ge(V.h, V.n)
            for j in range(nb_):
                mmt = TE.transpose(pTB[:, 128 * j:128 * (j + 1)], src_ap_fn(b0 + j),
                                   ident_bf[:, :])
            pe = PE.inc(mmt)
            SE.wait_ge(PE.h, pe)
            evac_fn(b0 // 8, pTB[:, 0:128 * nb_])

    def tapw(name, src_ap):
        if name in tap_bufs:
            SE.wait_ge(V.h, V.n)
            SE.wait_ge(SC.h, SC.n)
            return ST.inc(SE.dma_start(out=tap_bufs[name], in_=src_ap), 16)
        return None

    pwo = contextlib.ExitStack()
    pwi = contextlib.ExitStack()
    w_out_bf = sbuf("w_out_bf", [128, 4 * DM], BF16, pwo)
    w_in_bf = sbuf("w_in_bf", [128, 16 * 1024], BF16, pwi)

    p0 = contextlib.ExitStack()
    ident_f = sbuf("ident_f", [128, 128], F32, p0)
    rmsw_row = sbuf("rmsw_row", [1, DM], F32, p0)
    rmsw_bc = sbuf("rmsw_bc", [128, DM], F32, p0)

    nL = L_.inc(SY.dma_start(out=ident_f[:, :], in_=ident_in), 16)
    nL = L_.inc(SY.dma_start(out=convw_sb[:, :], in_=convw_in), 16)
    nL = L_.inc(SY.dma_start(out=convb_sb[:, :], in_=convb_in), 16)
    nL = L_.inc(SY.dma_start(out=dtb_sb[:, :], in_=dtb_in), 16)
    nL = L_.inc(SY.dma_start(out=dp_sb[:, :], in_=dp_in), 16)
    nL = L_.inc(SY.dma_start(out=rmsw_row[:, :], in_=rmsw_in), 16)
    nc.const_aps.register(F32, float(EPS), consts[:, 0:1])
    nc.const_aps.register(F32, 1.0, consts[:, 1:2])
    VE.wait_ge(L_.h, nL)
    VE.memset(consts[:, 0:1], float(EPS))
    VE.memset(consts[:, 1:2], 1.0)
    VE.memset(onesbf[:, :], 1.0)
    VE.memset(ones32[:, :], 1.0)
    VE.memset(junkB[:, :], 0.0)
    vE = V.inc(VE.tensor_copy(ident_bf[:, :], ident_f[:, :]))
    SY.wait_ge(V.h, vE)
    nLA = L_.inc(SY.dma_start(out=ident_f[:, 0:64], in_=alog_in), 16)
    SE.wait_ge(L_.h, nLA)
    SC.inc(SE.activation(A_sb[:, :], ident_f[:, 0:64], AF.Exp))
    VE.wait_ge(SC.h, SC.n)
    V.inc(VE.tensor_scalar(negA_sb[:, :], A_sb[:, :], -1.0, None, ALU.mult, ALU.bypass))

    TE.wait_ge(V.h, vE)
    for j in range(4):
        pe = PE.inc(TE.matmul(pR[:, :], ones32[:, :], rmsw_row[:, 512 * j:512 * (j + 1)],
                              start=True, stop=True))
        SE.wait_ge(PE.h, pe)
        scR = SC.inc(SE.copy(rmsw_bc[:, 512 * j:512 * (j + 1)], pR[:, :]))
        TE.wait_ge(SC.h, scR)

    xt = sbuf("xt", [128, DM], F32, p0)
    ln1t = sbuf("ln1t", [128, DM], F32, p0)
    hnt = sbuf("hnt", [128, DM], BF16, p0)
    hnev = sbuf("hnev", [128, 1024], BF16, p0)
    bns = sbuf("bns", [128, 24], F32, p0)
    mv = sbuf("mv", [128, 4], F32, p0)
    cf = sbuf("cf", [128, 4], F32, p0)

    st_hn = 0
    for i in range(16):
        SY.wait_ge(V.h, V.n)
        SY.wait_ge(SC.h, SC.n)
        nL = L_.inc(SY.dma_start(out=xt[:, :],
                                 in_=x_in.rearrange("(i p) f -> i p f", p=128)[i]), 16)
        VE.wait_ge(L_.h, nL)
        vw = ln_stats(xt, bns, mv)
        vw = ln_coefs(mv, cf, vw)
        VE.wait_ge(ST.h, ST.n)
        VE.tensor_scalar(ln1t[:, :], xt[:, :], cf[:, 0:1], cf[:, 1:2],
                         ALU.mult, ALU.add)
        spacer()
        vc1 = V.inc(spacer())
        vw2 = ln_stats(ln1t, bns, mv)
        SE.wait_ge(V.h, vw2)
        SE.activation(cf[:, 2:3], mv[:, 1:2], AF.Ln, bias=float(EPS))
        scg = SC.inc(SE.activation(cf[:, 2:3], cf[:, 2:3], AF.Exp, scale=-0.5))
        VE.wait_ge(SC.h, scg)
        VE.wait_ge(PE.h, PE.n)
        VE.scalar_tensor_tensor(hnt[:, :], ln1t[:, :], cf[:, 2:3],
                                rmsw_bc[:, :], ALU.mult, ALU.mult)
        spacer()
        vh = V.inc(spacer())
        SE.wait_ge(V.h, vc1)
        st_l = ST.inc(SE.dma_start(out=ln1_spill.rearrange("(i p) f -> i p f", p=128)[i],
                                   in_=ln1t[:, :]), 16)
        if i == 0:
            tapw("ln1", ln1t[:, :])
            tapw("hnt", hnt[:, :])
        TE.wait_ge(V.h, vh)
        for b in range(2):
            TE.wait_ge(SC.h, SC.n)
            for j in range(8):
                mmt = TE.transpose(pTB[:, 128 * j:128 * (j + 1)],
                                   hnt[:, 128 * (8 * b + j):128 * (8 * b + j + 1)],
                                   ident_bf[:, :])
            pe = PE.inc(mmt)
            SE.wait_ge(PE.h, pe)
            SE.wait_ge(ST.h, st_hn)
            SC.inc(SE.copy(hnev[:, :], pTB[:, :]))
            SE.copy(junkB[:, 0:4], junkA[:, 0:4])
            st_hn = ST.inc(SE.dma_start(
                out=hn_in.rearrange("(a p) t -> a p t", p=128)[8 * b:8 * b + 8][
                    :, :, 128 * i:128 * (i + 1)].transpose([1, 0, 2]),
                in_=hnev[:, :]), 16)

    GE.wait_ge(ST.h, st_hn)
    c_aghn.inc(GE.collective_compute(
        "AllGather", ALU.bypass, replica_groups=[list(range(NC))],
        ins=[hn_in.ap().opt()], outs=[hn_all.ap().opt()]))

    wstage = sbuf("wstage", [128, DM], F32, p0)

    def load_cast(dst_ap, src_ap, width):
        SY.wait_ge(SC.h, SC.n)
        nL = L_.inc(SY.dma_start(out=wstage[:, 0:width], in_=src_ap), 16)
        SE.wait_ge(L_.h, nL)
        return SC.inc(SE.copy(dst_ap, wstage[:, 0:width]))

    for kk in range(16):
        load_cast(w_in_bf[:, 1024 * kk:1024 * (kk + 1)],
                  w_in_sl.rearrange("(a p) c -> a p c", p=128)[kk], 1024)
    for kd in range(4):
        load_cast(w_out_bf[:, DM * kd:DM * (kd + 1)],
                  wout_in.rearrange("(a p) c -> a p c", p=128)[kd], DM)
    for kd in range(4):
        load_cast(w_x_bf[:, 160 * kd:160 * (kd + 1)],
                  wx_in.rearrange("(a p) c -> a p c", p=128)[kd], 160)
    load_cast(w_dt_bf[:, :], wdt_in, DOWN)

    barrier()
    p0.close()

    pm1 = contextlib.ExitStack()
    rhs_sb = sbuf("rhs_sb", [128, 16 * 512], BF16, pm1)
    u_ext = [sbuf(f"u_ext{m}", [128, 515], BF16, pm1) for m in range(4)]
    halo = [sbuf(f"halo{m}", [128, 3], BF16, pm1) for m in range(4)]
    cacc = [sbuf("cacc0", [128, 512], F32, pm1), sbuf("cacc1", [128, 512], F32, pm1)]
    u_c = [sbuf(f"u_c{m}", [128, CT], BF16, pm1) for m in range(4)]
    siluz = [sbuf(f"siluz{m}", [128, CT], BF16, pm1) for m in range(4)]
    dtev = sbuf("dtev", [128, 512], F32, pm1)

    st_spill = 0
    sc_bank = [SC.n, SC.n]
    for k in range(NCHUNK):
        if k % 4 == 0:
            vhz = V.n
            for m in range(4):
                vhz = V.inc(VE.memset(halo[m][:, :], 0.0))
        for n in range(4):
            SY.wait_ge(PE.h, PE.n)
            SY.wait_ge(c_aghn.h, 1)
            nL = L_.n
            for kk in range(16):
                for half in range(2):
                    r = 2 * n + half
                    nL = L_.inc(SY.dma_start(
                        out=rhs_sb[:, 512 * kk + 256 * half:512 * kk + 256 * (half + 1)],
                        in_=hn_all.rearrange("(r a p) t -> r a p t", r=NC, p=128)[r][kk][
                            :, STR * k:STR * (k + 1)]), 16)
            TE.wait_ge(L_.h, nL)
            for m in range(8):
                pa = pA[m % 2]
                TE.wait_ge(SC.h, sc_bank[m % 2])
                for kk in range(16):
                    mm = TE.matmul(
                        pa[:, :],
                        w_in_bf[:, 1024 * kk + 128 * m:1024 * kk + 128 * (m + 1)],
                        rhs_sb[:, 512 * kk:512 * (kk + 1)],
                        start=(kk == 0), stop=(kk == 15))
                pe = PE.inc(mm)
                SE.wait_ge(PE.h, pe)
                if m < 4:
                    SE.wait_ge(V.h, V.n)
                    if n == 0:
                        SE.wait_ge(ST.h, st_spill)
                    sc1 = SC.inc(SE.copy(u_ext[m][:, 3:515], pa[:, :]))
                    sc_bank[m % 2] = sc1
                    VE.wait_ge(SC.h, sc1)
                    VE.tensor_copy(u_ext[m][:, 0:3], halo[m][:, :])
                    VE.tensor_scalar(cacc[0][:, :], u_ext[m][:, 0:512],
                                     convw_sb[:, 4 * m:4 * m + 1], None,
                                     ALU.mult, ALU.bypass)
                    for j in range(1, 4):
                        srcb, dstb = (cacc[0], cacc[1]) if j % 2 == 1 else (cacc[1], cacc[0])
                        VE.scalar_tensor_tensor(dstb[:, :], u_ext[m][:, j:j + 512],
                                                convw_sb[:, 4 * m + j:4 * m + j + 1],
                                                srcb[:, :], ALU.mult, ALU.add)
                    vcv = V.inc(VE.tensor_copy(halo[m][:, :], u_ext[m][:, 512:515]))
                    SE.wait_ge(V.h, vcv)
                    SC.inc(SE.activation(u_c[m][:, 512 * n:512 * (n + 1)], cacc[1][:, :],
                                         AF.Silu, bias=convb_sb[:, m:m + 1]))
                else:
                    if n == 0:
                        SE.wait_ge(ST.h, st_spill)
                    sc_bank[m % 2] = SC.inc(SE.activation(
                        siluz[m - 4][:, 512 * n:512 * (n + 1)], pa[:, :], AF.Silu))
            TE.wait_ge(SC.h, SC.n)
            for kd in range(4):
                TE.matmul(pB[0][:, :], w_x_bf[:, 160 * kd:160 * kd + 128],
                          u_c[kd][:, 512 * n:512 * (n + 1)],
                          start=(kd == 0), stop=(kd == 3))
                mm2 = TE.matmul(pB[1][0:32, :], w_x_bf[:, 160 * kd + 128:160 * (kd + 1)],
                                u_c[kd][:, 512 * n:512 * (n + 1)],
                                start=(kd == 0), stop=(kd == 3))
            pe = PE.inc(mm2)
            SE.wait_ge(PE.h, pe)
            SC.inc(SE.copy(dtev[:, :], pB[0][:, :]))
            st_dt = ST.inc(SE.dma_start(out=ar_in[k][0:128, 512 * n:512 * (n + 1)],
                                        in_=dtev[:, :]), 16)
            SE.wait_ge(ST.h, st_dt)
            SC.inc(SE.copy(dtev[0:32, :], pB[1][0:32, :]))
            st1 = ST.inc(SE.dma_start(out=ar_in[k][128:160, 512 * n:512 * (n + 1)],
                                      in_=dtev[0:32, :]), 16)
            SE.wait_ge(ST.h, st1)
        if k == 0:
            tapw("uc0", u_c[0][:, :])
            tapw("sz0", siluz[0][:, :])
        SE.wait_ge(SC.h, SC.n)
        for m in range(4):
            ST.inc(SE.dma_start(
                out=uc_spill.rearrange("(k m p) t -> k m p t", k=NCHUNK, p=128)[k][m],
                in_=u_c[m][:, :]), 16)
            st_spill = ST.inc(SE.dma_start(
                out=sz_spill.rearrange("(k m p) t -> k m p t", k=NCHUNK, p=128)[k][m],
                in_=siluz[m][:, :]), 16)
        GE.wait_ge(ST.h, st_spill)
        c_ar[k].inc(GE.collective_compute(
            "AllReduce", ALU.add, replica_groups=[list(range(NC))],
            ins=[ar_in[k].ap().opt()], outs=[ar_out[k].ap().opt()]))

    pq = contextlib.ExitStack()
    wstage2 = sbuf("wstage2", [128, DM], F32, pq)
    qs = {
        "v": sbuf("q_v", [128, DM], F32, pq),
        "r05": sbuf("q_r05", [128, DM], F32, pq),
        "r1": sbuf("q_r1", [128, DM], F32, pq),
        "r2c": sbuf("q_r2c", [128, DM], F32, pq),
        "q": sbuf("q_q", [128, DM], F32, pq),
        "f8": sbuf("q_f8", [128, DM], FP8, pq),
        "m2": sbuf("q_m2", [128, DM], U8, pq),
        "m4": sbuf("q_m4", [128, DM], U8, pq),
        "am": sbuf("q_am", [128, 64], F32, pq),
        "ie": sbuf("q_ie", [128, 64], I32, pq),
        "ir": sbuf("q_ir", [128, 64], I32, pq),
        "is": sbuf("q_is", [128, 64], I32, pq),
    }
    qbf = sbuf("qbf", [128, DM], BF16, pq)
    qev = sbuf("qev", [128, 1024], BF16, pq)

    def q_t_store(src_tile_ap, n_kf, dst, col0):
        last = ST.n
        for b in range(n_kf // 8):
            TE.wait_ge(SC.h, SC.n)
            for j in range(8):
                kf = 8 * b + j
                mmt = TE.transpose(pTB[:, 128 * j:128 * (j + 1)],
                                   qbf[:, 128 * kf:128 * (kf + 1)], ident_bf[:, :])
            pe = PE.inc(mmt)
            SE.wait_ge(PE.h, pe)
            SE.wait_ge(ST.h, last)
            SC.inc(SE.copy(qev[:, :], pTB[:, :]))
            SE.copy(junkB[:, 0:4], junkA[:, 0:4])
            last = ST.inc(SE.dma_start(
                out=dst.rearrange("(a p) c -> a p c", p=128)[8 * b:8 * b + 8][
                    :, :, col0:col0 + 128].transpose([1, 0, 2]),
                in_=qev[:, :]), 16)
        return last

    for wt_dram, dst in ((wg_in, wgq_in), (wu_in, wuq_in)):
        for it in range(8):
            SY.wait_ge(V.h, V.n)
            nL = L_.inc(SY.dma_start(
                out=wstage2[:, :], in_=wt_dram.rearrange("(a p) c -> a p c", p=128)[it]), 16)
            VE.wait_ge(L_.h, nL)
            VE.wait_ge(PE.h, PE.n)
            vqd = quant_w(wstage2, qbf, qs, DM)
            TE.wait_ge(V.h, vqd)
            stq = q_t_store(0, 16, dst, 128 * it)
        GE.wait_ge(ST.h, stq)
        sem = c_agwg if wt_dram is wg_in else c_agwu
        sem.inc(GE.collective_compute(
            "AllGather", ALU.bypass, replica_groups=[list(range(NC))],
            ins=[(wgq_in if wt_dram is wg_in else wuq_in).ap().opt()],
            outs=[(wgq_all if wt_dram is wg_in else wuq_all).ap().opt()]))

    for it in range(2):
        for sub in range(4):
            SY.wait_ge(V.h, V.n)
            nL = L_.inc(SY.dma_start(
                out=wstage2[:, :],
                in_=wd_in.rearrange("(a p) (s c) -> a s p c", p=128, c=DM)[it][sub]), 16)
            VE.wait_ge(L_.h, nL)
            VE.wait_ge(PE.h, PE.n)
            vqd = quant_w(wstage2, qbf, qs, DM)
            TE.wait_ge(V.h, vqd)
            last = ST.n
            for b in range(2):
                TE.wait_ge(SC.h, SC.n)
                for j in range(8):
                    kf = 8 * b + j
                    mmt = TE.transpose(pTB[:, 128 * j:128 * (j + 1)],
                                       qbf[:, 128 * kf:128 * (kf + 1)], ident_bf[:, :])
                pe = PE.inc(mmt)
                SE.wait_ge(PE.h, pe)
                SE.wait_ge(ST.h, last)
                SC.inc(SE.copy(qev[:, :], pTB[:, :]))
                SE.copy(junkB[:, 0:4], junkA[:, 0:4])
                last = ST.inc(SE.dma_start(
                    out=wdq_in.rearrange("(a p) c -> a p c", p=128)[
                        sub * 16 + 8 * b:sub * 16 + 8 * b + 8][
                        :, :, 128 * it:128 * (it + 1)].transpose([1, 0, 2]),
                    in_=qev[:, :]), 16)
    GE.wait_ge(ST.h, last)
    c_agwd.inc(GE.collective_compute(
        "AllGather", ALU.bypass, replica_groups=[list(range(NC))],
        ins=[wdq_in.ap().opt()], outs=[wdq_all.ap().opt()]))

    pq.close()

    barrier()
    pm1.close()
    pwi.close()

    pm3 = contextlib.ExitStack()
    HC = CT // 2
    u_c2b = [[sbuf(f"u_c2{p}_{m}", [128, HC], BF16, pm3) for m in range(4)]
             for p in range(2)]
    siluz2b = [[sbuf(f"sz2{p}_{m}", [128, HC], BF16, pm3) for m in range(4)]
               for p in range(2)]
    dtevB = sbuf("dtevB", [128, 512], F32, pm3)
    dt_bf = sbuf("dt_bf", [128, CT], BF16, pm3)
    bc_sb = sbuf("bc_sb", [32, CT], F32, pm3)
    bc_bf = sbuf("bc_bf", [32, CT], BF16, pm3)
    Bbc = [sbuf("BbcA", [128, 16 * TT], BF16, pm3),
           sbuf("BbcB", [128, 16 * TT], BF16, pm3)]
    Cbc = [sbuf("CbcA", [128, 16 * TT], BF16, pm3),
           sbuf("CbcB", [128, 16 * TT], BF16, pm3)]
    delta_sb = [sbuf(f"delta{m}", [128, HC], F32, pm3) for m in range(4)]
    sp_a = sbuf("sp_a", [128, 512], F32, pm3)
    sp_b = sbuf("sp_b", [128, 512], F32, pm3)
    sp_c = sbuf("sp_c", [128, 512], F32, pm3)
    dA2 = [sbuf("dAA", [128, 16 * TT], F32, pm3), sbuf("dAB", [128, 16 * TT], F32, pm3)]
    x2b = [sbuf("x2bA", [128, 16 * TT], BF16, pm3), sbuf("x2bB", [128, 16 * TT], BF16, pm3)]
    du_sb = sbuf("du_sb", [128, TT], F32, pm3)
    h_sb = sbuf("h_sb", [128, 16 * TT], BF16, pm3)
    y23b = [[sbuf(f"y23_{p}_{m}", [128, TT], BF16, pm3) for m in range(4)]
            for p in range(2)]
    carry = sbuf("carry", [128, 64], F32, pm3)
    oo_sb = sbuf("oo_sb", [128, 1024], F32, pm3)
    bc_dram = [dram(f"bc_dram{kq}", [32, CT], BF16) for kq in range(NCHUNK)]

    def uc_load(H):
        kq, hq = H // 2, H % 2
        SY.wait_ge(V.h, V.n)
        SY.wait_ge(G.h, G.n)
        n = L_.n
        for m in range(4):
            n = L_.inc(SY.dma_start(
                out=u_c2b[H % 2][m][:, :],
                in_=uc_spill.rearrange("(k m p) t -> k m p t", k=NCHUNK, p=128)[kq][m][
                    :, HC * hq:HC * (hq + 1)]), 16)
            n = L_.inc(SY.dma_start(
                out=siluz2b[H % 2][m][:, :],
                in_=sz_spill.rearrange("(k m p) t -> k m p t", k=NCHUNK, p=128)[kq][m][
                    :, HC * hq:HC * (hq + 1)]), 16)
        return n

    uc_ready = [0, 0]
    uc_ready[0] = uc_load(0)
    def bc_bcast_load(kq, tauq, bparq):
        SY.wait_ge(G.h, bc_guard[bparq][0])
        SY.wait_ge(V.h, bc_guard[bparq][1])
        SY.wait_ge(ST.h, st_bc_all[kq])
        t0q = TT * tauq
        n = L_.inc(SY.dma_start(
            out=Bbc[bparq][:, :].rearrange("p (s t) -> p s t", s=16),
            in_=bc_dram[kq].ap()[0:16, t0q:t0q + TT]
            .unsqueeze(0).broadcast_to([128, 16, TT])), 16)
        n = L_.inc(SY.dma_start(
            out=Cbc[bparq][:, :].rearrange("p (s t) -> p s t", s=16),
            in_=bc_dram[kq].ap()[16:32, t0q:t0q + TT]
            .unsqueeze(0).broadcast_to([128, 16, TT])), 16)
        return n

    st_bc_all = [0] * NCHUNK
    bc_guard = [[0, 0], [0, 0]]
    nB = [0] * NTAU
    sc_exp = [SC.n, SC.n]
    v_scans = [V.n, V.n]
    v_tree = [V.n, V.n]
    g_x = [G.n, G.n]
    for k in range(NCHUNK):
        if k % 4 == 0:
            VE.wait_ge(PE.h, PE.n)
            VE.memset(carry[:, :], 0.0)
            V.inc(spacer())
        SY.wait_ge(c_ar[k].h, 1)
        for hh in range(4):
            SY.wait_ge(SC.h, SC.n)
            nL = L_.inc(SY.dma_start(out=dtevB[:, :],
                                     in_=ar_out[k][0:128, 512 * hh:512 * (hh + 1)]), 16)
            SE.wait_ge(L_.h, nL)
            SE.wait_ge(PE.h, PE.n)
            SC.inc(SE.copy(dt_bf[:, 512 * hh:512 * (hh + 1)], dtevB[:, :]))
        SY.wait_ge(V.h, V.n)
        nL = L_.inc(SY.dma_start(out=bc_sb[:, :], in_=ar_out[k][128:160, :]), 16)
        VE.wait_ge(L_.h, nL)
        VE.tensor_copy(bc_bf[:, :], bc_sb[:, :])
        spacer()
        vbc = V.inc(spacer())
        SE.wait_ge(V.h, vbc)
        st_bc_all[k] = ST.inc(SE.dma_start(out=bc_dram[k].ap(), in_=bc_bf[:, :]), 16)
        if k > 0:
            nB[0] = bc_bcast_load(k, 0, 0)
        if k == 0:
            tapw("dtbf0", dt_bf[:, :])

        pe_wout = PE.n
        pe_wout2 = [PE.n, PE.n]
        for half in range(2):
            ht0 = HC * half
            VE.wait_ge(PE.h, pe_wout)
            VE.wait_ge(G.h, G.n)
            for m in range(4):
                for nn in range(2):
                    pa = pA[nn % 2]
                    TE.wait_ge(SC.h, SC.n)
                    pe = PE.inc(TE.matmul(pa[:, :], w_dt_bf[:, 128 * m:128 * (m + 1)],
                                          dt_bf[:, ht0 + 512 * nn:ht0 + 512 * (nn + 1)],
                                          start=True, stop=True))
                    SE.wait_ge(PE.h, pe)
                    SE.activation(sp_a[:, :], pa[:, :], AF.Relu, bias=dtb_sb[:, m:m + 1])
                    SE.activation(sp_b[:, :], pa[:, :], AF.Abs, bias=dtb_sb[:, m:m + 1])
                    SE.activation(sp_c[:, :], sp_b[:, :], AF.Exp, scale=-1.0)
                    scd = SC.inc(SE.activation(sp_b[:, :], sp_c[:, :], AF.Ln, bias=1.0))
                    VE.wait_ge(SC.h, scd)
                    VE.tensor_tensor(delta_sb[m][:, 512 * nn:512 * (nn + 1)],
                                     sp_a[:, :], sp_b[:, :], ALU.add)
            spacer()
            v_delta = V.inc(spacer())
            H = 2 * k + half
            u_c2 = u_c2b[H % 2]
            siluz2 = siluz2b[H % 2]
            if H + 1 < 2 * NCHUNK:
                uc_ready[(H + 1) % 2] = uc_load(H + 1)
            VE.wait_ge(L_.h, uc_ready[H % 2])
            GE.wait_ge(L_.h, uc_ready[H % 2])
            for tau4 in range(4):
                tau = half * 4 + tau4
                t0 = TT * tau
                tl0 = TT * tau4
                bpar = tau % 2
                if k == 0 and half == 0 and tau4 == 0:
                    nB[0] = bc_bcast_load(0, 0, 0)
                if tau + 1 < NTAU:
                    nB[tau + 1] = bc_bcast_load(k, tau + 1, (tau + 1) % 2)
                y23 = y23b[tau % 2]
                VE.wait_ge(PE.h, pe_wout2[tau % 2])
                for m in range(4):
                    par = m % 2
                    SE.wait_ge(V.h, v_delta)
                    SE.wait_ge(V.h, v_scans[par])
                    for s in range(16):
                        se_i = SE.activation(dA2[par][:, TT * s:TT * (s + 1)],
                                             delta_sb[m][:, tl0:tl0 + TT], AF.Exp,
                                             scale=negA_sb[:, 16 * m + s:16 * m + s + 1])
                    sc_exp[par] = SC.inc(se_i)
                    GE.wait_ge(V.h, v_delta)
                    GE.wait_ge(V.h, v_scans[par])
                    GE.wait_ge(V.h, v_tree[par])
                    GE.wait_ge(L_.h, nB[tau])
                    GE.tensor_tensor(du_sb[:, :], delta_sb[m][:, tl0:tl0 + TT],
                                     u_c2[m][:, tl0:tl0 + TT], ALU.mult)
                    gx = G.inc(GE.tensor_tensor(
                        x2b[par][:, :].rearrange("p (s t) -> p s t", s=16),
                        du_sb[:, :].unsqueeze(1).broadcast_to([128, 16, TT]),
                        Bbc[bpar][:, :].rearrange("p (s t) -> p s t", s=16), ALU.mult))
                    g_x[par] = gx
                    VE.wait_ge(SC.h, sc_exp[par])
                    VE.wait_ge(G.h, gx)
                    for s in range(16):
                        VE.tensor_tensor_scan(h_sb[:, TT * s:TT * (s + 1)],
                                              dA2[par][:, TT * s:TT * (s + 1)],
                                              x2b[par][:, TT * s:TT * (s + 1)],
                                              carry[:, 16 * m + s:16 * m + s + 1],
                                              ALU.mult, ALU.add)
                    spacer()
                    v_scans[par] = V.inc(spacer())
                    VE.tensor_tensor(x2b[par][:, :], h_sb[:, :], Cbc[bpar][:, :],
                                     ALU.mult)
                    dp_ = x2b[par]
                    VE.tensor_tensor(dp_[:, 0:8 * TT], dp_[:, 0:8 * TT],
                                     dp_[:, 8 * TT:16 * TT], ALU.add)
                    VE.tensor_tensor(dp_[:, 8 * TT:12 * TT], dp_[:, 0:4 * TT],
                                     dp_[:, 4 * TT:8 * TT], ALU.add)
                    VE.tensor_tensor(dp_[:, 12 * TT:14 * TT], dp_[:, 8 * TT:10 * TT],
                                     dp_[:, 10 * TT:12 * TT], ALU.add)
                    VE.tensor_tensor(dp_[:, 14 * TT:15 * TT], dp_[:, 12 * TT:13 * TT],
                                     dp_[:, 13 * TT:14 * TT], ALU.add)
                    VE.scalar_tensor_tensor(y23[m][:, :], u_c2[m][:, tl0:tl0 + TT],
                                            dp_sb[:, m:m + 1], dp_[:, 14 * TT:15 * TT],
                                            ALU.mult, ALU.add)
                    VE.tensor_tensor(y23[m][:, :], y23[m][:, :],
                                     siluz2[m][:, tl0:tl0 + TT], ALU.mult)
                    v_tree[par] = V.inc(VE.tensor_copy(
                        carry[:, 16 * m:16 * (m + 1)],
                        h_sb[:, :].rearrange("p (s t) -> p s t", s=16)[:, :, TT - 1]))
                bc_guard[bpar] = [G.n, V.n]
                spacer()
                v_y = V.inc(spacer())
                TE.wait_ge(V.h, v_y)
                for mh in range(2):
                    for fn in range(4):
                        pb = pB[fn % 2]
                        TE.wait_ge(SC.h, SC.n)
                        for kd in range(4):
                            mmo = TE.matmul(
                                pb[:, :], y23[kd][:, 128 * mh:128 * (mh + 1)],
                                w_out_bf[:, DM * kd + 512 * fn:DM * kd + 512 * (fn + 1)],
                                start=(kd == 0), stop=(kd == 3))
                        pe_wout = PE.inc(mmo)
                        pe_wout2[tau % 2] = pe_wout
                        SE.wait_ge(PE.h, pe_wout)
                        if fn % 2 == 0:
                            SE.wait_ge(ST.h, ST.n)
                        SC.inc(SE.copy(oo_sb[:, 512 * (fn % 2):512 * (fn % 2 + 1)],
                                       pb[:, :]))
                        ST.inc(SE.dma_start(
                            out=rs_in[k][t0 + 128 * mh:t0 + 128 * (mh + 1),
                                         512 * fn:512 * (fn + 1)],
                            in_=oo_sb[:, 512 * (fn % 2):512 * (fn % 2 + 1)]), 16)
        if k == 0:
            tapw("y230", y23[0][:, :])
        GE.wait_ge(ST.h, ST.n)
        c_rs[k].inc(GE.collective_compute(
            "ReduceScatter", ALU.add, replica_groups=[list(range(NC))],
            ins=[rs_in[k].ap().opt()], outs=[rs_out[k].ap().opt()]))

    barrier()
    pm3.close()
    pwo.close()

    pfc = contextlib.ExitStack()
    fbns = sbuf("fbns", [128, 24], F32, pfc)
    fmv = sbuf("fmv", [128, 4], F32, pfc)
    fcf = sbuf("fcf", [128, 4], F32, pfc)

    pf12 = contextlib.ExitStack()
    xqT = sbuf("xqT", [128, 16 * TOK], BF16, pf12)
    fqs = {
        "v": sbuf("f_v", [128, DM], F32, pf12),
        "r1": sbuf("f_r1", [128, DM], F32, pf12),
        "q": sbuf("f_q", [128, DM], F32, pf12),
        "f8": sbuf("f_f8", [128, DM], FP8, pf12),
        "m2": sbuf("f_m2", [128, DM], U8, pf12),
        "am": sbuf("f_am", [128, 64], F32, pf12),
        "ie": sbuf("f_ie", [128, 64], I32, pf12),
        "ir": sbuf("f_ir", [128, 64], I32, pf12),
        "is": sbuf("f_is", [128, 64], I32, pf12),
    }

    pf1 = contextlib.ExitStack()
    xt2 = sbuf("xt2", [128, DM], F32, pf1)
    at2 = sbuf("at2", [128, DM], F32, pf1)
    x2t = sbuf("x2t", [128, DM], F32, pf1)
    x3n = sbuf("x3n", [128, DM], F32, pf1)
    xq_bf = sbuf("xq_bf", [128, DM], BF16, pf1)

    if "rs0" in tap_bufs:
        GE.wait_ge(c_rs[0].h, 1)
        ST.inc(GE.dma_start(out=tap_bufs["rs0"], in_=rs_out[0].ap()), 16)
    for i in range(16):
        kchunk, half = i // 2, i % 2
        SY.wait_ge(c_rs[kchunk].h, 1)
        SY.wait_ge(V.h, V.n)
        SY.wait_ge(SC.h, SC.n)
        nL = L_.inc(SY.dma_start(out=xt2[:, :],
                                 in_=x_in.rearrange("(i p) f -> i p f", p=128)[i]), 16)
        nL = L_.inc(SY.dma_start(out=at2[:, :],
                                 in_=ln1_spill.rearrange("(i p) f -> i p f", p=128)[i]), 16)
        nL = L_.inc(SY.dma_start(out=x2t[:, :],
                                 in_=rs_out[kchunk].rearrange("(h p) f -> h p f", p=128)[half]),
                    16)
        VE.wait_ge(L_.h, nL)
        VE.wait_ge(ST.h, ST.n)
        VE.tensor_tensor(at2[:, :], at2[:, :], x2t[:, :], ALU.add)
        vw = ln_stats(at2, fbns, fmv)
        vw = ln_coefs(fmv, fcf, vw)
        VE.tensor_scalar(x2t[:, :], at2[:, :], fcf[:, 0:1], fcf[:, 1:2],
                         ALU.mult, ALU.add)
        VE.tensor_tensor(x2t[:, :], x2t[:, :], xt2[:, :], ALU.add)
        spacer()
        vx2 = V.inc(spacer())
        SE.wait_ge(V.h, vx2)
        ST.inc(SE.dma_start(out=x2_spill.rearrange("(i p) f -> i p f", p=128)[i],
                            in_=x2t[:, :]), 16)
        vw = ln_stats(x2t, fbns, fmv)
        vw = ln_coefs(fmv, fcf, vw)
        VE.tensor_scalar(x3n[:, :], x2t[:, :], fcf[:, 0:1], fcf[:, 1:2],
                         ALU.mult, ALU.add)
        VE.wait_ge(PE.h, PE.n)
        vq = quant_a(x3n, xq_bf[:, :], fqs, DM)
        if i == 0:
            tapw("xq0", xq_bf[:, :])
        TE.wait_ge(V.h, vq)
        for b in range(2):
            TE.wait_ge(SC.h, SC.n)
            for j in range(8):
                kf = 8 * b + j
                mmt = TE.transpose(pTB[:, 128 * j:128 * (j + 1)],
                                   xq_bf[:, 128 * kf:128 * (kf + 1)], ident_bf[:, :])
            pe = PE.inc(mmt)
            SE.wait_ge(PE.h, pe)
            SC.inc(SE.copy(
                xqT[:, :].rearrange("p (a t) -> p a t", t=2048)[
                    :, 8 * b:8 * b + 8, 128 * i:128 * (i + 1)],
                pTB[:, :].rearrange("p (a t) -> p a t", t=128)))

    tapw("xqTv", xqT[:, 0:2048])
    barrier()
    pf1.close()

    if "x2sp" in tap_bufs:
        GE.wait_ge(ST.h, ST.n)
        ST.inc(GE.dma_start(out=tap_bufs["x2sp"], in_=x2_spill.ap()), 16)
    pf2 = contextlib.ExitStack()
    wgt = [sbuf("wgtA", [128, 16 * 512], BF16, pf2),
           sbuf("wgtB", [128, 16 * 512], BF16, pf2)]
    wut = [sbuf("wutA", [128, 16 * 512], BF16, pf2),
           sbuf("wutB", [128, 16 * 512], BF16, pf2)]
    sg_sb = [sbuf("sg_sbA", [128, 512], F32, pf2),
             sbuf("sg_sbB", [128, 512], F32, pf2)]
    inter = sbuf("inter", [128, 16 * 512], BF16, pf2)
    interq = inter
    iqev = [sbuf("iqevA", [128, 512], BF16, pf2), sbuf("iqevB", [128, 512], BF16, pf2)]

    SY.wait_ge(c_agwg.h, 1)
    SY.wait_ge(c_agwu.h, 1)
    sc_silu = [SC.n, SC.n]
    v_int = [V.n, V.n]

    def wgu_load(ic, wpar, pe_guard):
        r, icc = ic // 2, ic % 2
        SY.wait_ge(PE.h, pe_guard)
        n = L_.n
        for kf in range(16):
            n = L_.inc(SY.dma_start(
                out=wgt[wpar][:, 512 * kf:512 * (kf + 1)],
                in_=wgq_all.rearrange("(r a p) c -> r a p c", r=NC, p=128)[r][kf][
                    :, 512 * icc:512 * (icc + 1)]), 16)
            n = L_.inc(SY.dma_start(
                out=wut[wpar][:, 512 * kf:512 * (kf + 1)],
                in_=wuq_all.rearrange("(r a p) c -> r a p c", r=NC, p=128)[r][kf][
                    :, 512 * icc:512 * (icc + 1)]), 16)
        return n

    wgu_ready = [wgu_load(0, 0, 0), wgu_load(1, 1, 0)]
    pe_ic = [0, 0]
    for ic in range(16):
        wpar = ic % 2
        r, icc = ic // 2, ic % 2
        VE.wait_ge(PE.h, PE.n)
        nL = wgu_ready[wpar]
        TE.wait_ge(L_.h, nL)
        for mt in range(16):
            fpar = mt % 2
            pg, pu = pA[fpar], pB[fpar]
            TE.wait_ge(SC.h, sc_silu[fpar])
            TE.wait_ge(V.h, v_int[fpar])
            for kf in range(16):
                TE.matmul(pg[:, :],
                          xqT[:, 2048 * kf + 128 * mt:2048 * kf + 128 * (mt + 1)],
                          wgt[wpar][:, 512 * kf:512 * (kf + 1)],
                          start=(kf == 0), stop=(kf == 15))
            for kf in range(16):
                mmu = TE.matmul(pu[:, :],
                                xqT[:, 2048 * kf + 128 * mt:2048 * kf + 128 * (mt + 1)],
                                wut[wpar][:, 512 * kf:512 * (kf + 1)],
                                start=(kf == 0), stop=(kf == 15))
            pe = PE.inc(mmu)
            SE.wait_ge(PE.h, pe)
            scS = SC.inc(SE.activation(sg_sb[fpar][:, :], pg[:, :], AF.Silu))
            sc_silu[fpar] = scS
            VE.wait_ge(SC.h, scS)
            v_int[fpar] = V.inc(VE.tensor_tensor(
                inter[:, 512 * mt:512 * (mt + 1)], sg_sb[fpar][:, :],
                pu[:, :], ALU.mult))
        if ic == 0:
            tapw("wgt0", wgt[0][:, :])
            tq = tapw("intpre", inter[:, :])
            if tq:
                VE.wait_ge(ST.h, tq)
        pe_ic[wpar] = PE.n
        if ic + 2 < 16:
            wgu_ready[wpar] = wgu_load(ic + 2, wpar, pe_ic[wpar])
        for sub in range(4):
            vq = quant_a(inter[:, 2048 * sub:2048 * (sub + 1)],
                         interq[:, 2048 * sub:2048 * (sub + 1)], fqs, 2048)
        if ic == 0:
            tapw("iq0", interq[:, :])
        TE.wait_ge(V.h, vq)
        gI0 = r * 1024 + icc * 512
        st_iq = [ST.n, ST.n]
        for mt in range(16):
            par = mt % 2
            SE.wait_ge(ST.h, st_iq[par])
            TE.wait_ge(SC.h, SC.n)
            for sub in range(4):
                mmt = TE.transpose(
                    pTB[:, 128 * sub:128 * (sub + 1)],
                    interq[:, 512 * mt + 128 * sub:512 * mt + 128 * (sub + 1)],
                    ident_bf[:, :])
            pe = PE.inc(mmt)
            SE.wait_ge(PE.h, pe)
            SC.inc(SE.copy(iqev[par][:, :], pTB[:, 0:512]))
            SE.copy(junkB[:, 0:4], junkA[:, 0:4])
            st_iq[par] = ST.inc(SE.dma_start(
                out=ikt_spill.rearrange("(a p) t -> a p t", p=128)[
                    gI0 // 128:gI0 // 128 + 4][:, :, 128 * mt:128 * (mt + 1)]
                .transpose([1, 0, 2]),
                in_=iqev[par][:, :]), 16)

    barrier()
    pf2.close()
    pf12.close()

    pf3 = contextlib.ExitStack()
    ikt = [sbuf("iktA", [128, 64 * 128], BF16, pf3),
           sbuf("iktB", [128, 64 * 128], BF16, pf3)]
    wdt_s = [sbuf("wdt_sA", [128, 64 * 512], BF16, pf3),
             sbuf("wdt_sB", [128, 64 * 512], BF16, pf3)]
    dout_f = sbuf("dout_f", [128, DM], F32, pf3)
    ln4t = sbuf("ln4t", [128, DM], F32, pf3)
    xt3 = sbuf("xt3", [128, DM], F32, pf3)
    ot3 = [sbuf("ot3a", [128, DM], F32, pf3), sbuf("ot3b", [128, DM], F32, pf3)]

    SY.wait_ge(c_agwd.h, 1)

    def wd_load(fb2, par, pe_guard):
        SY.wait_ge(PE.h, pe_guard)
        nW = L_.n
        for half in range(2):
            fb = 2 * fb2 + half
            for kg in range(16):
                nW = L_.inc(SY.dma_start(
                    out=wdt_s[par][:, 2048 * kg:2048 * (kg + 1)].rearrange(
                        "p (kk c) -> p kk c", c=512)[:, :, 256 * half:256 * (half + 1)],
                    in_=wdq_all.rearrange("(r kk p) c -> r p kk c", r=NC, p=128)[fb][
                        :, 4 * kg:4 * (kg + 1), :]), 16)
        return nW

    def ikt_load(mt, par, pe_guard):
        SY.wait_ge(PE.h, pe_guard)
        nL = L_.n
        for kg in range(16):
            nL = L_.inc(SY.dma_start(
                out=ikt[par][:, 512 * kg:512 * (kg + 1)].rearrange(
                    "p (kk t) -> p kk t", t=128),
                in_=ikt_spill.rearrange("(kk p) t -> p kk t", p=128)[
                    :, 4 * kg:4 * (kg + 1), 128 * mt:128 * (mt + 1)]), 16)
        return nL

    wd_ready = [wd_load(0, 0, 0), wd_load(1, 1, 0)]
    nfb_total = 2
    ikt_ready = [ikt_load(0, 0, 0), ikt_load(1, 1, 0)]
    for mt in range(16):
        ipar = mt % 2
        for fb2 in range(4):
            par = (mt * 4 + fb2) % 2
            TE.wait_ge(L_.h, max(ikt_ready[ipar], wd_ready[par]))
            pb = pB[fb2 % 2]
            TE.wait_ge(SC.h, SC.n)
            TE.wait_ge(V.h, V.n)
            for kk in range(64):
                mmd = TE.matmul(pb[:, :], ikt[ipar][:, 128 * kk:128 * (kk + 1)],
                                wdt_s[par][:, 512 * kk:512 * (kk + 1)],
                                start=(kk == 0), stop=(kk == 63))
            mjunk = TE.matmul(pR[:, 0:128], onesbf[:, :], onesbf[:, :],
                              start=True, stop=True)
            pe = PE.inc(mjunk)
            if fb2 == 3 and mt + 2 < 16:
                ikt_ready[ipar] = ikt_load(mt + 2, ipar, pe)
            if nfb_total < 64:
                wd_ready[par] = wd_load(nfb_total % 4, par, pe)
                nfb_total += 1
            VE.wait_ge(PE.h, pe)
            spacer()
            V.inc(VE.tensor_copy(dout_f[:, 512 * fb2:512 * (fb2 + 1)], pb[:, :]))
        spacer()
        vw = ln_stats(dout_f, fbns, fmv)
        vw = ln_coefs(fmv, fcf, vw)
        SY.wait_ge(V.h, V.n)
        SY.wait_ge(ST.h, ST.n)
        nL2 = L_.inc(SY.dma_start(out=xt3[:, :],
                                  in_=x2_spill.rearrange("(i p) f -> i p f", p=128)[mt]), 16)
        VE.tensor_scalar(ln4t[:, :], dout_f[:, :], fcf[:, 0:1], fcf[:, 1:2],
                         ALU.mult, ALU.add)
        VE.wait_ge(L_.h, nL2)
        VE.wait_ge(ST.h, ST.n)
        VE.tensor_tensor(ot3[mt % 2][:, :], xt3[:, :], ln4t[:, :], ALU.add)
        spacer()
        spacer()
        vo = V.inc(spacer())
        SE.wait_ge(V.h, vo)
        ST.inc(SE.dma_start(out=out_own.rearrange("(i p) f -> i p f", p=128)[mt],
                            in_=ot3[mt % 2][:, :]), 16)

    SY.wait_ge(ST.h, ST.n)
    GE.wait_ge(ST.h, ST.n)

    pf3.close()
    pfc.close()
    es.close()
    return nc


def _slice_inputs(inputs, c):
    x = np.asarray(inputs["L0_attn_x"]).reshape(NT, DM)
    stripes = [x[(k * NC + c) * STR:(k * NC + c) * STR + STR] for k in range(NCHUNK)]
    x_own = np.ascontiguousarray(np.concatenate(stripes, axis=0))
    W_in = np.asarray(inputs["L0_attn_W_in"])
    w_in_sl = np.ascontiguousarray(np.concatenate(
        [W_in[:, c * DOWN:(c + 1) * DOWN], W_in[:, DI + c * DOWN:DI + (c + 1) * DOWN]],
        axis=1))
    cw = np.asarray(inputs["L0_attn_conv_w"])[c * DOWN:(c + 1) * DOWN, 0, :]
    convw = np.ascontiguousarray(cw.reshape(4, 128, 4).transpose(1, 0, 2).reshape(128, 16))
    convb = np.ascontiguousarray(
        np.asarray(inputs["L0_attn_conv_b"])[c * DOWN:(c + 1) * DOWN].reshape(4, 128).T)
    wx_sl = np.ascontiguousarray(np.asarray(inputs["L0_attn_W_x"])[c * DOWN:(c + 1) * DOWN])
    wdt_sl = np.ascontiguousarray(
        np.asarray(inputs["L0_attn_W_dt"])[:, c * DOWN:(c + 1) * DOWN])
    dtb = np.ascontiguousarray(
        np.asarray(inputs["L0_attn_dt_bias"])[c * DOWN:(c + 1) * DOWN].reshape(4, 128).T)
    alog = np.ascontiguousarray(
        np.asarray(inputs["L0_attn_A_log"])[c * DOWN:(c + 1) * DOWN].reshape(4, 128, 16)
        .transpose(1, 0, 2).reshape(128, 64))
    dp = np.ascontiguousarray(
        np.asarray(inputs["L0_attn_Dp"])[c * DOWN:(c + 1) * DOWN].reshape(4, 128).T)
    wout_sl = np.ascontiguousarray(np.asarray(inputs["L0_attn_W_out"])[c * DOWN:(c + 1) * DOWN])
    wg_sl = np.ascontiguousarray(np.asarray(inputs["L0_ffn_w_gate"])[c * 1024:(c + 1) * 1024])
    wu_sl = np.ascontiguousarray(np.asarray(inputs["L0_ffn_w_up"])[c * 1024:(c + 1) * 1024])
    wd_sl = np.ascontiguousarray(np.asarray(inputs["L0_ffn_w_down"])[c * 256:(c + 1) * 256])
    return dict(
        x_own=x_own.astype(np.float32),
        ident=np.eye(128, dtype=np.float32),
        rmsw=np.asarray(inputs["L0_attn_rms_w"]).reshape(1, DM).astype(np.float32),
        w_in_sl=w_in_sl.astype(np.float32), convw=convw.astype(np.float32),
        convb=convb.astype(np.float32), wx_sl=wx_sl.astype(np.float32),
        wdt_sl=wdt_sl.astype(np.float32), dtb=dtb.astype(np.float32),
        alog=alog.astype(np.float32), dp=dp.astype(np.float32),
        wout_sl=wout_sl.astype(np.float32), wg_sl=wg_sl.astype(np.float32),
        wu_sl=wu_sl.astype(np.float32), wd_sl=wd_sl.astype(np.float32),
    )


def kernel(**inputs):
    global _LAST_TAPS
    nc = build(taps=_DEBUG_TAPS)
    in_maps = [_slice_inputs(inputs, c) for c in range(NC)]
    res = run_bass_kernel_spmd(nc, in_maps, core_ids=list(range(NC)))
    _LAST_TAPS = {}
    for tname, _, _ in _DEBUG_TAPS:
        _LAST_TAPS[tname] = [r["tap_" + tname] for r in res.results]
    out = np.zeros((NT, DM), dtype=np.float32)
    for c in range(NC):
        o = res.results[c]["out_own"]
        for k in range(NCHUNK):
            out[(k * NC + c) * STR:(k * NC + c) * STR + STR] = o[k * STR:(k + 1) * STR]
    return out.reshape(B, L, DM)


# version 08
# speedup vs baseline: 1.0753x; 1.0753x over previous
import contextlib
import numpy as np
import concourse.bass as bass
import concourse.mybir as mybir
from concourse.bass_utils import run_bass_kernel_spmd

F32 = mybir.dt.float32
BF16 = mybir.dt.bfloat16
U8 = mybir.dt.uint8
I32 = mybir.dt.int32
FP8 = mybir.dt.float8e4
AF = mybir.ActivationFunctionType
ALU = mybir.AluOpType

NC = 8
B, L, DM = 2, 8192, 2048
NT = B * L
DI, DS, DR, KC = 4096, 16, 128, 4
IF = 8192
DOWN = DI // NC
TOK = NT // NC
STR = 256
NCHUNK = 8
CT = 2048
TT = 256
NTAU = CT // TT
EPS = 1e-5

S_8TH = 1.5 * 2.0 ** 20
S_HALF = 1.5 * 2.0 ** 22
S_ONE = 1.5 * 2.0 ** 23
S_TWO = 1.5 * 2.0 ** 24

_DEBUG_TAPS = ()
_LAST_TAPS = {}


class Sem:
    def __init__(self, nc, es, name):
        self.h = es.enter_context(nc.semaphore(name))
        self.n = 0

    def inc(self, instr, amt=1):
        instr.then_inc(self.h, amt)
        self.n += amt
        return self.n


def build(taps=()):
    nc = bass.Bass("TRN2", target_bir_lowering=False,
                   detect_race_conditions=False)
    es = contextlib.ExitStack()

    def din(name, shape, dt=F32):
        return nc.dram_tensor(name, shape, dt, kind="ExternalInput").ap()

    def dout(name, shape, dt=F32):
        return nc.dram_tensor(name, shape, dt, kind="ExternalOutput").ap()

    def dram(name, shape, dt=F32):
        return nc.dram_tensor(name, shape, dt)

    x_in = din("x_own", [TOK, DM])
    ident_in = din("ident", [128, 128])
    rmsw_in = din("rmsw", [1, DM])
    w_in_sl = din("w_in_sl", [DM, 1024])
    convw_in = din("convw", [128, 16])
    convb_in = din("convb", [128, 4])
    wx_in = din("wx_sl", [DOWN, 160])
    wdt_in = din("wdt_sl", [DR, DOWN])
    dtb_in = din("dtb", [128, 4])
    alog_in = din("alog", [128, 64])
    dp_in = din("dp", [128, 4])
    wout_in = din("wout_sl", [DOWN, DM])
    wg_in = din("wg_sl", [IF // NC, DM])
    wu_in = din("wu_sl", [IF // NC, DM])
    wd_in = din("wd_sl", [DM // NC, IF])

    out_own = dout("out_own", [TOK, DM])

    hn_in = dram("hn_in", [DM, TOK], BF16)
    hn_all = dram("hn_all", [NC * DM, TOK], BF16)
    ln1_spill = dram("ln1_spill", [TOK, DM], F32)
    x2_spill = dram("x2_spill", [TOK, DM], F32)
    ar_in = [dram(f"ar_in{k}", [160, CT], F32) for k in range(NCHUNK)]
    ar_out = [dram(f"ar_out{k}", [160, CT], F32) for k in range(NCHUNK)]
    rs_in = [dram(f"rs_in{k}", [CT, DM], F32) for k in range(NCHUNK)]
    rs_out = [dram(f"rs_out{k}", [STR, DM], F32) for k in range(NCHUNK)]
    wgq_in = dram("wgq_in", [DM, 1024], BF16)
    wgq_all = dram("wgq_all", [NC * DM, 1024], BF16)
    wuq_in = dram("wuq_in", [DM, 1024], BF16)
    wuq_all = dram("wuq_all", [NC * DM, 1024], BF16)
    wdq_in = dram("wdq_in", [IF, DM // NC], BF16)
    wdq_all = dram("wdq_all", [NC * IF, DM // NC], BF16)
    ikt_spill = dram("ikt_spill", [IF, TOK], BF16)
    dout_spill = dram("dout_spill", [TOK, DM], F32)
    uc_spill = dram("uc_spill", [NCHUNK * 512, CT], BF16)
    sz_spill = dram("sz_spill", [NCHUNK * 512, CT], BF16)

    tap_bufs = {}
    for tname, tshape, tdt in taps:
        tap_bufs[tname] = dout("tap_" + tname, list(tshape), tdt)

    L_ = Sem(nc, es, "Lq")
    ST = Sem(nc, es, "STq")
    V = Sem(nc, es, "Vm")
    SC = Sem(nc, es, "SCm")
    PE = Sem(nc, es, "PEm")
    G = Sem(nc, es, "Gm")
    c_aghn = Sem(nc, es, "c_aghn")
    c_agwg = Sem(nc, es, "c_agwg")
    c_agwu = Sem(nc, es, "c_agwu")
    c_agwd = Sem(nc, es, "c_agwd")
    c_ar = [Sem(nc, es, f"c_ar{k}") for k in range(NCHUNK)]
    c_rs = [Sem(nc, es, f"c_rs{k}") for k in range(NCHUNK)]

    def sbuf(name, shape, dt, stack=None):
        return (stack or es).enter_context(nc.sbuf_tensor(name, shape, dt))

    def psum(name, shape, dt=F32):
        return es.enter_context(nc.psum_tensor(name, shape, dt))

    w_x_bf = sbuf("w_x_bf", [128, 4 * 160], BF16)
    w_dt_bf = sbuf("w_dt_bf", [128, DOWN], BF16)
    A_sb = sbuf("A_sb", [128, 64], F32)
    negA_sb = sbuf("negA_sb", [128, 64], F32)
    convw_sb = sbuf("convw_sb", [128, 16], F32)
    convb_sb = sbuf("convb_sb", [128, 4], F32)
    dtb_sb = sbuf("dtb_sb", [128, 4], F32)
    dp_sb = sbuf("dp_sb", [128, 4], F32)
    onesbf = sbuf("onesbf", [1, 128], BF16)
    ones32 = sbuf("ones32", [1, 128], F32)
    ident_bf = sbuf("ident_bf", [128, 128], BF16)
    junkA = sbuf("junkA", [128, 8], F32)
    junkB = sbuf("junkB", [128, 8], F32)
    consts = sbuf("consts", [128, 4], F32)

    pA = [psum("pA0", [128, 512]), psum("pA1", [128, 512])]
    pB = [psum("pB0", [128, 512]), psum("pB1", [128, 512])]
    pT = [psum("pT0", [128, 128], BF16), psum("pT1", [128, 128], BF16)]
    pTB = psum("pTB", [128, 1024], BF16)
    pR = psum("pR", [128, 512])

    block = es.enter_context(nc.Block())
    SY, VE, SE, TE, GE = nc.sync, nc.vector, nc.scalar, nc.tensor, nc.gpsimd

    def spacer():
        return VE.tensor_copy(junkA[:, :], junkB[:, :])

    def barrier():
        vals = [(V.h, V.n), (SC.h, SC.n), (PE.h, PE.n), (G.h, G.n),
                (L_.h, L_.n), (ST.h, ST.n)]
        for e in (SY, VE, SE, TE, GE):
            for h, n in vals:
                if n > 0:
                    e.wait_ge(h, n)

    def ln_stats(src, bns, mv):
        for j in range(4):
            VE.bn_stats(bns[:, 6 * j:6 * j + 6], src[:, 512 * j:512 * (j + 1)])
        spacer()
        VE.bn_aggr(mv[:, 0:2], bns[:, 0:24])
        spacer()
        return V.inc(spacer())

    def ln_coefs(mv, cf, vwait):
        SE.wait_ge(V.h, vwait)
        SE.activation(cf[:, 2:3], mv[:, 1:2], AF.Ln, bias=float(EPS))
        sc = SC.inc(SE.activation(cf[:, 0:1], cf[:, 2:3], AF.Exp, scale=-0.5))
        VE.wait_ge(SC.h, sc)
        VE.scalar_tensor_tensor(cf[:, 1:2], mv[:, 0:1], -1.0, cf[:, 0:1],
                                ALU.mult, ALU.mult)
        spacer()
        return V.inc(spacer())

    def _block_scales(s, src_am_ap, nb):
        VE.tensor_scalar(s["ie"][:, 0:nb], src_am_ap.bitcast(I32), 23, None,
                         ALU.logical_shift_right, ALU.bypass)
        spacer()
        VE.tensor_scalar(s["ie"][:, 0:nb], s["ie"][:, 0:nb], 129, None,
                         ALU.subtract, ALU.bypass)
        spacer()
        VE.tensor_scalar(s["ie"][:, 0:nb], s["ie"][:, 0:nb], -127, 127,
                         ALU.max, ALU.min)
        spacer()
        VE.tensor_scalar(s["ir"][:, 0:nb], s["ie"][:, 0:nb], -1, 127,
                         ALU.mult, ALU.add)
        VE.tensor_scalar(s["is"][:, 0:nb], s["ie"][:, 0:nb], 127, None,
                         ALU.add, ALU.bypass)
        spacer()
        VE.tensor_scalar(s["ir"][:, 0:nb], s["ir"][:, 0:nb], 23, None,
                         ALU.logical_shift_left, ALU.bypass)
        VE.tensor_scalar(s["is"][:, 0:nb], s["is"][:, 0:nb], 23, None,
                         ALU.logical_shift_left, ALU.bypass)
        spacer()

    def quant_w(wt, qbf, s, Fw):
        nb = Fw // 32
        VE.tensor_reduce(s["am"][:, 0:nb],
                         wt[:, 0:Fw].rearrange("p (b e) -> p b e", e=32),
                         axis=mybir.AxisListType.X, op=ALU.max,
                         apply_absolute_value=True)
        spacer()
        _block_scales(s, s["am"][:, 0:nb], nb)
        VE.tensor_tensor(s["v"][:, 0:Fw].rearrange("p (b e) -> p b e", e=32),
                         wt[:, 0:Fw].rearrange("p (b e) -> p b e", e=32),
                         s["ir"][:, 0:nb].bitcast(F32).unsqueeze(2)
                         .broadcast_to([128, nb, 32]), ALU.mult)
        VE.tensor_scalar(s["q"][:, 0:Fw].bitcast(I32), s["v"][:, 0:Fw].bitcast(I32),
                         0x7FFFFFFF, None, ALU.bitwise_and, ALU.bypass)
        VE.tensor_scalar(s["r05"][:, 0:Fw], s["v"][:, 0:Fw], S_HALF, S_HALF,
                         ALU.add, ALU.subtract)
        VE.tensor_scalar(s["r1"][:, 0:Fw], s["v"][:, 0:Fw], S_ONE, S_ONE,
                         ALU.add, ALU.subtract)
        VE.tensor_scalar(s["r2c"][:, 0:Fw], s["v"][:, 0:Fw], S_TWO, S_TWO,
                         ALU.add, ALU.subtract)
        VE.tensor_scalar(s["m2"][:, 0:Fw], s["q"][:, 0:Fw].bitcast(I32), 0x40000000,
                         None, ALU.is_lt, ALU.bypass)
        VE.tensor_scalar(s["m4"][:, 0:Fw], s["q"][:, 0:Fw].bitcast(I32), 0x40800000,
                         None, ALU.is_lt, ALU.bypass)
        VE.tensor_copy(s["q"][:, 0:Fw], s["r2c"][:, 0:Fw])
        VE.copy_predicated(s["q"][:, 0:Fw], s["m4"][:, 0:Fw], s["r1"][:, 0:Fw])
        VE.copy_predicated(s["q"][:, 0:Fw], s["m2"][:, 0:Fw], s["r05"][:, 0:Fw])
        VE.tensor_scalar(s["q"][:, 0:Fw], s["q"][:, 0:Fw], 6.0, -6.0, ALU.min, ALU.max)
        VE.tensor_tensor(qbf[:, 0:Fw].rearrange("p (b e) -> p b e", e=32),
                         s["q"][:, 0:Fw].rearrange("p (b e) -> p b e", e=32),
                         s["is"][:, 0:nb].bitcast(F32).unsqueeze(2)
                         .broadcast_to([128, nb, 32]), ALU.mult)
        return V.inc(spacer())

    def quant_a(at, qbf_ap, s, Fw):
        nb = Fw // 32
        VE.tensor_reduce(s["am"][:, 0:nb],
                         at[:, 0:Fw].rearrange("p (b e) -> p b e", e=32),
                         axis=mybir.AxisListType.X, op=ALU.max,
                         apply_absolute_value=True)
        spacer()
        _block_scales(s, s["am"][:, 0:nb], nb)
        VE.tensor_tensor(s["v"][:, 0:Fw].rearrange("p (b e) -> p b e", e=32),
                         at[:, 0:Fw].rearrange("p (b e) -> p b e", e=32),
                         s["ir"][:, 0:nb].bitcast(F32).unsqueeze(2)
                         .broadcast_to([128, nb, 32]), ALU.mult)
        VE.tensor_copy(s["f8"][:, 0:Fw], s["v"][:, 0:Fw])
        VE.tensor_scalar(s["q"][:, 0:Fw].bitcast(I32), s["v"][:, 0:Fw].bitcast(I32),
                         0x7FFFFFFF, None, ALU.bitwise_and, ALU.bypass)
        VE.tensor_scalar(s["r1"][:, 0:Fw], s["v"][:, 0:Fw], S_8TH, S_8TH,
                         ALU.add, ALU.subtract)
        VE.tensor_scalar(s["m2"][:, 0:Fw], s["q"][:, 0:Fw].bitcast(I32), 0x3F800000,
                         None, ALU.is_lt, ALU.bypass)
        VE.tensor_copy(s["q"][:, 0:Fw], s["f8"][:, 0:Fw])
        VE.copy_predicated(s["q"][:, 0:Fw], s["m2"][:, 0:Fw], s["r1"][:, 0:Fw])
        VE.tensor_scalar(s["q"][:, 0:Fw], s["q"][:, 0:Fw], 7.5, -7.5, ALU.min, ALU.max)
        VE.tensor_tensor(qbf_ap.rearrange("p (b e) -> p b e", e=32),
                         s["q"][:, 0:Fw].rearrange("p (b e) -> p b e", e=32),
                         s["is"][:, 0:nb].bitcast(F32).unsqueeze(2)
                         .broadcast_to([128, nb, 32]), ALU.mult)
        return V.inc(spacer())


    def transpose8(src_ap_fn, n_tiles, evac_fn):
        for b0 in range(0, n_tiles, 8):
            nb_ = min(8, n_tiles - b0)
            TE.wait_ge(SC.h, SC.n)
            TE.wait_ge(V.h, V.n)
            for j in range(nb_):
                mmt = TE.transpose(pTB[:, 128 * j:128 * (j + 1)], src_ap_fn(b0 + j),
                                   ident_bf[:, :])
            pe = PE.inc(mmt)
            SE.wait_ge(PE.h, pe)
            evac_fn(b0 // 8, pTB[:, 0:128 * nb_])

    def tapw(name, src_ap):
        if name in tap_bufs:
            SE.wait_ge(V.h, V.n)
            SE.wait_ge(SC.h, SC.n)
            return ST.inc(SE.dma_start(out=tap_bufs[name], in_=src_ap), 16)
        return None

    pwo = contextlib.ExitStack()
    pwi = contextlib.ExitStack()
    w_out_bf = sbuf("w_out_bf", [128, 4 * DM], BF16, pwo)
    w_in_bf = sbuf("w_in_bf", [128, 16 * 1024], BF16, pwi)

    p0 = contextlib.ExitStack()
    ident_f = sbuf("ident_f", [128, 128], F32, p0)
    rmsw_row = sbuf("rmsw_row", [1, DM], F32, p0)
    rmsw_bc = sbuf("rmsw_bc", [128, DM], F32, p0)

    nL = L_.inc(SY.dma_start(out=ident_f[:, :], in_=ident_in), 16)
    nL = L_.inc(SY.dma_start(out=convw_sb[:, :], in_=convw_in), 16)
    nL = L_.inc(SY.dma_start(out=convb_sb[:, :], in_=convb_in), 16)
    nL = L_.inc(SY.dma_start(out=dtb_sb[:, :], in_=dtb_in), 16)
    nL = L_.inc(SY.dma_start(out=dp_sb[:, :], in_=dp_in), 16)
    nL = L_.inc(SY.dma_start(out=rmsw_row[:, :], in_=rmsw_in), 16)
    nc.const_aps.register(F32, float(EPS), consts[:, 0:1])
    nc.const_aps.register(F32, 1.0, consts[:, 1:2])
    VE.wait_ge(L_.h, nL)
    VE.memset(consts[:, 0:1], float(EPS))
    VE.memset(consts[:, 1:2], 1.0)
    VE.memset(onesbf[:, :], 1.0)
    VE.memset(ones32[:, :], 1.0)
    VE.memset(junkB[:, :], 0.0)
    vE = V.inc(VE.tensor_copy(ident_bf[:, :], ident_f[:, :]))
    SY.wait_ge(V.h, vE)
    nLA = L_.inc(SY.dma_start(out=ident_f[:, 0:64], in_=alog_in), 16)
    SE.wait_ge(L_.h, nLA)
    SC.inc(SE.activation(A_sb[:, :], ident_f[:, 0:64], AF.Exp))
    VE.wait_ge(SC.h, SC.n)
    V.inc(VE.tensor_scalar(negA_sb[:, :], A_sb[:, :], -1.0, None, ALU.mult, ALU.bypass))

    TE.wait_ge(V.h, vE)
    for j in range(4):
        pe = PE.inc(TE.matmul(pR[:, :], ones32[:, :], rmsw_row[:, 512 * j:512 * (j + 1)],
                              start=True, stop=True))
        SE.wait_ge(PE.h, pe)
        scR = SC.inc(SE.copy(rmsw_bc[:, 512 * j:512 * (j + 1)], pR[:, :]))
        TE.wait_ge(SC.h, scR)

    xt = sbuf("xt", [128, DM], F32, p0)
    ln1t = sbuf("ln1t", [128, DM], F32, p0)
    hnt = sbuf("hnt", [128, DM], BF16, p0)
    hnev = sbuf("hnev", [128, 1024], BF16, p0)
    bns = sbuf("bns", [128, 24], F32, p0)
    mv = sbuf("mv", [128, 4], F32, p0)
    cf = sbuf("cf", [128, 4], F32, p0)

    st_hn = 0
    for i in range(16):
        SY.wait_ge(V.h, V.n)
        SY.wait_ge(SC.h, SC.n)
        nL = L_.inc(SY.dma_start(out=xt[:, :],
                                 in_=x_in.rearrange("(i p) f -> i p f", p=128)[i]), 16)
        VE.wait_ge(L_.h, nL)
        vw = ln_stats(xt, bns, mv)
        vw = ln_coefs(mv, cf, vw)
        VE.wait_ge(ST.h, ST.n)
        VE.tensor_scalar(ln1t[:, :], xt[:, :], cf[:, 0:1], cf[:, 1:2],
                         ALU.mult, ALU.add)
        spacer()
        vc1 = V.inc(spacer())
        vw2 = ln_stats(ln1t, bns, mv)
        SE.wait_ge(V.h, vw2)
        SE.activation(cf[:, 2:3], mv[:, 1:2], AF.Ln, bias=float(EPS))
        scg = SC.inc(SE.activation(cf[:, 2:3], cf[:, 2:3], AF.Exp, scale=-0.5))
        VE.wait_ge(SC.h, scg)
        VE.wait_ge(PE.h, PE.n)
        VE.scalar_tensor_tensor(hnt[:, :], ln1t[:, :], cf[:, 2:3],
                                rmsw_bc[:, :], ALU.mult, ALU.mult)
        spacer()
        vh = V.inc(spacer())
        SE.wait_ge(V.h, vc1)
        st_l = ST.inc(SE.dma_start(out=ln1_spill.rearrange("(i p) f -> i p f", p=128)[i],
                                   in_=ln1t[:, :]), 16)
        if i == 0:
            tapw("ln1", ln1t[:, :])
            tapw("hnt", hnt[:, :])
        TE.wait_ge(V.h, vh)
        for b in range(2):
            TE.wait_ge(SC.h, SC.n)
            for j in range(8):
                mmt = TE.transpose(pTB[:, 128 * j:128 * (j + 1)],
                                   hnt[:, 128 * (8 * b + j):128 * (8 * b + j + 1)],
                                   ident_bf[:, :])
            pe = PE.inc(mmt)
            SE.wait_ge(PE.h, pe)
            SE.wait_ge(ST.h, st_hn)
            SC.inc(SE.copy(hnev[:, :], pTB[:, :]))
            SE.copy(junkB[:, 0:4], junkA[:, 0:4])
            st_hn = ST.inc(SE.dma_start(
                out=hn_in.rearrange("(a p) t -> a p t", p=128)[8 * b:8 * b + 8][
                    :, :, 128 * i:128 * (i + 1)].transpose([1, 0, 2]),
                in_=hnev[:, :]), 16)

    GE.wait_ge(ST.h, st_hn)
    c_aghn.inc(GE.collective_compute(
        "AllGather", ALU.bypass, replica_groups=[list(range(NC))],
        ins=[hn_in.ap().opt()], outs=[hn_all.ap().opt()]))

    wstage = sbuf("wstage", [128, DM], F32, p0)

    def load_cast(dst_ap, src_ap, width):
        SY.wait_ge(SC.h, SC.n)
        nL = L_.inc(SY.dma_start(out=wstage[:, 0:width], in_=src_ap), 16)
        SE.wait_ge(L_.h, nL)
        return SC.inc(SE.copy(dst_ap, wstage[:, 0:width]))

    for kk in range(16):
        load_cast(w_in_bf[:, 1024 * kk:1024 * (kk + 1)],
                  w_in_sl.rearrange("(a p) c -> a p c", p=128)[kk], 1024)
    for kd in range(4):
        load_cast(w_out_bf[:, DM * kd:DM * (kd + 1)],
                  wout_in.rearrange("(a p) c -> a p c", p=128)[kd], DM)
    for kd in range(4):
        load_cast(w_x_bf[:, 160 * kd:160 * (kd + 1)],
                  wx_in.rearrange("(a p) c -> a p c", p=128)[kd], 160)
    load_cast(w_dt_bf[:, :], wdt_in, DOWN)

    barrier()
    p0.close()

    pm1 = contextlib.ExitStack()
    rhs_sb = sbuf("rhs_sb", [128, 16 * 512], BF16, pm1)
    u_ext = [sbuf(f"u_ext{m}", [128, 515], BF16, pm1) for m in range(4)]
    halo = [sbuf(f"halo{m}", [128, 3], BF16, pm1) for m in range(4)]
    cacc = [sbuf("cacc0", [128, 512], F32, pm1), sbuf("cacc1", [128, 512], F32, pm1)]
    u_c = [sbuf(f"u_c{m}", [128, CT], BF16, pm1) for m in range(4)]
    siluz = [sbuf(f"siluz{m}", [128, CT], BF16, pm1) for m in range(4)]
    dtev = sbuf("dtev", [128, 512], F32, pm1)

    st_spill = 0
    sc_bank = [SC.n, SC.n]
    for k in range(NCHUNK):
        if k % 4 == 0:
            vhz = V.n
            for m in range(4):
                vhz = V.inc(VE.memset(halo[m][:, :], 0.0))
        for n in range(4):
            SY.wait_ge(PE.h, PE.n)
            SY.wait_ge(c_aghn.h, 1)
            nL = L_.n
            for kk in range(16):
                for half in range(2):
                    r = 2 * n + half
                    nL = L_.inc(SY.dma_start(
                        out=rhs_sb[:, 512 * kk + 256 * half:512 * kk + 256 * (half + 1)],
                        in_=hn_all.rearrange("(r a p) t -> r a p t", r=NC, p=128)[r][kk][
                            :, STR * k:STR * (k + 1)]), 16)
            TE.wait_ge(L_.h, nL)
            for m in range(8):
                pa = pA[m % 2]
                TE.wait_ge(SC.h, sc_bank[m % 2])
                for kk in range(16):
                    mm = TE.matmul(
                        pa[:, :],
                        w_in_bf[:, 1024 * kk + 128 * m:1024 * kk + 128 * (m + 1)],
                        rhs_sb[:, 512 * kk:512 * (kk + 1)],
                        start=(kk == 0), stop=(kk == 15))
                pe = PE.inc(mm)
                SE.wait_ge(PE.h, pe)
                if m < 4:
                    SE.wait_ge(V.h, V.n)
                    if n == 0:
                        SE.wait_ge(ST.h, st_spill)
                    sc1 = SC.inc(SE.copy(u_ext[m][:, 3:515], pa[:, :]))
                    sc_bank[m % 2] = sc1
                    VE.wait_ge(SC.h, sc1)
                    VE.tensor_copy(u_ext[m][:, 0:3], halo[m][:, :])
                    VE.tensor_scalar(cacc[0][:, :], u_ext[m][:, 0:512],
                                     convw_sb[:, 4 * m:4 * m + 1], None,
                                     ALU.mult, ALU.bypass)
                    for j in range(1, 4):
                        srcb, dstb = (cacc[0], cacc[1]) if j % 2 == 1 else (cacc[1], cacc[0])
                        VE.scalar_tensor_tensor(dstb[:, :], u_ext[m][:, j:j + 512],
                                                convw_sb[:, 4 * m + j:4 * m + j + 1],
                                                srcb[:, :], ALU.mult, ALU.add)
                    vcv = V.inc(VE.tensor_copy(halo[m][:, :], u_ext[m][:, 512:515]))
                    SE.wait_ge(V.h, vcv)
                    SC.inc(SE.activation(u_c[m][:, 512 * n:512 * (n + 1)], cacc[1][:, :],
                                         AF.Silu, bias=convb_sb[:, m:m + 1]))
                else:
                    if n == 0:
                        SE.wait_ge(ST.h, st_spill)
                    sc_bank[m % 2] = SC.inc(SE.activation(
                        siluz[m - 4][:, 512 * n:512 * (n + 1)], pa[:, :], AF.Silu))
            TE.wait_ge(SC.h, SC.n)
            for kd in range(4):
                TE.matmul(pB[0][:, :], w_x_bf[:, 160 * kd:160 * kd + 128],
                          u_c[kd][:, 512 * n:512 * (n + 1)],
                          start=(kd == 0), stop=(kd == 3))
                mm2 = TE.matmul(pB[1][0:32, :], w_x_bf[:, 160 * kd + 128:160 * (kd + 1)],
                                u_c[kd][:, 512 * n:512 * (n + 1)],
                                start=(kd == 0), stop=(kd == 3))
            pe = PE.inc(mm2)
            SE.wait_ge(PE.h, pe)
            SC.inc(SE.copy(dtev[:, :], pB[0][:, :]))
            st_dt = ST.inc(SE.dma_start(out=ar_in[k][0:128, 512 * n:512 * (n + 1)],
                                        in_=dtev[:, :]), 16)
            SE.wait_ge(ST.h, st_dt)
            SC.inc(SE.copy(dtev[0:32, :], pB[1][0:32, :]))
            st1 = ST.inc(SE.dma_start(out=ar_in[k][128:160, 512 * n:512 * (n + 1)],
                                      in_=dtev[0:32, :]), 16)
            SE.wait_ge(ST.h, st1)
        if k == 0:
            tapw("uc0", u_c[0][:, :])
            tapw("sz0", siluz[0][:, :])
        SE.wait_ge(SC.h, SC.n)
        for m in range(4):
            ST.inc(SE.dma_start(
                out=uc_spill.rearrange("(k m p) t -> k m p t", k=NCHUNK, p=128)[k][m],
                in_=u_c[m][:, :]), 16)
            st_spill = ST.inc(SE.dma_start(
                out=sz_spill.rearrange("(k m p) t -> k m p t", k=NCHUNK, p=128)[k][m],
                in_=siluz[m][:, :]), 16)
        GE.wait_ge(ST.h, st_spill)
        c_ar[k].inc(GE.collective_compute(
            "AllReduce", ALU.add, replica_groups=[list(range(NC))],
            ins=[ar_in[k].ap().opt()], outs=[ar_out[k].ap().opt()]))

    pq = contextlib.ExitStack()
    wstage2 = sbuf("wstage2", [128, DM], F32, pq)
    qs = {
        "v": sbuf("q_v", [128, DM], F32, pq),
        "r05": sbuf("q_r05", [128, DM], F32, pq),
        "r1": sbuf("q_r1", [128, DM], F32, pq),
        "r2c": sbuf("q_r2c", [128, DM], F32, pq),
        "q": sbuf("q_q", [128, DM], F32, pq),
        "f8": sbuf("q_f8", [128, DM], FP8, pq),
        "m2": sbuf("q_m2", [128, DM], U8, pq),
        "m4": sbuf("q_m4", [128, DM], U8, pq),
        "am": sbuf("q_am", [128, 64], F32, pq),
        "ie": sbuf("q_ie", [128, 64], I32, pq),
        "ir": sbuf("q_ir", [128, 64], I32, pq),
        "is": sbuf("q_is", [128, 64], I32, pq),
    }
    qbf = sbuf("qbf", [128, DM], BF16, pq)
    qev = sbuf("qev", [128, 1024], BF16, pq)

    def q_t_store(src_tile_ap, n_kf, dst, col0):
        last = ST.n
        for b in range(n_kf // 8):
            TE.wait_ge(SC.h, SC.n)
            for j in range(8):
                kf = 8 * b + j
                mmt = TE.transpose(pTB[:, 128 * j:128 * (j + 1)],
                                   qbf[:, 128 * kf:128 * (kf + 1)], ident_bf[:, :])
            pe = PE.inc(mmt)
            SE.wait_ge(PE.h, pe)
            SE.wait_ge(ST.h, last)
            SC.inc(SE.copy(qev[:, :], pTB[:, :]))
            SE.copy(junkB[:, 0:4], junkA[:, 0:4])
            last = ST.inc(SE.dma_start(
                out=dst.rearrange("(a p) c -> a p c", p=128)[8 * b:8 * b + 8][
                    :, :, col0:col0 + 128].transpose([1, 0, 2]),
                in_=qev[:, :]), 16)
        return last

    for wt_dram, dst in ((wg_in, wgq_in), (wu_in, wuq_in)):
        for it in range(8):
            SY.wait_ge(V.h, V.n)
            nL = L_.inc(SY.dma_start(
                out=wstage2[:, :], in_=wt_dram.rearrange("(a p) c -> a p c", p=128)[it]), 16)
            VE.wait_ge(L_.h, nL)
            VE.wait_ge(PE.h, PE.n)
            vqd = quant_w(wstage2, qbf, qs, DM)
            TE.wait_ge(V.h, vqd)
            stq = q_t_store(0, 16, dst, 128 * it)
        GE.wait_ge(ST.h, stq)
        sem = c_agwg if wt_dram is wg_in else c_agwu
        sem.inc(GE.collective_compute(
            "AllGather", ALU.bypass, replica_groups=[list(range(NC))],
            ins=[(wgq_in if wt_dram is wg_in else wuq_in).ap().opt()],
            outs=[(wgq_all if wt_dram is wg_in else wuq_all).ap().opt()]))

    for it in range(2):
        for sub in range(4):
            SY.wait_ge(V.h, V.n)
            nL = L_.inc(SY.dma_start(
                out=wstage2[:, :],
                in_=wd_in.rearrange("(a p) (s c) -> a s p c", p=128, c=DM)[it][sub]), 16)
            VE.wait_ge(L_.h, nL)
            VE.wait_ge(PE.h, PE.n)
            vqd = quant_w(wstage2, qbf, qs, DM)
            TE.wait_ge(V.h, vqd)
            last = ST.n
            for b in range(2):
                TE.wait_ge(SC.h, SC.n)
                for j in range(8):
                    kf = 8 * b + j
                    mmt = TE.transpose(pTB[:, 128 * j:128 * (j + 1)],
                                       qbf[:, 128 * kf:128 * (kf + 1)], ident_bf[:, :])
                pe = PE.inc(mmt)
                SE.wait_ge(PE.h, pe)
                SE.wait_ge(ST.h, last)
                SC.inc(SE.copy(qev[:, :], pTB[:, :]))
                SE.copy(junkB[:, 0:4], junkA[:, 0:4])
                last = ST.inc(SE.dma_start(
                    out=wdq_in.rearrange("(a p) c -> a p c", p=128)[
                        sub * 16 + 8 * b:sub * 16 + 8 * b + 8][
                        :, :, 128 * it:128 * (it + 1)].transpose([1, 0, 2]),
                    in_=qev[:, :]), 16)
    GE.wait_ge(ST.h, last)
    c_agwd.inc(GE.collective_compute(
        "AllGather", ALU.bypass, replica_groups=[list(range(NC))],
        ins=[wdq_in.ap().opt()], outs=[wdq_all.ap().opt()]))

    pq.close()

    barrier()
    pm1.close()
    pwi.close()

    pm3 = contextlib.ExitStack()
    HC = CT // 2
    u_c2b = [[sbuf(f"u_c2{p}_{m}", [128, HC], BF16, pm3) for m in range(4)]
             for p in range(2)]
    siluz2b = [[sbuf(f"sz2{p}_{m}", [128, HC], BF16, pm3) for m in range(4)]
               for p in range(2)]
    dtevB = sbuf("dtevB", [128, 512], F32, pm3)
    dt_bf = sbuf("dt_bf", [128, CT], BF16, pm3)
    bc_sb = sbuf("bc_sb", [32, CT], F32, pm3)
    bc_bf = sbuf("bc_bf", [32, CT], BF16, pm3)
    Bbc = [sbuf("BbcA", [128, 16 * TT], BF16, pm3),
           sbuf("BbcB", [128, 16 * TT], BF16, pm3)]
    Cbc = [sbuf("CbcA", [128, 16 * TT], BF16, pm3),
           sbuf("CbcB", [128, 16 * TT], BF16, pm3)]
    delta_sb = [sbuf(f"delta{m}", [128, HC], F32, pm3) for m in range(4)]
    sp_a = sbuf("sp_a", [128, 512], F32, pm3)
    sp_b = sbuf("sp_b", [128, 512], F32, pm3)
    sp_c = sbuf("sp_c", [128, 512], F32, pm3)
    dA2 = [sbuf("dAA", [128, 16 * TT], F32, pm3), sbuf("dAB", [128, 16 * TT], F32, pm3)]
    x2b = [sbuf("x2bA", [128, 16 * TT], BF16, pm3), sbuf("x2bB", [128, 16 * TT], BF16, pm3)]
    du_sb = sbuf("du_sb", [128, TT], F32, pm3)
    h_sb = sbuf("h_sb", [128, 16 * TT], BF16, pm3)
    y23b = [[sbuf(f"y23_{p}_{m}", [128, TT], BF16, pm3) for m in range(4)]
            for p in range(2)]
    carry = sbuf("carry", [128, 64], F32, pm3)
    oo_sb = sbuf("oo_sb", [128, 1024], F32, pm3)
    bc_dram = [dram(f"bc_dram{kq}", [32, CT], BF16) for kq in range(NCHUNK)]

    def uc_load(H):
        kq, hq = H // 2, H % 2
        SY.wait_ge(V.h, V.n)
        SY.wait_ge(G.h, G.n)
        n = L_.n
        for m in range(4):
            n = L_.inc(SY.dma_start(
                out=u_c2b[H % 2][m][:, :],
                in_=uc_spill.rearrange("(k m p) t -> k m p t", k=NCHUNK, p=128)[kq][m][
                    :, HC * hq:HC * (hq + 1)]), 16)
            n = L_.inc(SY.dma_start(
                out=siluz2b[H % 2][m][:, :],
                in_=sz_spill.rearrange("(k m p) t -> k m p t", k=NCHUNK, p=128)[kq][m][
                    :, HC * hq:HC * (hq + 1)]), 16)
        return n

    uc_ready = [0, 0]
    uc_ready[0] = uc_load(0)
    def bc_bcast_load(kq, tauq, bparq):
        SY.wait_ge(G.h, bc_guard[bparq][0])
        SY.wait_ge(V.h, bc_guard[bparq][1])
        SY.wait_ge(ST.h, st_bc_all[kq])
        t0q = TT * tauq
        n = L_.inc(SY.dma_start(
            out=Bbc[bparq][:, :].rearrange("p (s t) -> p s t", s=16),
            in_=bc_dram[kq].ap()[0:16, t0q:t0q + TT]
            .unsqueeze(0).broadcast_to([128, 16, TT])), 16)
        n = L_.inc(SY.dma_start(
            out=Cbc[bparq][:, :].rearrange("p (s t) -> p s t", s=16),
            in_=bc_dram[kq].ap()[16:32, t0q:t0q + TT]
            .unsqueeze(0).broadcast_to([128, 16, TT])), 16)
        return n

    st_bc_all = [0] * NCHUNK
    bc_guard = [[0, 0], [0, 0]]
    nB = [0] * NTAU
    sc_exp = [SC.n, SC.n]
    v_scans = [V.n, V.n]
    v_tree = [V.n, V.n]
    g_x = [G.n, G.n]
    for k in range(NCHUNK):
        if k % 4 == 0:
            VE.wait_ge(PE.h, PE.n)
            VE.memset(carry[:, :], 0.0)
            V.inc(spacer())
        SY.wait_ge(c_ar[k].h, 1)
        for hh in range(4):
            SY.wait_ge(SC.h, SC.n)
            nL = L_.inc(SY.dma_start(out=dtevB[:, :],
                                     in_=ar_out[k][0:128, 512 * hh:512 * (hh + 1)]), 16)
            SE.wait_ge(L_.h, nL)
            SE.wait_ge(PE.h, PE.n)
            SC.inc(SE.copy(dt_bf[:, 512 * hh:512 * (hh + 1)], dtevB[:, :]))
        SY.wait_ge(V.h, V.n)
        nL = L_.inc(SY.dma_start(out=bc_sb[:, :], in_=ar_out[k][128:160, :]), 16)
        VE.wait_ge(L_.h, nL)
        VE.tensor_copy(bc_bf[:, :], bc_sb[:, :])
        spacer()
        vbc = V.inc(spacer())
        SE.wait_ge(V.h, vbc)
        st_bc_all[k] = ST.inc(SE.dma_start(out=bc_dram[k].ap(), in_=bc_bf[:, :]), 16)
        if k > 0:
            nB[0] = bc_bcast_load(k, 0, 0)
        if k == 0:
            tapw("dtbf0", dt_bf[:, :])

        pe_wout = PE.n
        pe_wout2 = [PE.n, PE.n]
        for half in range(2):
            ht0 = HC * half
            VE.wait_ge(PE.h, pe_wout)
            VE.wait_ge(G.h, G.n)
            for m in range(4):
                for nn in range(2):
                    pa = pA[nn % 2]
                    TE.wait_ge(SC.h, SC.n)
                    pe = PE.inc(TE.matmul(pa[:, :], w_dt_bf[:, 128 * m:128 * (m + 1)],
                                          dt_bf[:, ht0 + 512 * nn:ht0 + 512 * (nn + 1)],
                                          start=True, stop=True))
                    SE.wait_ge(PE.h, pe)
                    SE.activation(sp_a[:, :], pa[:, :], AF.Relu, bias=dtb_sb[:, m:m + 1])
                    SE.activation(sp_b[:, :], pa[:, :], AF.Abs, bias=dtb_sb[:, m:m + 1])
                    SE.activation(sp_c[:, :], sp_b[:, :], AF.Exp, scale=-1.0)
                    scd = SC.inc(SE.activation(sp_b[:, :], sp_c[:, :], AF.Ln, bias=1.0))
                    VE.wait_ge(SC.h, scd)
                    VE.tensor_tensor(delta_sb[m][:, 512 * nn:512 * (nn + 1)],
                                     sp_a[:, :], sp_b[:, :], ALU.add)
            spacer()
            v_delta = V.inc(spacer())
            H = 2 * k + half
            u_c2 = u_c2b[H % 2]
            siluz2 = siluz2b[H % 2]
            if H + 1 < 2 * NCHUNK:
                uc_ready[(H + 1) % 2] = uc_load(H + 1)
            VE.wait_ge(L_.h, uc_ready[H % 2])
            GE.wait_ge(L_.h, uc_ready[H % 2])
            for tau4 in range(4):
                tau = half * 4 + tau4
                t0 = TT * tau
                tl0 = TT * tau4
                bpar = tau % 2
                if k == 0 and half == 0 and tau4 == 0:
                    nB[0] = bc_bcast_load(0, 0, 0)
                if tau + 1 < NTAU:
                    nB[tau + 1] = bc_bcast_load(k, tau + 1, (tau + 1) % 2)
                y23 = y23b[tau % 2]
                VE.wait_ge(PE.h, pe_wout2[tau % 2])
                for m in range(4):
                    par = m % 2
                    SE.wait_ge(V.h, v_delta)
                    SE.wait_ge(V.h, v_scans[par])
                    for s in range(16):
                        se_i = SE.activation(dA2[par][:, TT * s:TT * (s + 1)],
                                             delta_sb[m][:, tl0:tl0 + TT], AF.Exp,
                                             scale=negA_sb[:, 16 * m + s:16 * m + s + 1])
                    sc_exp[par] = SC.inc(se_i)
                    GE.wait_ge(V.h, v_delta)
                    GE.wait_ge(V.h, v_scans[par])
                    GE.wait_ge(V.h, v_tree[par])
                    GE.wait_ge(L_.h, nB[tau])
                    GE.tensor_tensor(du_sb[:, :], delta_sb[m][:, tl0:tl0 + TT],
                                     u_c2[m][:, tl0:tl0 + TT], ALU.mult)
                    gx = G.inc(GE.tensor_tensor(
                        x2b[par][:, :].rearrange("p (s t) -> p s t", s=16),
                        du_sb[:, :].unsqueeze(1).broadcast_to([128, 16, TT]),
                        Bbc[bpar][:, :].rearrange("p (s t) -> p s t", s=16), ALU.mult))
                    g_x[par] = gx
                    VE.wait_ge(SC.h, sc_exp[par])
                    VE.wait_ge(G.h, gx)
                    for s in range(16):
                        VE.tensor_tensor_scan(h_sb[:, TT * s:TT * (s + 1)],
                                              dA2[par][:, TT * s:TT * (s + 1)],
                                              x2b[par][:, TT * s:TT * (s + 1)],
                                              carry[:, 16 * m + s:16 * m + s + 1],
                                              ALU.mult, ALU.add)
                    spacer()
                    v_scans[par] = V.inc(spacer())
                    VE.tensor_tensor(x2b[par][:, :], h_sb[:, :], Cbc[bpar][:, :],
                                     ALU.mult)
                    dp_ = x2b[par]
                    VE.tensor_tensor(dp_[:, 0:8 * TT], dp_[:, 0:8 * TT],
                                     dp_[:, 8 * TT:16 * TT], ALU.add)
                    VE.tensor_tensor(dp_[:, 8 * TT:12 * TT], dp_[:, 0:4 * TT],
                                     dp_[:, 4 * TT:8 * TT], ALU.add)
                    VE.tensor_tensor(dp_[:, 12 * TT:14 * TT], dp_[:, 8 * TT:10 * TT],
                                     dp_[:, 10 * TT:12 * TT], ALU.add)
                    VE.tensor_tensor(dp_[:, 14 * TT:15 * TT], dp_[:, 12 * TT:13 * TT],
                                     dp_[:, 13 * TT:14 * TT], ALU.add)
                    VE.scalar_tensor_tensor(y23[m][:, :], u_c2[m][:, tl0:tl0 + TT],
                                            dp_sb[:, m:m + 1], dp_[:, 14 * TT:15 * TT],
                                            ALU.mult, ALU.add)
                    VE.tensor_tensor(y23[m][:, :], y23[m][:, :],
                                     siluz2[m][:, tl0:tl0 + TT], ALU.mult)
                    v_tree[par] = V.inc(VE.tensor_copy(
                        carry[:, 16 * m:16 * (m + 1)],
                        h_sb[:, :].rearrange("p (s t) -> p s t", s=16)[:, :, TT - 1]))
                bc_guard[bpar] = [G.n, V.n]
                spacer()
                v_y = V.inc(spacer())
                TE.wait_ge(V.h, v_y)
                for mh in range(2):
                    for fn in range(4):
                        pb = pB[fn % 2]
                        TE.wait_ge(SC.h, SC.n)
                        for kd in range(4):
                            mmo = TE.matmul(
                                pb[:, :], y23[kd][:, 128 * mh:128 * (mh + 1)],
                                w_out_bf[:, DM * kd + 512 * fn:DM * kd + 512 * (fn + 1)],
                                start=(kd == 0), stop=(kd == 3))
                        pe_wout = PE.inc(mmo)
                        pe_wout2[tau % 2] = pe_wout
                        SE.wait_ge(PE.h, pe_wout)
                        if fn % 2 == 0:
                            SE.wait_ge(ST.h, ST.n)
                        SC.inc(SE.copy(oo_sb[:, 512 * (fn % 2):512 * (fn % 2 + 1)],
                                       pb[:, :]))
                        ST.inc(SE.dma_start(
                            out=rs_in[k][t0 + 128 * mh:t0 + 128 * (mh + 1),
                                         512 * fn:512 * (fn + 1)],
                            in_=oo_sb[:, 512 * (fn % 2):512 * (fn % 2 + 1)]), 16)
        if k == 0:
            tapw("y230", y23[0][:, :])
        GE.wait_ge(ST.h, ST.n)
        c_rs[k].inc(GE.collective_compute(
            "ReduceScatter", ALU.add, replica_groups=[list(range(NC))],
            ins=[rs_in[k].ap().opt()], outs=[rs_out[k].ap().opt()]))

    barrier()
    pm3.close()
    pwo.close()

    pfc = contextlib.ExitStack()
    fbns = sbuf("fbns", [128, 24], F32, pfc)
    fmv = sbuf("fmv", [128, 4], F32, pfc)
    fcf = sbuf("fcf", [128, 4], F32, pfc)

    pf12 = contextlib.ExitStack()
    xqT = sbuf("xqT", [128, 16 * TOK], BF16, pf12)
    fqs = {
        "v": sbuf("f_v", [128, DM], F32, pf12),
        "r1": sbuf("f_r1", [128, DM], F32, pf12),
        "q": sbuf("f_q", [128, DM], F32, pf12),
        "f8": sbuf("f_f8", [128, DM], FP8, pf12),
        "m2": sbuf("f_m2", [128, DM], U8, pf12),
        "am": sbuf("f_am", [128, 64], F32, pf12),
        "ie": sbuf("f_ie", [128, 64], I32, pf12),
        "ir": sbuf("f_ir", [128, 64], I32, pf12),
        "is": sbuf("f_is", [128, 64], I32, pf12),
    }

    pf1 = contextlib.ExitStack()
    xt2 = sbuf("xt2", [128, DM], F32, pf1)
    at2 = sbuf("at2", [128, DM], F32, pf1)
    x2t = sbuf("x2t", [128, DM], F32, pf1)
    x3n = sbuf("x3n", [128, DM], F32, pf1)
    xq_bf = sbuf("xq_bf", [128, DM], BF16, pf1)

    if "rs0" in tap_bufs:
        GE.wait_ge(c_rs[0].h, 1)
        ST.inc(GE.dma_start(out=tap_bufs["rs0"], in_=rs_out[0].ap()), 16)
    for i in range(16):
        kchunk, half = i // 2, i % 2
        SY.wait_ge(c_rs[kchunk].h, 1)
        SY.wait_ge(V.h, V.n)
        SY.wait_ge(SC.h, SC.n)
        nL = L_.inc(SY.dma_start(out=xt2[:, :],
                                 in_=x_in.rearrange("(i p) f -> i p f", p=128)[i]), 16)
        nL = L_.inc(SY.dma_start(out=at2[:, :],
                                 in_=ln1_spill.rearrange("(i p) f -> i p f", p=128)[i]), 16)
        nL = L_.inc(SY.dma_start(out=x2t[:, :],
                                 in_=rs_out[kchunk].rearrange("(h p) f -> h p f", p=128)[half]),
                    16)
        VE.wait_ge(L_.h, nL)
        VE.wait_ge(ST.h, ST.n)
        VE.tensor_tensor(at2[:, :], at2[:, :], x2t[:, :], ALU.add)
        vw = ln_stats(at2, fbns, fmv)
        vw = ln_coefs(fmv, fcf, vw)
        VE.tensor_scalar(x2t[:, :], at2[:, :], fcf[:, 0:1], fcf[:, 1:2],
                         ALU.mult, ALU.add)
        VE.tensor_tensor(x2t[:, :], x2t[:, :], xt2[:, :], ALU.add)
        spacer()
        vx2 = V.inc(spacer())
        SE.wait_ge(V.h, vx2)
        ST.inc(SE.dma_start(out=x2_spill.rearrange("(i p) f -> i p f", p=128)[i],
                            in_=x2t[:, :]), 16)
        vw = ln_stats(x2t, fbns, fmv)
        vw = ln_coefs(fmv, fcf, vw)
        VE.tensor_scalar(x3n[:, :], x2t[:, :], fcf[:, 0:1], fcf[:, 1:2],
                         ALU.mult, ALU.add)
        VE.wait_ge(PE.h, PE.n)
        vq = quant_a(x3n, xq_bf[:, :], fqs, DM)
        if i == 0:
            tapw("xq0", xq_bf[:, :])
        TE.wait_ge(V.h, vq)
        for b in range(2):
            TE.wait_ge(SC.h, SC.n)
            for j in range(8):
                kf = 8 * b + j
                mmt = TE.transpose(pTB[:, 128 * j:128 * (j + 1)],
                                   xq_bf[:, 128 * kf:128 * (kf + 1)], ident_bf[:, :])
            pe = PE.inc(mmt)
            SE.wait_ge(PE.h, pe)
            SC.inc(SE.copy(
                xqT[:, :].rearrange("p (a t) -> p a t", t=2048)[
                    :, 8 * b:8 * b + 8, 128 * i:128 * (i + 1)],
                pTB[:, :].rearrange("p (a t) -> p a t", t=128)))

    tapw("xqTv", xqT[:, 0:2048])
    barrier()
    pf1.close()

    if "x2sp" in tap_bufs:
        GE.wait_ge(ST.h, ST.n)
        ST.inc(GE.dma_start(out=tap_bufs["x2sp"], in_=x2_spill.ap()), 16)
    pf2 = contextlib.ExitStack()
    wgt = [sbuf("wgtA", [128, 16 * 512], BF16, pf2),
           sbuf("wgtB", [128, 16 * 512], BF16, pf2)]
    wut = [sbuf("wutA", [128, 16 * 512], BF16, pf2),
           sbuf("wutB", [128, 16 * 512], BF16, pf2)]
    sg_sb = [sbuf("sg_sbA", [128, 512], F32, pf2),
             sbuf("sg_sbB", [128, 512], F32, pf2)]
    inter = sbuf("inter", [128, 16 * 512], BF16, pf2)
    interq = inter
    iqev = [sbuf("iqevA", [128, 512], BF16, pf2), sbuf("iqevB", [128, 512], BF16, pf2)]

    SY.wait_ge(c_agwg.h, 1)
    SY.wait_ge(c_agwu.h, 1)
    sc_silu = [SC.n, SC.n]
    v_int = [V.n, V.n]

    def wgu_load(ic, wpar, pe_guard):
        r, icc = ic // 2, ic % 2
        SY.wait_ge(PE.h, pe_guard)
        n = L_.n
        for kf in range(16):
            n = L_.inc(SY.dma_start(
                out=wgt[wpar][:, 512 * kf:512 * (kf + 1)],
                in_=wgq_all.rearrange("(r a p) c -> r a p c", r=NC, p=128)[r][kf][
                    :, 512 * icc:512 * (icc + 1)]), 16)
            n = L_.inc(SY.dma_start(
                out=wut[wpar][:, 512 * kf:512 * (kf + 1)],
                in_=wuq_all.rearrange("(r a p) c -> r a p c", r=NC, p=128)[r][kf][
                    :, 512 * icc:512 * (icc + 1)]), 16)
        return n

    wgu_ready = [wgu_load(0, 0, 0), wgu_load(1, 1, 0)]
    pe_ic = [0, 0]
    for ic in range(16):
        wpar = ic % 2
        r, icc = ic // 2, ic % 2
        VE.wait_ge(PE.h, PE.n)
        nL = wgu_ready[wpar]
        TE.wait_ge(L_.h, nL)
        for mt in range(16):
            fpar = mt % 2
            pg, pu = pA[fpar], pB[fpar]
            TE.wait_ge(SC.h, sc_silu[fpar])
            TE.wait_ge(V.h, v_int[fpar])
            for kf in range(16):
                TE.matmul(pg[:, :],
                          xqT[:, 2048 * kf + 128 * mt:2048 * kf + 128 * (mt + 1)],
                          wgt[wpar][:, 512 * kf:512 * (kf + 1)],
                          start=(kf == 0), stop=(kf == 15))
            for kf in range(16):
                mmu = TE.matmul(pu[:, :],
                                xqT[:, 2048 * kf + 128 * mt:2048 * kf + 128 * (mt + 1)],
                                wut[wpar][:, 512 * kf:512 * (kf + 1)],
                                start=(kf == 0), stop=(kf == 15))
            pe = PE.inc(mmu)
            SE.wait_ge(PE.h, pe)
            scS = SC.inc(SE.activation(sg_sb[fpar][:, :], pg[:, :], AF.Silu))
            sc_silu[fpar] = scS
            VE.wait_ge(SC.h, scS)
            v_int[fpar] = V.inc(VE.tensor_tensor(
                inter[:, 512 * mt:512 * (mt + 1)], sg_sb[fpar][:, :],
                pu[:, :], ALU.mult))
        if ic == 0:
            tapw("wgt0", wgt[0][:, :])
            tq = tapw("intpre", inter[:, :])
            if tq:
                VE.wait_ge(ST.h, tq)
        pe_ic[wpar] = PE.n
        if ic + 2 < 16:
            wgu_ready[wpar] = wgu_load(ic + 2, wpar, pe_ic[wpar])
        for sub in range(4):
            vq = quant_a(inter[:, 2048 * sub:2048 * (sub + 1)],
                         interq[:, 2048 * sub:2048 * (sub + 1)], fqs, 2048)
        if ic == 0:
            tapw("iq0", interq[:, :])
        TE.wait_ge(V.h, vq)
        gI0 = r * 1024 + icc * 512
        st_iq = [ST.n, ST.n]
        for mt in range(16):
            par = mt % 2
            SE.wait_ge(ST.h, st_iq[par])
            TE.wait_ge(SC.h, SC.n)
            for sub in range(4):
                mmt = TE.transpose(
                    pTB[:, 128 * sub:128 * (sub + 1)],
                    interq[:, 512 * mt + 128 * sub:512 * mt + 128 * (sub + 1)],
                    ident_bf[:, :])
            pe = PE.inc(mmt)
            SE.wait_ge(PE.h, pe)
            SC.inc(SE.copy(iqev[par][:, :], pTB[:, 0:512]))
            SE.copy(junkB[:, 0:4], junkA[:, 0:4])
            st_iq[par] = ST.inc(SE.dma_start(
                out=ikt_spill.rearrange("(a p) t -> a p t", p=128)[
                    gI0 // 128:gI0 // 128 + 4][:, :, 128 * mt:128 * (mt + 1)]
                .transpose([1, 0, 2]),
                in_=iqev[par][:, :]), 16)

    barrier()
    pf2.close()
    pf12.close()

    pf3 = contextlib.ExitStack()
    ikt = [sbuf("iktA", [128, 64 * 128], BF16, pf3),
           sbuf("iktB", [128, 64 * 128], BF16, pf3)]
    wdt_s = [sbuf("wdt_sA", [128, 64 * 512], BF16, pf3),
             sbuf("wdt_sB", [128, 64 * 512], BF16, pf3)]
    dstage = [sbuf("dstgA", [128, 512], F32, pf3),
              sbuf("dstgB", [128, 512], F32, pf3)]
    dout_f = sbuf("dout_f", [128, DM], F32, pf3)
    ln4t = sbuf("ln4t", [128, DM], F32, pf3)
    xt3 = sbuf("xt3", [128, DM], F32, pf3)
    ot3 = sbuf("ot3a", [128, DM], F32, pf3)

    SY.wait_ge(c_agwd.h, 1)

    def wd_load(fb2, par, pe_guard):
        SY.wait_ge(PE.h, pe_guard)
        nW = L_.n
        for half in range(2):
            fb = 2 * fb2 + half
            for kg in range(16):
                nW = L_.inc(SY.dma_start(
                    out=wdt_s[par][:, 2048 * kg:2048 * (kg + 1)].rearrange(
                        "p (kk c) -> p kk c", c=512)[:, :, 256 * half:256 * (half + 1)],
                    in_=wdq_all.rearrange("(r kk p) c -> r p kk c", r=NC, p=128)[fb][
                        :, 4 * kg:4 * (kg + 1), :]), 16)
        return nW

    def ikt_load(mt, par, pe_guard):
        SY.wait_ge(PE.h, pe_guard)
        nL = L_.n
        for kg in range(16):
            nL = L_.inc(SY.dma_start(
                out=ikt[par][:, 512 * kg:512 * (kg + 1)].rearrange(
                    "p (kk t) -> p kk t", t=128),
                in_=ikt_spill.rearrange("(kk p) t -> p kk t", p=128)[
                    :, 4 * kg:4 * (kg + 1), 128 * mt:128 * (mt + 1)]), 16)
        return nL

    wd_ready = [wd_load(0, 0, 0), wd_load(1, 1, 0)]
    ikt_ready = [ikt_load(0, 0, 0), ikt_load(1, 1, 0)]
    st_do = [ST.n, ST.n]
    for fb2 in range(4):
        wpar = fb2 % 2
        for mt in range(16):
            ipar = mt % 2
            TE.wait_ge(L_.h, max(ikt_ready[ipar], wd_ready[wpar]))
            pb = pB[mt % 2]
            TE.wait_ge(SC.h, SC.n)
            TE.wait_ge(V.h, V.n)
            for kk in range(64):
                mmd = TE.matmul(pb[:, :], ikt[ipar][:, 128 * kk:128 * (kk + 1)],
                                wdt_s[wpar][:, 512 * kk:512 * (kk + 1)],
                                start=(kk == 0), stop=(kk == 63))
            mjunk = TE.matmul(pR[:, 0:128], onesbf[:, :], onesbf[:, :],
                              start=True, stop=True)
            pe = PE.inc(mjunk)
            nxt = mt + 2
            if nxt < 16:
                ikt_ready[ipar] = ikt_load(nxt, ipar, pe)
            elif fb2 < 3:
                ikt_ready[ipar] = ikt_load(nxt - 16, ipar, pe)
            if mt == 0 and fb2 in (1, 2):
                wd_ready[(fb2 + 1) % 2] = wd_load(fb2 + 1, (fb2 + 1) % 2, pe)
            VE.wait_ge(PE.h, pe)
            VE.wait_ge(ST.h, st_do[ipar])
            spacer()
            V.inc(VE.tensor_copy(dstage[ipar][:, :], pb[:, :]))
            spacer()
            spacer()
            vo = V.inc(spacer())
            SE.wait_ge(V.h, vo)
            st_do[ipar] = ST.inc(SE.dma_start(
                out=dout_spill.rearrange("(i p) f -> i p f", p=128)[mt][
                    :, 512 * fb2:512 * (fb2 + 1)],
                in_=dstage[ipar][:, :]), 16)

    SY.wait_ge(ST.h, ST.n)
    st_out = ST.n
    for mt in range(16):
        SY.wait_ge(V.h, V.n)
        nL2 = L_.inc(SY.dma_start(out=dout_f[:, :],
                                  in_=dout_spill.rearrange("(i p) f -> i p f", p=128)[mt]), 16)
        nL2 = L_.inc(SY.dma_start(out=xt3[:, :],
                                  in_=x2_spill.rearrange("(i p) f -> i p f", p=128)[mt]), 16)
        VE.wait_ge(L_.h, nL2)
        vw = ln_stats(dout_f, fbns, fmv)
        vw = ln_coefs(fmv, fcf, vw)
        VE.tensor_scalar(ln4t[:, :], dout_f[:, :], fcf[:, 0:1], fcf[:, 1:2],
                         ALU.mult, ALU.add)
        VE.wait_ge(ST.h, st_out)
        VE.tensor_tensor(ot3[:, :], xt3[:, :], ln4t[:, :], ALU.add)
        spacer()
        spacer()
        vo = V.inc(spacer())
        SE.wait_ge(V.h, vo)
        st_out = ST.inc(SE.dma_start(
            out=out_own.rearrange("(i p) f -> i p f", p=128)[mt],
            in_=ot3[:, :]), 16)

    SY.wait_ge(ST.h, ST.n)
    GE.wait_ge(ST.h, ST.n)

    pf3.close()
    pfc.close()
    es.close()
    return nc


def _slice_inputs(inputs, c):
    x = np.asarray(inputs["L0_attn_x"]).reshape(NT, DM)
    stripes = [x[(k * NC + c) * STR:(k * NC + c) * STR + STR] for k in range(NCHUNK)]
    x_own = np.ascontiguousarray(np.concatenate(stripes, axis=0))
    W_in = np.asarray(inputs["L0_attn_W_in"])
    w_in_sl = np.ascontiguousarray(np.concatenate(
        [W_in[:, c * DOWN:(c + 1) * DOWN], W_in[:, DI + c * DOWN:DI + (c + 1) * DOWN]],
        axis=1))
    cw = np.asarray(inputs["L0_attn_conv_w"])[c * DOWN:(c + 1) * DOWN, 0, :]
    convw = np.ascontiguousarray(cw.reshape(4, 128, 4).transpose(1, 0, 2).reshape(128, 16))
    convb = np.ascontiguousarray(
        np.asarray(inputs["L0_attn_conv_b"])[c * DOWN:(c + 1) * DOWN].reshape(4, 128).T)
    wx_sl = np.ascontiguousarray(np.asarray(inputs["L0_attn_W_x"])[c * DOWN:(c + 1) * DOWN])
    wdt_sl = np.ascontiguousarray(
        np.asarray(inputs["L0_attn_W_dt"])[:, c * DOWN:(c + 1) * DOWN])
    dtb = np.ascontiguousarray(
        np.asarray(inputs["L0_attn_dt_bias"])[c * DOWN:(c + 1) * DOWN].reshape(4, 128).T)
    alog = np.ascontiguousarray(
        np.asarray(inputs["L0_attn_A_log"])[c * DOWN:(c + 1) * DOWN].reshape(4, 128, 16)
        .transpose(1, 0, 2).reshape(128, 64))
    dp = np.ascontiguousarray(
        np.asarray(inputs["L0_attn_Dp"])[c * DOWN:(c + 1) * DOWN].reshape(4, 128).T)
    wout_sl = np.ascontiguousarray(np.asarray(inputs["L0_attn_W_out"])[c * DOWN:(c + 1) * DOWN])
    wg_sl = np.ascontiguousarray(np.asarray(inputs["L0_ffn_w_gate"])[c * 1024:(c + 1) * 1024])
    wu_sl = np.ascontiguousarray(np.asarray(inputs["L0_ffn_w_up"])[c * 1024:(c + 1) * 1024])
    wd_sl = np.ascontiguousarray(np.asarray(inputs["L0_ffn_w_down"])[c * 256:(c + 1) * 256])
    return dict(
        x_own=x_own.astype(np.float32),
        ident=np.eye(128, dtype=np.float32),
        rmsw=np.asarray(inputs["L0_attn_rms_w"]).reshape(1, DM).astype(np.float32),
        w_in_sl=w_in_sl.astype(np.float32), convw=convw.astype(np.float32),
        convb=convb.astype(np.float32), wx_sl=wx_sl.astype(np.float32),
        wdt_sl=wdt_sl.astype(np.float32), dtb=dtb.astype(np.float32),
        alog=alog.astype(np.float32), dp=dp.astype(np.float32),
        wout_sl=wout_sl.astype(np.float32), wg_sl=wg_sl.astype(np.float32),
        wu_sl=wu_sl.astype(np.float32), wd_sl=wd_sl.astype(np.float32),
    )


def kernel(**inputs):
    global _LAST_TAPS
    nc = build(taps=_DEBUG_TAPS)
    in_maps = [_slice_inputs(inputs, c) for c in range(NC)]
    res = run_bass_kernel_spmd(nc, in_maps, core_ids=list(range(NC)))
    _LAST_TAPS = {}
    for tname, _, _ in _DEBUG_TAPS:
        _LAST_TAPS[tname] = [r["tap_" + tname] for r in res.results]
    out = np.zeros((NT, DM), dtype=np.float32)
    for c in range(NC):
        o = res.results[c]["out_own"]
        for k in range(NCHUNK):
            out[(k * NC + c) * STR:(k * NC + c) * STR + STR] = o[k * STR:(k + 1) * STR]
    return out.reshape(B, L, DM)


# version 09
# speedup vs baseline: 1.0886x; 1.0124x over previous
import contextlib
import numpy as np
import concourse.bass as bass
import concourse.mybir as mybir
from concourse.bass_utils import run_bass_kernel_spmd

F32 = mybir.dt.float32
BF16 = mybir.dt.bfloat16
U8 = mybir.dt.uint8
I32 = mybir.dt.int32
FP8 = mybir.dt.float8e4
AF = mybir.ActivationFunctionType
ALU = mybir.AluOpType

NC = 8
B, L, DM = 2, 8192, 2048
NT = B * L
DI, DS, DR, KC = 4096, 16, 128, 4
IF = 8192
DOWN = DI // NC
TOK = NT // NC
STR = 256
NCHUNK = 8
CT = 2048
TT = 256
NTAU = CT // TT
EPS = 1e-5

S_8TH = 1.5 * 2.0 ** 20
S_HALF = 1.5 * 2.0 ** 22
S_ONE = 1.5 * 2.0 ** 23
S_TWO = 1.5 * 2.0 ** 24

_DEBUG_TAPS = ()
_LAST_TAPS = {}


class Sem:
    def __init__(self, nc, es, name):
        self.h = es.enter_context(nc.semaphore(name))
        self.n = 0

    def inc(self, instr, amt=1):
        instr.then_inc(self.h, amt)
        self.n += amt
        return self.n


def build(taps=()):
    nc = bass.Bass("TRN2", target_bir_lowering=False,
                   detect_race_conditions=False)
    es = contextlib.ExitStack()

    def din(name, shape, dt=F32):
        return nc.dram_tensor(name, shape, dt, kind="ExternalInput").ap()

    def dout(name, shape, dt=F32):
        return nc.dram_tensor(name, shape, dt, kind="ExternalOutput").ap()

    def dram(name, shape, dt=F32):
        return nc.dram_tensor(name, shape, dt)

    x_in = din("x_own", [TOK, DM])
    ident_in = din("ident", [128, 128])
    rmsw_in = din("rmsw", [1, DM])
    w_in_sl = din("w_in_sl", [DM, 1024])
    convw_in = din("convw", [128, 16])
    convb_in = din("convb", [128, 4])
    wx_in = din("wx_sl", [DOWN, 160])
    wdt_in = din("wdt_sl", [DR, DOWN])
    dtb_in = din("dtb", [128, 4])
    alog_in = din("alog", [128, 64])
    dp_in = din("dp", [128, 4])
    wout_in = din("wout_sl", [DOWN, DM])
    wg_in = din("wg_sl", [IF // NC, DM])
    wu_in = din("wu_sl", [IF // NC, DM])
    wd_in = din("wd_sl", [DM // NC, IF])

    out_own = dout("out_own", [TOK, DM])

    hn_in = dram("hn_in", [DM, TOK], BF16)
    hn_all = dram("hn_all", [NC * DM, TOK], BF16)
    ln1_spill = dram("ln1_spill", [TOK, DM], F32)
    x2_spill = dram("x2_spill", [TOK, DM], F32)
    ar_in = [dram(f"ar_in{k}", [160, CT], F32) for k in range(NCHUNK)]
    ar_out = [dram(f"ar_out{k}", [160, CT], F32) for k in range(NCHUNK)]
    rs_in = [dram(f"rs_in{k}", [CT, DM], F32) for k in range(NCHUNK)]
    rs_out = [dram(f"rs_out{k}", [STR, DM], F32) for k in range(NCHUNK)]
    wgq_in = dram("wgq_in", [DM, 1024], BF16)
    wgq_all = dram("wgq_all", [NC * DM, 1024], BF16)
    wuq_in = dram("wuq_in", [DM, 1024], BF16)
    wuq_all = dram("wuq_all", [NC * DM, 1024], BF16)
    wdq_in = dram("wdq_in", [IF, DM // NC], BF16)
    wdq_all = dram("wdq_all", [NC * IF, DM // NC], BF16)
    ikt_spill = dram("ikt_spill", [IF, TOK], BF16)
    dout_spill = dram("dout_spill", [TOK, DM], F32)
    uc_spill = dram("uc_spill", [NCHUNK * 512, CT], BF16)
    sz_spill = dram("sz_spill", [NCHUNK * 512, CT], BF16)

    tap_bufs = {}
    for tname, tshape, tdt in taps:
        tap_bufs[tname] = dout("tap_" + tname, list(tshape), tdt)

    L_ = Sem(nc, es, "Lq")
    ST = Sem(nc, es, "STq")
    V = Sem(nc, es, "Vm")
    SC = Sem(nc, es, "SCm")
    PE = Sem(nc, es, "PEm")
    G = Sem(nc, es, "Gm")
    c_aghn = Sem(nc, es, "c_aghn")
    c_agwg = Sem(nc, es, "c_agwg")
    c_agwu = Sem(nc, es, "c_agwu")
    c_agwd = Sem(nc, es, "c_agwd")
    c_ar = [Sem(nc, es, f"c_ar{k}") for k in range(NCHUNK)]
    c_rs = [Sem(nc, es, f"c_rs{k}") for k in range(NCHUNK)]

    def sbuf(name, shape, dt, stack=None):
        return (stack or es).enter_context(nc.sbuf_tensor(name, shape, dt))

    def psum(name, shape, dt=F32):
        return es.enter_context(nc.psum_tensor(name, shape, dt))

    w_x_bf = sbuf("w_x_bf", [128, 4 * 160], BF16)
    w_dt_bf = sbuf("w_dt_bf", [128, DOWN], BF16)
    A_sb = sbuf("A_sb", [128, 64], F32)
    negA_sb = sbuf("negA_sb", [128, 64], F32)
    convw_sb = sbuf("convw_sb", [128, 16], F32)
    convb_sb = sbuf("convb_sb", [128, 4], F32)
    dtb_sb = sbuf("dtb_sb", [128, 4], F32)
    dp_sb = sbuf("dp_sb", [128, 4], F32)
    onesbf = sbuf("onesbf", [1, 128], BF16)
    ones32 = sbuf("ones32", [1, 128], F32)
    ident_bf = sbuf("ident_bf", [128, 128], BF16)
    junkA = sbuf("junkA", [128, 8], F32)
    junkB = sbuf("junkB", [128, 8], F32)
    consts = sbuf("consts", [128, 4], F32)

    pA = [psum("pA0", [128, 512]), psum("pA1", [128, 512])]
    pB = [psum("pB0", [128, 512]), psum("pB1", [128, 512])]
    pT = [psum("pT0", [128, 128], BF16), psum("pT1", [128, 128], BF16)]
    pTB = psum("pTB", [128, 1024], BF16)
    pR = psum("pR", [128, 512])

    block = es.enter_context(nc.Block())
    SY, VE, SE, TE, GE = nc.sync, nc.vector, nc.scalar, nc.tensor, nc.gpsimd

    def spacer():
        return VE.tensor_copy(junkA[:, :], junkB[:, :])

    def barrier():
        vals = [(V.h, V.n), (SC.h, SC.n), (PE.h, PE.n), (G.h, G.n),
                (L_.h, L_.n), (ST.h, ST.n)]
        for e in (SY, VE, SE, TE, GE):
            for h, n in vals:
                if n > 0:
                    e.wait_ge(h, n)

    def ln_stats(src, bns, mv):
        for j in range(4):
            VE.bn_stats(bns[:, 6 * j:6 * j + 6], src[:, 512 * j:512 * (j + 1)])
        spacer()
        VE.bn_aggr(mv[:, 0:2], bns[:, 0:24])
        spacer()
        return V.inc(spacer())

    def ln_coefs(mv, cf, vwait):
        SE.wait_ge(V.h, vwait)
        SE.activation(cf[:, 2:3], mv[:, 1:2], AF.Ln, bias=float(EPS))
        sc = SC.inc(SE.activation(cf[:, 0:1], cf[:, 2:3], AF.Exp, scale=-0.5))
        VE.wait_ge(SC.h, sc)
        VE.scalar_tensor_tensor(cf[:, 1:2], mv[:, 0:1], -1.0, cf[:, 0:1],
                                ALU.mult, ALU.mult)
        spacer()
        return V.inc(spacer())

    def _block_scales(s, src_am_ap, nb):
        VE.tensor_scalar(s["ie"][:, 0:nb], src_am_ap.bitcast(I32), 23, None,
                         ALU.logical_shift_right, ALU.bypass)
        spacer()
        VE.tensor_scalar(s["ie"][:, 0:nb], s["ie"][:, 0:nb], 129, None,
                         ALU.subtract, ALU.bypass)
        spacer()
        VE.tensor_scalar(s["ie"][:, 0:nb], s["ie"][:, 0:nb], -127, 127,
                         ALU.max, ALU.min)
        spacer()
        VE.tensor_scalar(s["ir"][:, 0:nb], s["ie"][:, 0:nb], -1, 127,
                         ALU.mult, ALU.add)
        VE.tensor_scalar(s["is"][:, 0:nb], s["ie"][:, 0:nb], 127, None,
                         ALU.add, ALU.bypass)
        spacer()
        VE.tensor_scalar(s["ir"][:, 0:nb], s["ir"][:, 0:nb], 23, None,
                         ALU.logical_shift_left, ALU.bypass)
        VE.tensor_scalar(s["is"][:, 0:nb], s["is"][:, 0:nb], 23, None,
                         ALU.logical_shift_left, ALU.bypass)
        spacer()

    def quant_w(wt, qbf, s, Fw):
        nb = Fw // 32
        VE.tensor_reduce(s["am"][:, 0:nb],
                         wt[:, 0:Fw].rearrange("p (b e) -> p b e", e=32),
                         axis=mybir.AxisListType.X, op=ALU.max,
                         apply_absolute_value=True)
        spacer()
        _block_scales(s, s["am"][:, 0:nb], nb)
        VE.tensor_tensor(s["v"][:, 0:Fw].rearrange("p (b e) -> p b e", e=32),
                         wt[:, 0:Fw].rearrange("p (b e) -> p b e", e=32),
                         s["ir"][:, 0:nb].bitcast(F32).unsqueeze(2)
                         .broadcast_to([128, nb, 32]), ALU.mult)
        VE.tensor_scalar(s["q"][:, 0:Fw].bitcast(I32), s["v"][:, 0:Fw].bitcast(I32),
                         0x7FFFFFFF, None, ALU.bitwise_and, ALU.bypass)
        VE.tensor_scalar(s["r05"][:, 0:Fw], s["v"][:, 0:Fw], S_HALF, S_HALF,
                         ALU.add, ALU.subtract)
        VE.tensor_scalar(s["r1"][:, 0:Fw], s["v"][:, 0:Fw], S_ONE, S_ONE,
                         ALU.add, ALU.subtract)
        VE.tensor_scalar(s["r2c"][:, 0:Fw], s["v"][:, 0:Fw], S_TWO, S_TWO,
                         ALU.add, ALU.subtract)
        VE.tensor_scalar(s["m2"][:, 0:Fw], s["q"][:, 0:Fw].bitcast(I32), 0x40000000,
                         None, ALU.is_lt, ALU.bypass)
        VE.tensor_scalar(s["m4"][:, 0:Fw], s["q"][:, 0:Fw].bitcast(I32), 0x40800000,
                         None, ALU.is_lt, ALU.bypass)
        VE.tensor_copy(s["q"][:, 0:Fw], s["r2c"][:, 0:Fw])
        VE.copy_predicated(s["q"][:, 0:Fw], s["m4"][:, 0:Fw], s["r1"][:, 0:Fw])
        VE.copy_predicated(s["q"][:, 0:Fw], s["m2"][:, 0:Fw], s["r05"][:, 0:Fw])
        VE.tensor_scalar(s["q"][:, 0:Fw], s["q"][:, 0:Fw], 6.0, -6.0, ALU.min, ALU.max)
        VE.tensor_tensor(qbf[:, 0:Fw].rearrange("p (b e) -> p b e", e=32),
                         s["q"][:, 0:Fw].rearrange("p (b e) -> p b e", e=32),
                         s["is"][:, 0:nb].bitcast(F32).unsqueeze(2)
                         .broadcast_to([128, nb, 32]), ALU.mult)
        return V.inc(spacer())

    def quant_a(at, qbf_ap, s, Fw):
        nb = Fw // 32
        VE.tensor_reduce(s["am"][:, 0:nb],
                         at[:, 0:Fw].rearrange("p (b e) -> p b e", e=32),
                         axis=mybir.AxisListType.X, op=ALU.max,
                         apply_absolute_value=True)
        spacer()
        _block_scales(s, s["am"][:, 0:nb], nb)
        VE.tensor_tensor(s["v"][:, 0:Fw].rearrange("p (b e) -> p b e", e=32),
                         at[:, 0:Fw].rearrange("p (b e) -> p b e", e=32),
                         s["ir"][:, 0:nb].bitcast(F32).unsqueeze(2)
                         .broadcast_to([128, nb, 32]), ALU.mult)
        VE.tensor_copy(s["f8"][:, 0:Fw], s["v"][:, 0:Fw])
        VE.tensor_scalar(s["q"][:, 0:Fw].bitcast(I32), s["v"][:, 0:Fw].bitcast(I32),
                         0x7FFFFFFF, None, ALU.bitwise_and, ALU.bypass)
        VE.tensor_scalar(s["r1"][:, 0:Fw], s["v"][:, 0:Fw], S_8TH, S_8TH,
                         ALU.add, ALU.subtract)
        VE.tensor_scalar(s["m2"][:, 0:Fw], s["q"][:, 0:Fw].bitcast(I32), 0x3F800000,
                         None, ALU.is_lt, ALU.bypass)
        VE.tensor_copy(s["q"][:, 0:Fw], s["f8"][:, 0:Fw])
        VE.copy_predicated(s["q"][:, 0:Fw], s["m2"][:, 0:Fw], s["r1"][:, 0:Fw])
        VE.tensor_scalar(s["q"][:, 0:Fw], s["q"][:, 0:Fw], 7.5, -7.5, ALU.min, ALU.max)
        VE.tensor_tensor(qbf_ap.rearrange("p (b e) -> p b e", e=32),
                         s["q"][:, 0:Fw].rearrange("p (b e) -> p b e", e=32),
                         s["is"][:, 0:nb].bitcast(F32).unsqueeze(2)
                         .broadcast_to([128, nb, 32]), ALU.mult)
        return V.inc(spacer())


    def transpose8(src_ap_fn, n_tiles, evac_fn):
        for b0 in range(0, n_tiles, 8):
            nb_ = min(8, n_tiles - b0)
            TE.wait_ge(SC.h, SC.n)
            TE.wait_ge(V.h, V.n)
            for j in range(nb_):
                mmt = TE.transpose(pTB[:, 128 * j:128 * (j + 1)], src_ap_fn(b0 + j),
                                   ident_bf[:, :])
            pe = PE.inc(mmt)
            SE.wait_ge(PE.h, pe)
            evac_fn(b0 // 8, pTB[:, 0:128 * nb_])

    def tapw(name, src_ap):
        if name in tap_bufs:
            SE.wait_ge(V.h, V.n)
            SE.wait_ge(SC.h, SC.n)
            return ST.inc(SE.dma_start(out=tap_bufs[name], in_=src_ap), 16)
        return None

    pwo = contextlib.ExitStack()
    pwi = contextlib.ExitStack()
    w_out_bf = sbuf("w_out_bf", [128, 4 * DM], BF16, pwo)
    w_in_bf = sbuf("w_in_bf", [128, 16 * 1024], BF16, pwi)

    p0 = contextlib.ExitStack()
    ident_f = sbuf("ident_f", [128, 128], F32, p0)
    rmsw_row = sbuf("rmsw_row", [1, DM], F32, p0)
    rmsw_bc = sbuf("rmsw_bc", [128, DM], F32, p0)

    nL = L_.inc(SY.dma_start(out=ident_f[:, :], in_=ident_in), 16)
    nL = L_.inc(SY.dma_start(out=convw_sb[:, :], in_=convw_in), 16)
    nL = L_.inc(SY.dma_start(out=convb_sb[:, :], in_=convb_in), 16)
    nL = L_.inc(SY.dma_start(out=dtb_sb[:, :], in_=dtb_in), 16)
    nL = L_.inc(SY.dma_start(out=dp_sb[:, :], in_=dp_in), 16)
    nL = L_.inc(SY.dma_start(out=rmsw_row[:, :], in_=rmsw_in), 16)
    nc.const_aps.register(F32, float(EPS), consts[:, 0:1])
    nc.const_aps.register(F32, 1.0, consts[:, 1:2])
    VE.wait_ge(L_.h, nL)
    VE.memset(consts[:, 0:1], float(EPS))
    VE.memset(consts[:, 1:2], 1.0)
    VE.memset(onesbf[:, :], 1.0)
    VE.memset(ones32[:, :], 1.0)
    VE.memset(junkB[:, :], 0.0)
    vE = V.inc(VE.tensor_copy(ident_bf[:, :], ident_f[:, :]))
    SY.wait_ge(V.h, vE)
    nLA = L_.inc(SY.dma_start(out=ident_f[:, 0:64], in_=alog_in), 16)
    SE.wait_ge(L_.h, nLA)
    SC.inc(SE.activation(A_sb[:, :], ident_f[:, 0:64], AF.Exp))
    VE.wait_ge(SC.h, SC.n)
    V.inc(VE.tensor_scalar(negA_sb[:, :], A_sb[:, :], -1.0, None, ALU.mult, ALU.bypass))

    TE.wait_ge(V.h, vE)
    for j in range(4):
        pe = PE.inc(TE.matmul(pR[:, :], ones32[:, :], rmsw_row[:, 512 * j:512 * (j + 1)],
                              start=True, stop=True))
        SE.wait_ge(PE.h, pe)
        scR = SC.inc(SE.copy(rmsw_bc[:, 512 * j:512 * (j + 1)], pR[:, :]))
        TE.wait_ge(SC.h, scR)

    xt = sbuf("xt", [128, DM], F32, p0)
    ln1t = sbuf("ln1t", [128, DM], F32, p0)
    hnt = sbuf("hnt", [128, DM], BF16, p0)
    hnev = sbuf("hnev", [128, 1024], BF16, p0)
    bns = sbuf("bns", [128, 24], F32, p0)
    mv = sbuf("mv", [128, 4], F32, p0)
    cf = sbuf("cf", [128, 4], F32, p0)

    st_hn = 0
    for i in range(16):
        SY.wait_ge(V.h, V.n)
        SY.wait_ge(SC.h, SC.n)
        nL = L_.inc(SY.dma_start(out=xt[:, :],
                                 in_=x_in.rearrange("(i p) f -> i p f", p=128)[i]), 16)
        VE.wait_ge(L_.h, nL)
        vw = ln_stats(xt, bns, mv)
        vw = ln_coefs(mv, cf, vw)
        VE.wait_ge(ST.h, ST.n)
        VE.tensor_scalar(ln1t[:, :], xt[:, :], cf[:, 0:1], cf[:, 1:2],
                         ALU.mult, ALU.add)
        spacer()
        vc1 = V.inc(spacer())
        vw2 = ln_stats(ln1t, bns, mv)
        SE.wait_ge(V.h, vw2)
        SE.activation(cf[:, 2:3], mv[:, 1:2], AF.Ln, bias=float(EPS))
        scg = SC.inc(SE.activation(cf[:, 2:3], cf[:, 2:3], AF.Exp, scale=-0.5))
        VE.wait_ge(SC.h, scg)
        VE.wait_ge(PE.h, PE.n)
        VE.scalar_tensor_tensor(hnt[:, :], ln1t[:, :], cf[:, 2:3],
                                rmsw_bc[:, :], ALU.mult, ALU.mult)
        spacer()
        vh = V.inc(spacer())
        SE.wait_ge(V.h, vc1)
        st_l = ST.inc(SE.dma_start(out=ln1_spill.rearrange("(i p) f -> i p f", p=128)[i],
                                   in_=ln1t[:, :]), 16)
        if i == 0:
            tapw("ln1", ln1t[:, :])
            tapw("hnt", hnt[:, :])
        TE.wait_ge(V.h, vh)
        for b in range(2):
            TE.wait_ge(SC.h, SC.n)
            for j in range(8):
                mmt = TE.transpose(pTB[:, 128 * j:128 * (j + 1)],
                                   hnt[:, 128 * (8 * b + j):128 * (8 * b + j + 1)],
                                   ident_bf[:, :])
            pe = PE.inc(mmt)
            SE.wait_ge(PE.h, pe)
            SE.wait_ge(ST.h, st_hn)
            SC.inc(SE.copy(hnev[:, :], pTB[:, :]))
            SE.copy(junkB[:, 0:4], junkA[:, 0:4])
            st_hn = ST.inc(SE.dma_start(
                out=hn_in.rearrange("(a p) t -> a p t", p=128)[8 * b:8 * b + 8][
                    :, :, 128 * i:128 * (i + 1)].transpose([1, 0, 2]),
                in_=hnev[:, :]), 16)

    GE.wait_ge(ST.h, st_hn)
    c_aghn.inc(GE.collective_compute(
        "AllGather", ALU.bypass, replica_groups=[list(range(NC))],
        ins=[hn_in.ap().opt()], outs=[hn_all.ap().opt()]))

    wstage = sbuf("wstage", [128, DM], F32, p0)

    def load_cast(dst_ap, src_ap, width):
        SY.wait_ge(SC.h, SC.n)
        nL = L_.inc(SY.dma_start(out=wstage[:, 0:width], in_=src_ap), 16)
        SE.wait_ge(L_.h, nL)
        return SC.inc(SE.copy(dst_ap, wstage[:, 0:width]))

    for kk in range(16):
        load_cast(w_in_bf[:, 1024 * kk:1024 * (kk + 1)],
                  w_in_sl.rearrange("(a p) c -> a p c", p=128)[kk], 1024)
    for kd in range(4):
        load_cast(w_out_bf[:, DM * kd:DM * (kd + 1)],
                  wout_in.rearrange("(a p) c -> a p c", p=128)[kd], DM)
    for kd in range(4):
        load_cast(w_x_bf[:, 160 * kd:160 * (kd + 1)],
                  wx_in.rearrange("(a p) c -> a p c", p=128)[kd], 160)
    load_cast(w_dt_bf[:, :], wdt_in, DOWN)

    barrier()
    p0.close()

    pm1 = contextlib.ExitStack()
    rhs_sb = sbuf("rhs_sb", [128, 16 * 512], BF16, pm1)
    u_ext = [sbuf(f"u_ext{m}", [128, 515], BF16, pm1) for m in range(4)]
    halo = [sbuf(f"halo{m}", [128, 3], BF16, pm1) for m in range(4)]
    cacc = [sbuf("cacc0", [128, 512], F32, pm1), sbuf("cacc1", [128, 512], F32, pm1)]
    u_c = [sbuf(f"u_c{m}", [128, CT], BF16, pm1) for m in range(4)]
    siluz = [sbuf(f"siluz{m}", [128, CT], BF16, pm1) for m in range(4)]
    dtev = sbuf("dtev", [128, 512], F32, pm1)

    st_spill = 0
    sc_bank = [SC.n, SC.n]
    for k in range(NCHUNK):
        if k % 4 == 0:
            vhz = V.n
            for m in range(4):
                vhz = V.inc(VE.memset(halo[m][:, :], 0.0))
        for n in range(4):
            SY.wait_ge(PE.h, PE.n)
            SY.wait_ge(c_aghn.h, 1)
            nL = L_.n
            for kk in range(16):
                for half in range(2):
                    r = 2 * n + half
                    nL = L_.inc(SY.dma_start(
                        out=rhs_sb[:, 512 * kk + 256 * half:512 * kk + 256 * (half + 1)],
                        in_=hn_all.rearrange("(r a p) t -> r a p t", r=NC, p=128)[r][kk][
                            :, STR * k:STR * (k + 1)]), 16)
            TE.wait_ge(L_.h, nL)
            for m in range(8):
                pa = pA[m % 2]
                TE.wait_ge(SC.h, sc_bank[m % 2])
                for kk in range(16):
                    mm = TE.matmul(
                        pa[:, :],
                        w_in_bf[:, 1024 * kk + 128 * m:1024 * kk + 128 * (m + 1)],
                        rhs_sb[:, 512 * kk:512 * (kk + 1)],
                        start=(kk == 0), stop=(kk == 15))
                pe = PE.inc(mm)
                SE.wait_ge(PE.h, pe)
                if m < 4:
                    SE.wait_ge(V.h, V.n)
                    if n == 0:
                        SE.wait_ge(ST.h, st_spill)
                    sc1 = SC.inc(SE.copy(u_ext[m][:, 3:515], pa[:, :]))
                    sc_bank[m % 2] = sc1
                    VE.wait_ge(SC.h, sc1)
                    VE.tensor_copy(u_ext[m][:, 0:3], halo[m][:, :])
                    VE.tensor_scalar(cacc[0][:, :], u_ext[m][:, 0:512],
                                     convw_sb[:, 4 * m:4 * m + 1], None,
                                     ALU.mult, ALU.bypass)
                    for j in range(1, 4):
                        srcb, dstb = (cacc[0], cacc[1]) if j % 2 == 1 else (cacc[1], cacc[0])
                        VE.scalar_tensor_tensor(dstb[:, :], u_ext[m][:, j:j + 512],
                                                convw_sb[:, 4 * m + j:4 * m + j + 1],
                                                srcb[:, :], ALU.mult, ALU.add)
                    vcv = V.inc(VE.tensor_copy(halo[m][:, :], u_ext[m][:, 512:515]))
                    SE.wait_ge(V.h, vcv)
                    SC.inc(SE.activation(u_c[m][:, 512 * n:512 * (n + 1)], cacc[1][:, :],
                                         AF.Silu, bias=convb_sb[:, m:m + 1]))
                else:
                    if n == 0:
                        SE.wait_ge(ST.h, st_spill)
                    sc_bank[m % 2] = SC.inc(SE.activation(
                        siluz[m - 4][:, 512 * n:512 * (n + 1)], pa[:, :], AF.Silu))
            TE.wait_ge(SC.h, SC.n)
            for kd in range(4):
                TE.matmul(pB[0][:, :], w_x_bf[:, 160 * kd:160 * kd + 128],
                          u_c[kd][:, 512 * n:512 * (n + 1)],
                          start=(kd == 0), stop=(kd == 3))
                mm2 = TE.matmul(pB[1][0:32, :], w_x_bf[:, 160 * kd + 128:160 * (kd + 1)],
                                u_c[kd][:, 512 * n:512 * (n + 1)],
                                start=(kd == 0), stop=(kd == 3))
            pe = PE.inc(mm2)
            SE.wait_ge(PE.h, pe)
            SC.inc(SE.copy(dtev[:, :], pB[0][:, :]))
            st_dt = ST.inc(SE.dma_start(out=ar_in[k][0:128, 512 * n:512 * (n + 1)],
                                        in_=dtev[:, :]), 16)
            SE.wait_ge(ST.h, st_dt)
            SC.inc(SE.copy(dtev[0:32, :], pB[1][0:32, :]))
            st1 = ST.inc(SE.dma_start(out=ar_in[k][128:160, 512 * n:512 * (n + 1)],
                                      in_=dtev[0:32, :]), 16)
            SE.wait_ge(ST.h, st1)
        if k == 0:
            tapw("uc0", u_c[0][:, :])
            tapw("sz0", siluz[0][:, :])
        SE.wait_ge(SC.h, SC.n)
        for m in range(4):
            ST.inc(SE.dma_start(
                out=uc_spill.rearrange("(k m p) t -> k m p t", k=NCHUNK, p=128)[k][m],
                in_=u_c[m][:, :]), 16)
            st_spill = ST.inc(SE.dma_start(
                out=sz_spill.rearrange("(k m p) t -> k m p t", k=NCHUNK, p=128)[k][m],
                in_=siluz[m][:, :]), 16)
        GE.wait_ge(ST.h, st_spill)
        c_ar[k].inc(GE.collective_compute(
            "AllReduce", ALU.add, replica_groups=[list(range(NC))],
            ins=[ar_in[k].ap().opt()], outs=[ar_out[k].ap().opt()]))

    pq = contextlib.ExitStack()
    wstage2 = sbuf("wstage2", [128, DM], F32, pq)
    qs = {
        "v": sbuf("q_v", [128, DM], F32, pq),
        "r05": sbuf("q_r05", [128, DM], F32, pq),
        "r1": sbuf("q_r1", [128, DM], F32, pq),
        "r2c": sbuf("q_r2c", [128, DM], F32, pq),
        "q": sbuf("q_q", [128, DM], F32, pq),
        "f8": sbuf("q_f8", [128, DM], FP8, pq),
        "m2": sbuf("q_m2", [128, DM], U8, pq),
        "m4": sbuf("q_m4", [128, DM], U8, pq),
        "am": sbuf("q_am", [128, 64], F32, pq),
        "ie": sbuf("q_ie", [128, 64], I32, pq),
        "ir": sbuf("q_ir", [128, 64], I32, pq),
        "is": sbuf("q_is", [128, 64], I32, pq),
    }
    qbf = sbuf("qbf", [128, DM], BF16, pq)
    qev = sbuf("qev", [128, 1024], BF16, pq)

    def q_t_store(src_tile_ap, n_kf, dst, col0):
        last = ST.n
        for b in range(n_kf // 8):
            TE.wait_ge(SC.h, SC.n)
            for j in range(8):
                kf = 8 * b + j
                mmt = TE.transpose(pTB[:, 128 * j:128 * (j + 1)],
                                   qbf[:, 128 * kf:128 * (kf + 1)], ident_bf[:, :])
            pe = PE.inc(mmt)
            SE.wait_ge(PE.h, pe)
            SE.wait_ge(ST.h, last)
            SC.inc(SE.copy(qev[:, :], pTB[:, :]))
            SE.copy(junkB[:, 0:4], junkA[:, 0:4])
            last = ST.inc(SE.dma_start(
                out=dst.rearrange("(a p) c -> a p c", p=128)[8 * b:8 * b + 8][
                    :, :, col0:col0 + 128].transpose([1, 0, 2]),
                in_=qev[:, :]), 16)
        return last

    for wt_dram, dst in ((wg_in, wgq_in), (wu_in, wuq_in)):
        for it in range(8):
            SY.wait_ge(V.h, V.n)
            nL = L_.inc(SY.dma_start(
                out=wstage2[:, :], in_=wt_dram.rearrange("(a p) c -> a p c", p=128)[it]), 16)
            VE.wait_ge(L_.h, nL)
            VE.wait_ge(PE.h, PE.n)
            vqd = quant_w(wstage2, qbf, qs, DM)
            TE.wait_ge(V.h, vqd)
            stq = q_t_store(0, 16, dst, 128 * it)
        GE.wait_ge(ST.h, stq)
        sem = c_agwg if wt_dram is wg_in else c_agwu
        sem.inc(GE.collective_compute(
            "AllGather", ALU.bypass, replica_groups=[list(range(NC))],
            ins=[(wgq_in if wt_dram is wg_in else wuq_in).ap().opt()],
            outs=[(wgq_all if wt_dram is wg_in else wuq_all).ap().opt()]))

    for it in range(2):
        for sub in range(4):
            SY.wait_ge(V.h, V.n)
            nL = L_.inc(SY.dma_start(
                out=wstage2[:, :],
                in_=wd_in.rearrange("(a p) (s c) -> a s p c", p=128, c=DM)[it][sub]), 16)
            VE.wait_ge(L_.h, nL)
            VE.wait_ge(PE.h, PE.n)
            vqd = quant_w(wstage2, qbf, qs, DM)
            TE.wait_ge(V.h, vqd)
            last = ST.n
            for b in range(2):
                TE.wait_ge(SC.h, SC.n)
                for j in range(8):
                    kf = 8 * b + j
                    mmt = TE.transpose(pTB[:, 128 * j:128 * (j + 1)],
                                       qbf[:, 128 * kf:128 * (kf + 1)], ident_bf[:, :])
                pe = PE.inc(mmt)
                SE.wait_ge(PE.h, pe)
                SE.wait_ge(ST.h, last)
                SC.inc(SE.copy(qev[:, :], pTB[:, :]))
                SE.copy(junkB[:, 0:4], junkA[:, 0:4])
                last = ST.inc(SE.dma_start(
                    out=wdq_in.rearrange("(a p) c -> a p c", p=128)[
                        sub * 16 + 8 * b:sub * 16 + 8 * b + 8][
                        :, :, 128 * it:128 * (it + 1)].transpose([1, 0, 2]),
                    in_=qev[:, :]), 16)
    GE.wait_ge(ST.h, last)
    c_agwd.inc(GE.collective_compute(
        "AllGather", ALU.bypass, replica_groups=[list(range(NC))],
        ins=[wdq_in.ap().opt()], outs=[wdq_all.ap().opt()]))

    pq.close()

    barrier()
    pm1.close()
    pwi.close()

    pm3 = contextlib.ExitStack()
    HC = CT // 2
    u_c2b = [[sbuf(f"u_c2{p}_{m}", [128, HC], BF16, pm3) for m in range(4)]
             for p in range(2)]
    siluz2b = [[sbuf(f"sz2{p}_{m}", [128, HC], BF16, pm3) for m in range(4)]
               for p in range(2)]
    dtevB = sbuf("dtevB", [128, 512], F32, pm3)
    dt_bf = sbuf("dt_bf", [128, CT], BF16, pm3)
    bc_sb = sbuf("bc_sb", [32, CT], F32, pm3)
    bc_bf = sbuf("bc_bf", [32, CT], BF16, pm3)
    Bbc = [sbuf("BbcA", [128, 16 * TT], BF16, pm3),
           sbuf("BbcB", [128, 16 * TT], BF16, pm3)]
    Cbc = [sbuf("CbcA", [128, 16 * TT], BF16, pm3),
           sbuf("CbcB", [128, 16 * TT], BF16, pm3)]
    delta_sb = [sbuf(f"delta{m}", [128, HC], F32, pm3) for m in range(4)]
    sp_a = sbuf("sp_a", [128, 512], F32, pm3)
    sp_b = sbuf("sp_b", [128, 512], F32, pm3)
    sp_c = sbuf("sp_c", [128, 512], F32, pm3)
    dA2 = [sbuf("dAA", [128, 16 * TT], F32, pm3), sbuf("dAB", [128, 16 * TT], F32, pm3)]
    x2b = [sbuf("x2bA", [128, 16 * TT], BF16, pm3), sbuf("x2bB", [128, 16 * TT], BF16, pm3)]
    du_sb = sbuf("du_sb", [128, TT], F32, pm3)
    h_sb = sbuf("h_sb", [128, 16 * TT], BF16, pm3)
    y23b = [[sbuf(f"y23_{p}_{m}", [128, TT], BF16, pm3) for m in range(4)]
            for p in range(2)]
    carry = sbuf("carry", [128, 64], F32, pm3)
    oo_sb = sbuf("oo_sb", [128, 1024], F32, pm3)
    bc_dram = [dram(f"bc_dram{kq}", [32, CT], BF16) for kq in range(NCHUNK)]

    def uc_load(H):
        kq, hq = H // 2, H % 2
        SY.wait_ge(V.h, V.n)
        SY.wait_ge(G.h, G.n)
        n = L_.n
        for m in range(4):
            n = L_.inc(SY.dma_start(
                out=u_c2b[H % 2][m][:, :],
                in_=uc_spill.rearrange("(k m p) t -> k m p t", k=NCHUNK, p=128)[kq][m][
                    :, HC * hq:HC * (hq + 1)]), 16)
            n = L_.inc(SY.dma_start(
                out=siluz2b[H % 2][m][:, :],
                in_=sz_spill.rearrange("(k m p) t -> k m p t", k=NCHUNK, p=128)[kq][m][
                    :, HC * hq:HC * (hq + 1)]), 16)
        return n

    uc_ready = [0, 0]
    uc_ready[0] = uc_load(0)
    def bc_bcast_load(kq, tauq, bparq):
        SY.wait_ge(G.h, bc_guard[bparq][0])
        SY.wait_ge(V.h, bc_guard[bparq][1])
        SY.wait_ge(ST.h, st_bc_all[kq])
        t0q = TT * tauq
        n = L_.inc(SY.dma_start(
            out=Bbc[bparq][:, :].rearrange("p (s t) -> p s t", s=16),
            in_=bc_dram[kq].ap()[0:16, t0q:t0q + TT]
            .unsqueeze(0).broadcast_to([128, 16, TT])), 16)
        n = L_.inc(SY.dma_start(
            out=Cbc[bparq][:, :].rearrange("p (s t) -> p s t", s=16),
            in_=bc_dram[kq].ap()[16:32, t0q:t0q + TT]
            .unsqueeze(0).broadcast_to([128, 16, TT])), 16)
        return n

    st_bc_all = [0] * NCHUNK
    bc_guard = [[0, 0], [0, 0]]
    nB = [0] * NTAU
    sc_exp = [SC.n, SC.n]
    v_scans = [V.n, V.n]
    v_tree = [V.n, V.n]
    g_x = [G.n, G.n]
    for k in range(NCHUNK):
        if k % 4 == 0:
            VE.wait_ge(PE.h, PE.n)
            VE.memset(carry[:, :], 0.0)
            V.inc(spacer())
        SY.wait_ge(c_ar[k].h, 1)
        for hh in range(4):
            SY.wait_ge(SC.h, SC.n)
            nL = L_.inc(SY.dma_start(out=dtevB[:, :],
                                     in_=ar_out[k][0:128, 512 * hh:512 * (hh + 1)]), 16)
            SE.wait_ge(L_.h, nL)
            SE.wait_ge(PE.h, PE.n)
            SC.inc(SE.copy(dt_bf[:, 512 * hh:512 * (hh + 1)], dtevB[:, :]))
        SY.wait_ge(V.h, V.n)
        nL = L_.inc(SY.dma_start(out=bc_sb[:, :], in_=ar_out[k][128:160, :]), 16)
        VE.wait_ge(L_.h, nL)
        VE.tensor_copy(bc_bf[:, :], bc_sb[:, :])
        spacer()
        vbc = V.inc(spacer())
        SE.wait_ge(V.h, vbc)
        st_bc_all[k] = ST.inc(SE.dma_start(out=bc_dram[k].ap(), in_=bc_bf[:, :]), 16)
        if k > 0:
            nB[0] = bc_bcast_load(k, 0, 0)
        if k == 0:
            tapw("dtbf0", dt_bf[:, :])

        pe_wout = PE.n
        pe_wout2 = [PE.n, PE.n]
        for half in range(2):
            ht0 = HC * half
            VE.wait_ge(PE.h, pe_wout)
            VE.wait_ge(G.h, G.n)
            for m in range(4):
                for nn in range(2):
                    pa = pA[nn % 2]
                    TE.wait_ge(SC.h, SC.n)
                    pe = PE.inc(TE.matmul(pa[:, :], w_dt_bf[:, 128 * m:128 * (m + 1)],
                                          dt_bf[:, ht0 + 512 * nn:ht0 + 512 * (nn + 1)],
                                          start=True, stop=True))
                    SE.wait_ge(PE.h, pe)
                    SE.activation(sp_a[:, :], pa[:, :], AF.Relu, bias=dtb_sb[:, m:m + 1])
                    SE.activation(sp_b[:, :], pa[:, :], AF.Abs, bias=dtb_sb[:, m:m + 1])
                    SE.activation(sp_c[:, :], sp_b[:, :], AF.Exp, scale=-1.0)
                    scd = SC.inc(SE.activation(sp_b[:, :], sp_c[:, :], AF.Ln, bias=1.0))
                    VE.wait_ge(SC.h, scd)
                    VE.tensor_tensor(delta_sb[m][:, 512 * nn:512 * (nn + 1)],
                                     sp_a[:, :], sp_b[:, :], ALU.add)
            spacer()
            v_delta = V.inc(spacer())
            H = 2 * k + half
            u_c2 = u_c2b[H % 2]
            siluz2 = siluz2b[H % 2]
            if H + 1 < 2 * NCHUNK:
                uc_ready[(H + 1) % 2] = uc_load(H + 1)
            VE.wait_ge(L_.h, uc_ready[H % 2])
            GE.wait_ge(L_.h, uc_ready[H % 2])
            for tau4 in range(4):
                tau = half * 4 + tau4
                t0 = TT * tau
                tl0 = TT * tau4
                bpar = tau % 2
                if k == 0 and half == 0 and tau4 == 0:
                    nB[0] = bc_bcast_load(0, 0, 0)
                if tau + 1 < NTAU:
                    nB[tau + 1] = bc_bcast_load(k, tau + 1, (tau + 1) % 2)
                y23 = y23b[tau % 2]
                VE.wait_ge(PE.h, pe_wout2[tau % 2])
                for m in range(4):
                    par = m % 2
                    SE.wait_ge(V.h, v_delta)
                    SE.wait_ge(V.h, v_scans[par])
                    for s in range(16):
                        se_i = SE.activation(dA2[par][:, TT * s:TT * (s + 1)],
                                             delta_sb[m][:, tl0:tl0 + TT], AF.Exp,
                                             scale=negA_sb[:, 16 * m + s:16 * m + s + 1])
                    sc_exp[par] = SC.inc(se_i)
                    GE.wait_ge(V.h, v_delta)
                    GE.wait_ge(V.h, v_scans[par])
                    GE.wait_ge(V.h, v_tree[par])
                    GE.wait_ge(L_.h, nB[tau])
                    GE.tensor_tensor(du_sb[:, :], delta_sb[m][:, tl0:tl0 + TT],
                                     u_c2[m][:, tl0:tl0 + TT], ALU.mult)
                    gx = G.inc(GE.tensor_tensor(
                        x2b[par][:, :].rearrange("p (s t) -> p s t", s=16),
                        du_sb[:, :].unsqueeze(1).broadcast_to([128, 16, TT]),
                        Bbc[bpar][:, :].rearrange("p (s t) -> p s t", s=16), ALU.mult))
                    g_x[par] = gx
                    VE.wait_ge(SC.h, sc_exp[par])
                    VE.wait_ge(G.h, gx)
                    for s in range(16):
                        VE.tensor_tensor_scan(h_sb[:, TT * s:TT * (s + 1)],
                                              dA2[par][:, TT * s:TT * (s + 1)],
                                              x2b[par][:, TT * s:TT * (s + 1)],
                                              carry[:, 16 * m + s:16 * m + s + 1],
                                              ALU.mult, ALU.add)
                    spacer()
                    v_scans[par] = V.inc(spacer())
                    VE.tensor_tensor(x2b[par][:, :], h_sb[:, :], Cbc[bpar][:, :],
                                     ALU.mult)
                    dp_ = x2b[par]
                    VE.tensor_tensor(dp_[:, 0:8 * TT], dp_[:, 0:8 * TT],
                                     dp_[:, 8 * TT:16 * TT], ALU.add)
                    VE.tensor_tensor(dp_[:, 8 * TT:12 * TT], dp_[:, 0:4 * TT],
                                     dp_[:, 4 * TT:8 * TT], ALU.add)
                    VE.tensor_tensor(dp_[:, 12 * TT:14 * TT], dp_[:, 8 * TT:10 * TT],
                                     dp_[:, 10 * TT:12 * TT], ALU.add)
                    VE.tensor_tensor(dp_[:, 14 * TT:15 * TT], dp_[:, 12 * TT:13 * TT],
                                     dp_[:, 13 * TT:14 * TT], ALU.add)
                    VE.scalar_tensor_tensor(y23[m][:, :], u_c2[m][:, tl0:tl0 + TT],
                                            dp_sb[:, m:m + 1], dp_[:, 14 * TT:15 * TT],
                                            ALU.mult, ALU.add)
                    VE.tensor_tensor(y23[m][:, :], y23[m][:, :],
                                     siluz2[m][:, tl0:tl0 + TT], ALU.mult)
                    v_tree[par] = V.inc(VE.tensor_copy(
                        carry[:, 16 * m:16 * (m + 1)],
                        h_sb[:, :].rearrange("p (s t) -> p s t", s=16)[:, :, TT - 1]))
                bc_guard[bpar] = [G.n, V.n]
                spacer()
                v_y = V.inc(spacer())
                TE.wait_ge(V.h, v_y)
                for mh in range(2):
                    for fn in range(4):
                        pb = pB[fn % 2]
                        TE.wait_ge(SC.h, SC.n)
                        for kd in range(4):
                            mmo = TE.matmul(
                                pb[:, :], y23[kd][:, 128 * mh:128 * (mh + 1)],
                                w_out_bf[:, DM * kd + 512 * fn:DM * kd + 512 * (fn + 1)],
                                start=(kd == 0), stop=(kd == 3))
                        pe_wout = PE.inc(mmo)
                        pe_wout2[tau % 2] = pe_wout
                        SE.wait_ge(PE.h, pe_wout)
                        if fn % 2 == 0:
                            SE.wait_ge(ST.h, ST.n)
                        SC.inc(SE.copy(oo_sb[:, 512 * (fn % 2):512 * (fn % 2 + 1)],
                                       pb[:, :]))
                        ST.inc(SE.dma_start(
                            out=rs_in[k][t0 + 128 * mh:t0 + 128 * (mh + 1),
                                         512 * fn:512 * (fn + 1)],
                            in_=oo_sb[:, 512 * (fn % 2):512 * (fn % 2 + 1)]), 16)
        if k == 0:
            tapw("y230", y23[0][:, :])
        GE.wait_ge(ST.h, ST.n)
        c_rs[k].inc(GE.collective_compute(
            "ReduceScatter", ALU.add, replica_groups=[list(range(NC))],
            ins=[rs_in[k].ap().opt()], outs=[rs_out[k].ap().opt()]))

    barrier()
    pm3.close()
    pwo.close()

    pfc = contextlib.ExitStack()
    fbns = sbuf("fbns", [128, 24], F32, pfc)
    fmv = sbuf("fmv", [128, 4], F32, pfc)
    fcf = sbuf("fcf", [128, 4], F32, pfc)

    pf12 = contextlib.ExitStack()
    xqT = sbuf("xqT", [128, 16 * TOK], BF16, pf12)
    fqs = {
        "v": sbuf("f_v", [128, DM], F32, pf12),
        "r1": sbuf("f_r1", [128, DM], F32, pf12),
        "q": sbuf("f_q", [128, DM], F32, pf12),
        "f8": sbuf("f_f8", [128, DM], FP8, pf12),
        "m2": sbuf("f_m2", [128, DM], U8, pf12),
        "am": sbuf("f_am", [128, 64], F32, pf12),
        "ie": sbuf("f_ie", [128, 64], I32, pf12),
        "ir": sbuf("f_ir", [128, 64], I32, pf12),
        "is": sbuf("f_is", [128, 64], I32, pf12),
    }

    pf1 = contextlib.ExitStack()
    xt2 = sbuf("xt2", [128, DM], F32, pf1)
    at2 = sbuf("at2", [128, DM], F32, pf1)
    x2t = sbuf("x2t", [128, DM], F32, pf1)
    x3n = sbuf("x3n", [128, DM], F32, pf1)
    xq_bf = sbuf("xq_bf", [128, DM], BF16, pf1)

    if "rs0" in tap_bufs:
        GE.wait_ge(c_rs[0].h, 1)
        ST.inc(GE.dma_start(out=tap_bufs["rs0"], in_=rs_out[0].ap()), 16)
    for i in range(16):
        kchunk, half = i // 2, i % 2
        SY.wait_ge(c_rs[kchunk].h, 1)
        SY.wait_ge(V.h, V.n)
        SY.wait_ge(SC.h, SC.n)
        nL = L_.inc(SY.dma_start(out=xt2[:, :],
                                 in_=x_in.rearrange("(i p) f -> i p f", p=128)[i]), 16)
        nL = L_.inc(SY.dma_start(out=at2[:, :],
                                 in_=ln1_spill.rearrange("(i p) f -> i p f", p=128)[i]), 16)
        nL = L_.inc(SY.dma_start(out=x2t[:, :],
                                 in_=rs_out[kchunk].rearrange("(h p) f -> h p f", p=128)[half]),
                    16)
        VE.wait_ge(L_.h, nL)
        VE.wait_ge(ST.h, ST.n)
        VE.tensor_tensor(at2[:, :], at2[:, :], x2t[:, :], ALU.add)
        vw = ln_stats(at2, fbns, fmv)
        vw = ln_coefs(fmv, fcf, vw)
        VE.tensor_scalar(x2t[:, :], at2[:, :], fcf[:, 0:1], fcf[:, 1:2],
                         ALU.mult, ALU.add)
        VE.tensor_tensor(x2t[:, :], x2t[:, :], xt2[:, :], ALU.add)
        spacer()
        vx2 = V.inc(spacer())
        SE.wait_ge(V.h, vx2)
        ST.inc(SE.dma_start(out=x2_spill.rearrange("(i p) f -> i p f", p=128)[i],
                            in_=x2t[:, :]), 16)
        vw = ln_stats(x2t, fbns, fmv)
        vw = ln_coefs(fmv, fcf, vw)
        VE.tensor_scalar(x3n[:, :], x2t[:, :], fcf[:, 0:1], fcf[:, 1:2],
                         ALU.mult, ALU.add)
        VE.wait_ge(PE.h, PE.n)
        vq = quant_a(x3n, xq_bf[:, :], fqs, DM)
        if i == 0:
            tapw("xq0", xq_bf[:, :])
        TE.wait_ge(V.h, vq)
        for b in range(2):
            TE.wait_ge(SC.h, SC.n)
            for j in range(8):
                kf = 8 * b + j
                mmt = TE.transpose(pTB[:, 128 * j:128 * (j + 1)],
                                   xq_bf[:, 128 * kf:128 * (kf + 1)], ident_bf[:, :])
            pe = PE.inc(mmt)
            SE.wait_ge(PE.h, pe)
            SC.inc(SE.copy(
                xqT[:, :].rearrange("p (a t) -> p a t", t=2048)[
                    :, 8 * b:8 * b + 8, 128 * i:128 * (i + 1)],
                pTB[:, :].rearrange("p (a t) -> p a t", t=128)))

    tapw("xqTv", xqT[:, 0:2048])
    barrier()
    pf1.close()

    if "x2sp" in tap_bufs:
        GE.wait_ge(ST.h, ST.n)
        ST.inc(GE.dma_start(out=tap_bufs["x2sp"], in_=x2_spill.ap()), 16)
    pf2 = contextlib.ExitStack()
    wgt = [sbuf("wgtA", [128, 16 * 512], BF16, pf2),
           sbuf("wgtB", [128, 16 * 512], BF16, pf2)]
    wut = [sbuf("wutA", [128, 16 * 512], BF16, pf2),
           sbuf("wutB", [128, 16 * 512], BF16, pf2)]
    sg_sb = [sbuf("sg_sbA", [128, 512], F32, pf2),
             sbuf("sg_sbB", [128, 512], F32, pf2)]
    interB = [sbuf("interA", [128, 16 * 512], BF16, pf2),
              sbuf("interC", [128, 16 * 512], BF16, pf2)]
    iqev = [sbuf("iqevA", [128, 512], BF16, pf2), sbuf("iqevB", [128, 512], BF16, pf2)]

    SY.wait_ge(c_agwg.h, 1)
    SY.wait_ge(c_agwu.h, 1)
    sc_silu = [SC.n, SC.n]
    v_int = [V.n, V.n]

    def wgu_load(ic, wpar, pe_guard):
        r, icc = ic // 2, ic % 2
        SY.wait_ge(PE.h, pe_guard)
        n = L_.n
        for kf in range(16):
            n = L_.inc(SY.dma_start(
                out=wgt[wpar][:, 512 * kf:512 * (kf + 1)],
                in_=wgq_all.rearrange("(r a p) c -> r a p c", r=NC, p=128)[r][kf][
                    :, 512 * icc:512 * (icc + 1)]), 16)
            n = L_.inc(SY.dma_start(
                out=wut[wpar][:, 512 * kf:512 * (kf + 1)],
                in_=wuq_all.rearrange("(r a p) c -> r a p c", r=NC, p=128)[r][kf][
                    :, 512 * icc:512 * (icc + 1)]), 16)
        return n

    wgu_ready = [wgu_load(0, 0, 0), wgu_load(1, 1, 0)]
    pe_ic = [0, 0]
    st_iq = [ST.n, ST.n]

    def emit_tr(tic, vq_t):
        rr, ticc = tic // 2, tic % 2
        gI0 = rr * 1024 + ticc * 512
        TE.wait_ge(V.h, vq_t)
        for mt in range(16):
            par = mt % 2
            SE.wait_ge(ST.h, st_iq[par])
            TE.wait_ge(SC.h, SC.n)
            for sub in range(4):
                mmt = TE.transpose(
                    pTB[:, 128 * sub:128 * (sub + 1)],
                    interB[tic % 2][:, 512 * mt + 128 * sub:
                                    512 * mt + 128 * (sub + 1)],
                    ident_bf[:, :])
            pe = PE.inc(mmt)
            SE.wait_ge(PE.h, pe)
            SC.inc(SE.copy(iqev[par][:, :], pTB[:, 0:512]))
            SE.copy(junkB[:, 0:4], junkA[:, 0:4])
            st_iq[par] = ST.inc(SE.dma_start(
                out=ikt_spill.rearrange("(a p) t -> a p t", p=128)[
                    gI0 // 128:gI0 // 128 + 4][:, :, 128 * mt:128 * (mt + 1)]
                .transpose([1, 0, 2]),
                in_=iqev[par][:, :]), 16)

    vq_prev = None
    for ic in range(16):
        wpar = ic % 2
        inter = interB[ic % 2]
        VE.wait_ge(PE.h, PE.n)
        nL = wgu_ready[wpar]
        TE.wait_ge(L_.h, nL)
        for mt in range(16):
            fpar = mt % 2
            pg, pu = pA[fpar], pB[fpar]
            TE.wait_ge(SC.h, sc_silu[fpar])
            TE.wait_ge(V.h, v_int[fpar])
            for kf in range(16):
                TE.matmul(pg[:, :],
                          xqT[:, 2048 * kf + 128 * mt:2048 * kf + 128 * (mt + 1)],
                          wgt[wpar][:, 512 * kf:512 * (kf + 1)],
                          start=(kf == 0), stop=(kf == 15))
            for kf in range(16):
                mmu = TE.matmul(pu[:, :],
                                xqT[:, 2048 * kf + 128 * mt:2048 * kf + 128 * (mt + 1)],
                                wut[wpar][:, 512 * kf:512 * (kf + 1)],
                                start=(kf == 0), stop=(kf == 15))
            pe = PE.inc(mmu)
            SE.wait_ge(PE.h, pe)
            scS = SC.inc(SE.activation(sg_sb[fpar][:, :], pg[:, :], AF.Silu))
            sc_silu[fpar] = scS
            VE.wait_ge(SC.h, scS)
            v_int[fpar] = V.inc(VE.tensor_tensor(
                inter[:, 512 * mt:512 * (mt + 1)], sg_sb[fpar][:, :],
                pu[:, :], ALU.mult))
        if ic == 0:
            tapw("wgt0", wgt[0][:, :])
            tq = tapw("intpre", inter[:, :])
            if tq:
                VE.wait_ge(ST.h, tq)
        pe_ic[wpar] = PE.n
        if ic + 2 < 16:
            wgu_ready[wpar] = wgu_load(ic + 2, wpar, pe_ic[wpar])
        for sub in range(4):
            vq = quant_a(inter[:, 2048 * sub:2048 * (sub + 1)],
                         inter[:, 2048 * sub:2048 * (sub + 1)], fqs, 2048)
        if ic == 0:
            tapw("iq0", inter[:, :])
        if vq_prev is not None:
            emit_tr(ic - 1, vq_prev)
        vq_prev = vq
    emit_tr(15, vq_prev)

    barrier()
    pf2.close()
    pf12.close()

    pf3 = contextlib.ExitStack()
    ikt = [sbuf("iktA", [128, 64 * 128], BF16, pf3),
           sbuf("iktB", [128, 64 * 128], BF16, pf3)]
    wdt_s = [sbuf("wdt_sA", [128, 64 * 512], BF16, pf3),
             sbuf("wdt_sB", [128, 64 * 512], BF16, pf3)]
    dstage = [sbuf("dstgA", [128, 512], F32, pf3),
              sbuf("dstgB", [128, 512], F32, pf3)]
    dout_f = sbuf("dout_f", [128, DM], F32, pf3)
    ln4t = sbuf("ln4t", [128, DM], F32, pf3)
    xt3 = sbuf("xt3", [128, DM], F32, pf3)
    ot3 = sbuf("ot3a", [128, DM], F32, pf3)

    SY.wait_ge(c_agwd.h, 1)

    def wd_load(fb2, par, pe_guard):
        SY.wait_ge(PE.h, pe_guard)
        nW = L_.n
        for half in range(2):
            fb = 2 * fb2 + half
            for kg in range(16):
                nW = L_.inc(SY.dma_start(
                    out=wdt_s[par][:, 2048 * kg:2048 * (kg + 1)].rearrange(
                        "p (kk c) -> p kk c", c=512)[:, :, 256 * half:256 * (half + 1)],
                    in_=wdq_all.rearrange("(r kk p) c -> r p kk c", r=NC, p=128)[fb][
                        :, 4 * kg:4 * (kg + 1), :]), 16)
        return nW

    def ikt_load(mt, par, pe_guard):
        SY.wait_ge(PE.h, pe_guard)
        nL = L_.n
        for kg in range(16):
            nL = L_.inc(SY.dma_start(
                out=ikt[par][:, 512 * kg:512 * (kg + 1)].rearrange(
                    "p (kk t) -> p kk t", t=128),
                in_=ikt_spill.rearrange("(kk p) t -> p kk t", p=128)[
                    :, 4 * kg:4 * (kg + 1), 128 * mt:128 * (mt + 1)]), 16)
        return nL

    wd_ready = [wd_load(0, 0, 0), wd_load(1, 1, 0)]
    ikt_ready = [ikt_load(0, 0, 0), ikt_load(1, 1, 0)]
    st_do = [ST.n, ST.n]
    for fb2 in range(4):
        wpar = fb2 % 2
        for mt in range(16):
            ipar = mt % 2
            TE.wait_ge(L_.h, max(ikt_ready[ipar], wd_ready[wpar]))
            pb = pB[mt % 2]
            TE.wait_ge(SC.h, SC.n)
            TE.wait_ge(V.h, V.n)
            for kk in range(64):
                mmd = TE.matmul(pb[:, :], ikt[ipar][:, 128 * kk:128 * (kk + 1)],
                                wdt_s[wpar][:, 512 * kk:512 * (kk + 1)],
                                start=(kk == 0), stop=(kk == 63))
            mjunk = TE.matmul(pR[:, 0:128], onesbf[:, :], onesbf[:, :],
                              start=True, stop=True)
            pe = PE.inc(mjunk)
            nxt = mt + 2
            if nxt < 16:
                ikt_ready[ipar] = ikt_load(nxt, ipar, pe)
            elif fb2 < 3:
                ikt_ready[ipar] = ikt_load(nxt - 16, ipar, pe)
            if mt == 0 and fb2 in (1, 2):
                wd_ready[(fb2 + 1) % 2] = wd_load(fb2 + 1, (fb2 + 1) % 2, pe)
            VE.wait_ge(PE.h, pe)
            VE.wait_ge(ST.h, st_do[ipar])
            spacer()
            V.inc(VE.tensor_copy(dstage[ipar][:, :], pb[:, :]))
            spacer()
            spacer()
            vo = V.inc(spacer())
            SE.wait_ge(V.h, vo)
            st_do[ipar] = ST.inc(SE.dma_start(
                out=dout_spill.rearrange("(i p) f -> i p f", p=128)[mt][
                    :, 512 * fb2:512 * (fb2 + 1)],
                in_=dstage[ipar][:, :]), 16)

    SY.wait_ge(ST.h, ST.n)
    st_out = ST.n
    for mt in range(16):
        SY.wait_ge(V.h, V.n)
        nL2 = L_.inc(SY.dma_start(out=dout_f[:, :],
                                  in_=dout_spill.rearrange("(i p) f -> i p f", p=128)[mt]), 16)
        nL2 = L_.inc(SY.dma_start(out=xt3[:, :],
                                  in_=x2_spill.rearrange("(i p) f -> i p f", p=128)[mt]), 16)
        VE.wait_ge(L_.h, nL2)
        vw = ln_stats(dout_f, fbns, fmv)
        vw = ln_coefs(fmv, fcf, vw)
        VE.tensor_scalar(ln4t[:, :], dout_f[:, :], fcf[:, 0:1], fcf[:, 1:2],
                         ALU.mult, ALU.add)
        VE.wait_ge(ST.h, st_out)
        VE.tensor_tensor(ot3[:, :], xt3[:, :], ln4t[:, :], ALU.add)
        spacer()
        spacer()
        vo = V.inc(spacer())
        SE.wait_ge(V.h, vo)
        st_out = ST.inc(SE.dma_start(
            out=out_own.rearrange("(i p) f -> i p f", p=128)[mt],
            in_=ot3[:, :]), 16)

    SY.wait_ge(ST.h, ST.n)
    GE.wait_ge(ST.h, ST.n)

    pf3.close()
    pfc.close()
    es.close()
    return nc


def _slice_inputs(inputs, c):
    x = np.asarray(inputs["L0_attn_x"]).reshape(NT, DM)
    stripes = [x[(k * NC + c) * STR:(k * NC + c) * STR + STR] for k in range(NCHUNK)]
    x_own = np.ascontiguousarray(np.concatenate(stripes, axis=0))
    W_in = np.asarray(inputs["L0_attn_W_in"])
    w_in_sl = np.ascontiguousarray(np.concatenate(
        [W_in[:, c * DOWN:(c + 1) * DOWN], W_in[:, DI + c * DOWN:DI + (c + 1) * DOWN]],
        axis=1))
    cw = np.asarray(inputs["L0_attn_conv_w"])[c * DOWN:(c + 1) * DOWN, 0, :]
    convw = np.ascontiguousarray(cw.reshape(4, 128, 4).transpose(1, 0, 2).reshape(128, 16))
    convb = np.ascontiguousarray(
        np.asarray(inputs["L0_attn_conv_b"])[c * DOWN:(c + 1) * DOWN].reshape(4, 128).T)
    wx_sl = np.ascontiguousarray(np.asarray(inputs["L0_attn_W_x"])[c * DOWN:(c + 1) * DOWN])
    wdt_sl = np.ascontiguousarray(
        np.asarray(inputs["L0_attn_W_dt"])[:, c * DOWN:(c + 1) * DOWN])
    dtb = np.ascontiguousarray(
        np.asarray(inputs["L0_attn_dt_bias"])[c * DOWN:(c + 1) * DOWN].reshape(4, 128).T)
    alog = np.ascontiguousarray(
        np.asarray(inputs["L0_attn_A_log"])[c * DOWN:(c + 1) * DOWN].reshape(4, 128, 16)
        .transpose(1, 0, 2).reshape(128, 64))
    dp = np.ascontiguousarray(
        np.asarray(inputs["L0_attn_Dp"])[c * DOWN:(c + 1) * DOWN].reshape(4, 128).T)
    wout_sl = np.ascontiguousarray(np.asarray(inputs["L0_attn_W_out"])[c * DOWN:(c + 1) * DOWN])
    wg_sl = np.ascontiguousarray(np.asarray(inputs["L0_ffn_w_gate"])[c * 1024:(c + 1) * 1024])
    wu_sl = np.ascontiguousarray(np.asarray(inputs["L0_ffn_w_up"])[c * 1024:(c + 1) * 1024])
    wd_sl = np.ascontiguousarray(np.asarray(inputs["L0_ffn_w_down"])[c * 256:(c + 1) * 256])
    return dict(
        x_own=x_own.astype(np.float32),
        ident=np.eye(128, dtype=np.float32),
        rmsw=np.asarray(inputs["L0_attn_rms_w"]).reshape(1, DM).astype(np.float32),
        w_in_sl=w_in_sl.astype(np.float32), convw=convw.astype(np.float32),
        convb=convb.astype(np.float32), wx_sl=wx_sl.astype(np.float32),
        wdt_sl=wdt_sl.astype(np.float32), dtb=dtb.astype(np.float32),
        alog=alog.astype(np.float32), dp=dp.astype(np.float32),
        wout_sl=wout_sl.astype(np.float32), wg_sl=wg_sl.astype(np.float32),
        wu_sl=wu_sl.astype(np.float32), wd_sl=wd_sl.astype(np.float32),
    )


def kernel(**inputs):
    global _LAST_TAPS
    nc = build(taps=_DEBUG_TAPS)
    in_maps = [_slice_inputs(inputs, c) for c in range(NC)]
    res = run_bass_kernel_spmd(nc, in_maps, core_ids=list(range(NC)))
    _LAST_TAPS = {}
    for tname, _, _ in _DEBUG_TAPS:
        _LAST_TAPS[tname] = [r["tap_" + tname] for r in res.results]
    out = np.zeros((NT, DM), dtype=np.float32)
    for c in range(NC):
        o = res.results[c]["out_own"]
        for k in range(NCHUNK):
            out[(k * NC + c) * STR:(k * NC + c) * STR + STR] = o[k * STR:(k + 1) * STR]
    return out.reshape(B, L, DM)


# version 10
# speedup vs baseline: 1.1109x; 1.0204x over previous
import contextlib
import numpy as np
import concourse.bass as bass
import concourse.mybir as mybir
from concourse.bass_utils import run_bass_kernel_spmd

F32 = mybir.dt.float32
BF16 = mybir.dt.bfloat16
U8 = mybir.dt.uint8
I32 = mybir.dt.int32
FP8 = mybir.dt.float8e4
AF = mybir.ActivationFunctionType
ALU = mybir.AluOpType

NC = 8
B, L, DM = 2, 8192, 2048
NT = B * L
DI, DS, DR, KC = 4096, 16, 128, 4
IF = 8192
DOWN = DI // NC
TOK = NT // NC
STR = 256
NCHUNK = 8
CT = 2048
TT = 256
NTAU = CT // TT
EPS = 1e-5

S_8TH = 1.5 * 2.0 ** 20
S_HALF = 1.5 * 2.0 ** 22
S_ONE = 1.5 * 2.0 ** 23
S_TWO = 1.5 * 2.0 ** 24

_DEBUG_TAPS = ()
_LAST_TAPS = {}


class Sem:
    def __init__(self, nc, es, name):
        self.h = es.enter_context(nc.semaphore(name))
        self.n = 0

    def inc(self, instr, amt=1):
        instr.then_inc(self.h, amt)
        self.n += amt
        return self.n


def build(taps=()):
    nc = bass.Bass("TRN2", target_bir_lowering=False,
                   detect_race_conditions=False)
    es = contextlib.ExitStack()

    def din(name, shape, dt=F32):
        return nc.dram_tensor(name, shape, dt, kind="ExternalInput").ap()

    def dout(name, shape, dt=F32):
        return nc.dram_tensor(name, shape, dt, kind="ExternalOutput").ap()

    def dram(name, shape, dt=F32):
        return nc.dram_tensor(name, shape, dt)

    x_in = din("x_own", [TOK, DM])
    ident_in = din("ident", [128, 128])
    rmsw_in = din("rmsw", [1, DM])
    w_in_sl = din("w_in_sl", [DM, 1024])
    convw_in = din("convw", [128, 16])
    convb_in = din("convb", [128, 4])
    wx_in = din("wx_sl", [DOWN, 160])
    wdt_in = din("wdt_sl", [DR, DOWN])
    dtb_in = din("dtb", [128, 4])
    alog_in = din("alog", [128, 64])
    dp_in = din("dp", [128, 4])
    wout_in = din("wout_sl", [DOWN, DM])
    wg_in = din("wg_sl", [IF // NC, DM])
    wu_in = din("wu_sl", [IF // NC, DM])
    wd_in = din("wd_sl", [DM // NC, IF])

    out_own = dout("out_own", [TOK, DM])

    hn_in_h = [dram("hn_in_a", [DM, TOK // 2], BF16),
               dram("hn_in_b", [DM, TOK // 2], BF16)]
    hn_all_h = [dram("hn_all_a", [NC * DM, TOK // 2], BF16),
                dram("hn_all_b", [NC * DM, TOK // 2], BF16)]
    ln1_spill = dram("ln1_spill", [TOK, DM], F32)
    x2_spill = dram("x2_spill", [TOK, DM], F32)
    ar_in = [dram(f"ar_in{k}", [160, CT], F32) for k in range(NCHUNK)]
    ar_out = [dram(f"ar_out{k}", [160, CT], F32) for k in range(NCHUNK)]
    rs_in = [dram(f"rs_in{k}", [CT, DM], F32) for k in range(NCHUNK)]
    rs_out = [dram(f"rs_out{k}", [STR, DM], F32) for k in range(NCHUNK)]
    wgq_in = dram("wgq_in", [DM, 1024], BF16)
    wgq_all = dram("wgq_all", [NC * DM, 1024], BF16)
    wuq_in = dram("wuq_in", [DM, 1024], BF16)
    wuq_all = dram("wuq_all", [NC * DM, 1024], BF16)
    wdq_in = dram("wdq_in", [IF, DM // NC], BF16)
    wdq_all = dram("wdq_all", [NC * IF, DM // NC], BF16)
    ikt_spill = dram("ikt_spill", [IF, TOK], BF16)
    dout_spill = dram("dout_spill", [TOK, DM], F32)
    uc_spill = dram("uc_spill", [NCHUNK * 512, CT], BF16)
    sz_spill = dram("sz_spill", [NCHUNK * 512, CT], BF16)

    tap_bufs = {}
    for tname, tshape, tdt in taps:
        tap_bufs[tname] = dout("tap_" + tname, list(tshape), tdt)

    L_ = Sem(nc, es, "Lq")
    ST = Sem(nc, es, "STq")
    V = Sem(nc, es, "Vm")
    SC = Sem(nc, es, "SCm")
    PE = Sem(nc, es, "PEm")
    G = Sem(nc, es, "Gm")
    c_aghn = [Sem(nc, es, "c_aghn_a"), Sem(nc, es, "c_aghn_b")]
    c_agwg = Sem(nc, es, "c_agwg")
    c_agwu = Sem(nc, es, "c_agwu")
    c_agwd = Sem(nc, es, "c_agwd")
    c_ar = [Sem(nc, es, f"c_ar{k}") for k in range(NCHUNK)]
    c_rs = [Sem(nc, es, f"c_rs{k}") for k in range(NCHUNK)]

    def sbuf(name, shape, dt, stack=None):
        return (stack or es).enter_context(nc.sbuf_tensor(name, shape, dt))

    def psum(name, shape, dt=F32):
        return es.enter_context(nc.psum_tensor(name, shape, dt))

    w_x_bf = sbuf("w_x_bf", [128, 4 * 160], BF16)
    w_dt_bf = sbuf("w_dt_bf", [128, DOWN], BF16)
    A_sb = sbuf("A_sb", [128, 64], F32)
    negA_sb = sbuf("negA_sb", [128, 64], F32)
    convw_sb = sbuf("convw_sb", [128, 16], F32)
    convb_sb = sbuf("convb_sb", [128, 4], F32)
    dtb_sb = sbuf("dtb_sb", [128, 4], F32)
    dp_sb = sbuf("dp_sb", [128, 4], F32)
    onesbf = sbuf("onesbf", [1, 128], BF16)
    ones32 = sbuf("ones32", [1, 128], F32)
    ident_bf = sbuf("ident_bf", [128, 128], BF16)
    junkA = sbuf("junkA", [128, 8], F32)
    junkB = sbuf("junkB", [128, 8], F32)
    consts = sbuf("consts", [128, 4], F32)

    pA = [psum("pA0", [128, 512]), psum("pA1", [128, 512])]
    pB = [psum("pB0", [128, 512]), psum("pB1", [128, 512])]
    pT = [psum("pT0", [128, 128], BF16), psum("pT1", [128, 128], BF16)]
    pTB = psum("pTB", [128, 1024], BF16)
    pR = psum("pR", [128, 512])

    block = es.enter_context(nc.Block())
    SY, VE, SE, TE, GE = nc.sync, nc.vector, nc.scalar, nc.tensor, nc.gpsimd

    def spacer():
        return VE.tensor_copy(junkA[:, :], junkB[:, :])

    def barrier():
        vals = [(V.h, V.n), (SC.h, SC.n), (PE.h, PE.n), (G.h, G.n),
                (L_.h, L_.n), (ST.h, ST.n)]
        for e in (SY, VE, SE, TE, GE):
            for h, n in vals:
                if n > 0:
                    e.wait_ge(h, n)

    def ln_stats(src, bns, mv):
        for j in range(4):
            VE.bn_stats(bns[:, 6 * j:6 * j + 6], src[:, 512 * j:512 * (j + 1)])
        spacer()
        VE.bn_aggr(mv[:, 0:2], bns[:, 0:24])
        spacer()
        return V.inc(spacer())

    def ln_coefs(mv, cf, vwait):
        SE.wait_ge(V.h, vwait)
        SE.activation(cf[:, 2:3], mv[:, 1:2], AF.Ln, bias=float(EPS))
        sc = SC.inc(SE.activation(cf[:, 0:1], cf[:, 2:3], AF.Exp, scale=-0.5))
        VE.wait_ge(SC.h, sc)
        VE.scalar_tensor_tensor(cf[:, 1:2], mv[:, 0:1], -1.0, cf[:, 0:1],
                                ALU.mult, ALU.mult)
        spacer()
        return V.inc(spacer())

    def _block_scales(s, src_am_ap, nb):
        VE.tensor_scalar(s["ie"][:, 0:nb], src_am_ap.bitcast(I32), 23, None,
                         ALU.logical_shift_right, ALU.bypass)
        spacer()
        VE.tensor_scalar(s["ie"][:, 0:nb], s["ie"][:, 0:nb], 129, None,
                         ALU.subtract, ALU.bypass)
        spacer()
        VE.tensor_scalar(s["ie"][:, 0:nb], s["ie"][:, 0:nb], -127, 127,
                         ALU.max, ALU.min)
        spacer()
        VE.tensor_scalar(s["ir"][:, 0:nb], s["ie"][:, 0:nb], -1, 127,
                         ALU.mult, ALU.add)
        VE.tensor_scalar(s["is"][:, 0:nb], s["ie"][:, 0:nb], 127, None,
                         ALU.add, ALU.bypass)
        spacer()
        VE.tensor_scalar(s["ir"][:, 0:nb], s["ir"][:, 0:nb], 23, None,
                         ALU.logical_shift_left, ALU.bypass)
        VE.tensor_scalar(s["is"][:, 0:nb], s["is"][:, 0:nb], 23, None,
                         ALU.logical_shift_left, ALU.bypass)
        spacer()

    def quant_w(wt, qbf, s, Fw):
        nb = Fw // 32
        VE.tensor_reduce(s["am"][:, 0:nb],
                         wt[:, 0:Fw].rearrange("p (b e) -> p b e", e=32),
                         axis=mybir.AxisListType.X, op=ALU.max,
                         apply_absolute_value=True)
        spacer()
        _block_scales(s, s["am"][:, 0:nb], nb)
        VE.tensor_tensor(s["v"][:, 0:Fw].rearrange("p (b e) -> p b e", e=32),
                         wt[:, 0:Fw].rearrange("p (b e) -> p b e", e=32),
                         s["ir"][:, 0:nb].bitcast(F32).unsqueeze(2)
                         .broadcast_to([128, nb, 32]), ALU.mult)
        VE.tensor_scalar(s["q"][:, 0:Fw].bitcast(I32), s["v"][:, 0:Fw].bitcast(I32),
                         0x7FFFFFFF, None, ALU.bitwise_and, ALU.bypass)
        VE.tensor_scalar(s["r05"][:, 0:Fw], s["v"][:, 0:Fw], S_HALF, S_HALF,
                         ALU.add, ALU.subtract)
        VE.tensor_scalar(s["r1"][:, 0:Fw], s["v"][:, 0:Fw], S_ONE, S_ONE,
                         ALU.add, ALU.subtract)
        VE.tensor_scalar(s["r2c"][:, 0:Fw], s["v"][:, 0:Fw], S_TWO, S_TWO,
                         ALU.add, ALU.subtract)
        VE.tensor_scalar(s["m2"][:, 0:Fw], s["q"][:, 0:Fw].bitcast(I32), 0x40000000,
                         None, ALU.is_lt, ALU.bypass)
        VE.tensor_scalar(s["m4"][:, 0:Fw], s["q"][:, 0:Fw].bitcast(I32), 0x40800000,
                         None, ALU.is_lt, ALU.bypass)
        VE.tensor_copy(s["q"][:, 0:Fw], s["r2c"][:, 0:Fw])
        VE.copy_predicated(s["q"][:, 0:Fw], s["m4"][:, 0:Fw], s["r1"][:, 0:Fw])
        VE.copy_predicated(s["q"][:, 0:Fw], s["m2"][:, 0:Fw], s["r05"][:, 0:Fw])
        VE.tensor_scalar(s["q"][:, 0:Fw], s["q"][:, 0:Fw], 6.0, -6.0, ALU.min, ALU.max)
        VE.tensor_tensor(qbf[:, 0:Fw].rearrange("p (b e) -> p b e", e=32),
                         s["q"][:, 0:Fw].rearrange("p (b e) -> p b e", e=32),
                         s["is"][:, 0:nb].bitcast(F32).unsqueeze(2)
                         .broadcast_to([128, nb, 32]), ALU.mult)
        return V.inc(spacer())

    def quant_a(at, qbf_ap, s, Fw):
        nb = Fw // 32
        VE.tensor_reduce(s["am"][:, 0:nb],
                         at[:, 0:Fw].rearrange("p (b e) -> p b e", e=32),
                         axis=mybir.AxisListType.X, op=ALU.max,
                         apply_absolute_value=True)
        spacer()
        _block_scales(s, s["am"][:, 0:nb], nb)
        VE.tensor_tensor(s["v"][:, 0:Fw].rearrange("p (b e) -> p b e", e=32),
                         at[:, 0:Fw].rearrange("p (b e) -> p b e", e=32),
                         s["ir"][:, 0:nb].bitcast(F32).unsqueeze(2)
                         .broadcast_to([128, nb, 32]), ALU.mult)
        VE.tensor_copy(s["f8"][:, 0:Fw], s["v"][:, 0:Fw])
        VE.tensor_scalar(s["q"][:, 0:Fw].bitcast(I32), s["v"][:, 0:Fw].bitcast(I32),
                         0x7FFFFFFF, None, ALU.bitwise_and, ALU.bypass)
        VE.tensor_scalar(s["r1"][:, 0:Fw], s["v"][:, 0:Fw], S_8TH, S_8TH,
                         ALU.add, ALU.subtract)
        VE.tensor_scalar(s["m2"][:, 0:Fw], s["q"][:, 0:Fw].bitcast(I32), 0x3F800000,
                         None, ALU.is_lt, ALU.bypass)
        VE.tensor_copy(s["q"][:, 0:Fw], s["f8"][:, 0:Fw])
        VE.copy_predicated(s["q"][:, 0:Fw], s["m2"][:, 0:Fw], s["r1"][:, 0:Fw])
        VE.tensor_scalar(s["q"][:, 0:Fw], s["q"][:, 0:Fw], 7.5, -7.5, ALU.min, ALU.max)
        VE.tensor_tensor(qbf_ap.rearrange("p (b e) -> p b e", e=32),
                         s["q"][:, 0:Fw].rearrange("p (b e) -> p b e", e=32),
                         s["is"][:, 0:nb].bitcast(F32).unsqueeze(2)
                         .broadcast_to([128, nb, 32]), ALU.mult)
        return V.inc(spacer())


    def transpose8(src_ap_fn, n_tiles, evac_fn):
        for b0 in range(0, n_tiles, 8):
            nb_ = min(8, n_tiles - b0)
            TE.wait_ge(SC.h, SC.n)
            TE.wait_ge(V.h, V.n)
            for j in range(nb_):
                mmt = TE.transpose(pTB[:, 128 * j:128 * (j + 1)], src_ap_fn(b0 + j),
                                   ident_bf[:, :])
            pe = PE.inc(mmt)
            SE.wait_ge(PE.h, pe)
            evac_fn(b0 // 8, pTB[:, 0:128 * nb_])

    def tapw(name, src_ap):
        if name in tap_bufs:
            SE.wait_ge(V.h, V.n)
            SE.wait_ge(SC.h, SC.n)
            return ST.inc(SE.dma_start(out=tap_bufs[name], in_=src_ap), 16)
        return None

    pwo = contextlib.ExitStack()
    pwi = contextlib.ExitStack()
    w_out_bf = sbuf("w_out_bf", [128, 4 * DM], BF16, pwo)
    w_in_bf = sbuf("w_in_bf", [128, 16 * 1024], BF16, pwi)

    p0 = contextlib.ExitStack()
    ident_f = sbuf("ident_f", [128, 128], F32, p0)
    rmsw_row = sbuf("rmsw_row", [1, DM], F32, p0)
    rmsw_bc = sbuf("rmsw_bc", [128, DM], F32, p0)

    nL = L_.inc(SY.dma_start(out=ident_f[:, :], in_=ident_in), 16)
    nL = L_.inc(SY.dma_start(out=convw_sb[:, :], in_=convw_in), 16)
    nL = L_.inc(SY.dma_start(out=convb_sb[:, :], in_=convb_in), 16)
    nL = L_.inc(SY.dma_start(out=dtb_sb[:, :], in_=dtb_in), 16)
    nL = L_.inc(SY.dma_start(out=dp_sb[:, :], in_=dp_in), 16)
    nL = L_.inc(SY.dma_start(out=rmsw_row[:, :], in_=rmsw_in), 16)
    nc.const_aps.register(F32, float(EPS), consts[:, 0:1])
    nc.const_aps.register(F32, 1.0, consts[:, 1:2])
    VE.wait_ge(L_.h, nL)
    VE.memset(consts[:, 0:1], float(EPS))
    VE.memset(consts[:, 1:2], 1.0)
    VE.memset(onesbf[:, :], 1.0)
    VE.memset(ones32[:, :], 1.0)
    VE.memset(junkB[:, :], 0.0)
    vE = V.inc(VE.tensor_copy(ident_bf[:, :], ident_f[:, :]))
    SY.wait_ge(V.h, vE)
    nLA = L_.inc(SY.dma_start(out=ident_f[:, 0:64], in_=alog_in), 16)
    SE.wait_ge(L_.h, nLA)
    SC.inc(SE.activation(A_sb[:, :], ident_f[:, 0:64], AF.Exp))
    VE.wait_ge(SC.h, SC.n)
    V.inc(VE.tensor_scalar(negA_sb[:, :], A_sb[:, :], -1.0, None, ALU.mult, ALU.bypass))

    TE.wait_ge(V.h, vE)
    for j in range(4):
        pe = PE.inc(TE.matmul(pR[:, :], ones32[:, :], rmsw_row[:, 512 * j:512 * (j + 1)],
                              start=True, stop=True))
        SE.wait_ge(PE.h, pe)
        scR = SC.inc(SE.copy(rmsw_bc[:, 512 * j:512 * (j + 1)], pR[:, :]))
        TE.wait_ge(SC.h, scR)

    xt = sbuf("xt", [128, DM], F32, p0)
    ln1t = sbuf("ln1t", [128, DM], F32, p0)
    hnt = sbuf("hnt", [128, DM], BF16, p0)
    hnev = sbuf("hnev", [128, 1024], BF16, p0)
    bns = sbuf("bns", [128, 24], F32, p0)
    mv = sbuf("mv", [128, 4], F32, p0)
    cf = sbuf("cf", [128, 4], F32, p0)

    st_hn = 0
    for i in range(16):
        SY.wait_ge(V.h, V.n)
        SY.wait_ge(SC.h, SC.n)
        nL = L_.inc(SY.dma_start(out=xt[:, :],
                                 in_=x_in.rearrange("(i p) f -> i p f", p=128)[i]), 16)
        VE.wait_ge(L_.h, nL)
        vw = ln_stats(xt, bns, mv)
        vw = ln_coefs(mv, cf, vw)
        VE.wait_ge(ST.h, ST.n)
        VE.tensor_scalar(ln1t[:, :], xt[:, :], cf[:, 0:1], cf[:, 1:2],
                         ALU.mult, ALU.add)
        spacer()
        vc1 = V.inc(spacer())
        vw2 = ln_stats(ln1t, bns, mv)
        SE.wait_ge(V.h, vw2)
        SE.activation(cf[:, 2:3], mv[:, 1:2], AF.Ln, bias=float(EPS))
        scg = SC.inc(SE.activation(cf[:, 2:3], cf[:, 2:3], AF.Exp, scale=-0.5))
        VE.wait_ge(SC.h, scg)
        VE.wait_ge(PE.h, PE.n)
        VE.scalar_tensor_tensor(hnt[:, :], ln1t[:, :], cf[:, 2:3],
                                rmsw_bc[:, :], ALU.mult, ALU.mult)
        spacer()
        vh = V.inc(spacer())
        SE.wait_ge(V.h, vc1)
        st_l = ST.inc(SE.dma_start(out=ln1_spill.rearrange("(i p) f -> i p f", p=128)[i],
                                   in_=ln1t[:, :]), 16)
        if i == 0:
            tapw("ln1", ln1t[:, :])
            tapw("hnt", hnt[:, :])
        TE.wait_ge(V.h, vh)
        for b in range(2):
            TE.wait_ge(SC.h, SC.n)
            for j in range(8):
                mmt = TE.transpose(pTB[:, 128 * j:128 * (j + 1)],
                                   hnt[:, 128 * (8 * b + j):128 * (8 * b + j + 1)],
                                   ident_bf[:, :])
            pe = PE.inc(mmt)
            SE.wait_ge(PE.h, pe)
            SE.wait_ge(ST.h, st_hn)
            SC.inc(SE.copy(hnev[:, :], pTB[:, :]))
            SE.copy(junkB[:, 0:4], junkA[:, 0:4])
            st_hn = ST.inc(SE.dma_start(
                out=hn_in_h[i // 8].rearrange("(a p) t -> a p t", p=128)[
                    8 * b:8 * b + 8][
                    :, :, 128 * (i % 8):128 * (i % 8 + 1)].transpose([1, 0, 2]),
                in_=hnev[:, :]), 16)
        if i == 7:
            GE.wait_ge(ST.h, st_hn)
            c_aghn[0].inc(GE.collective_compute(
                "AllGather", ALU.bypass, replica_groups=[list(range(NC))],
                ins=[hn_in_h[0].ap().opt()], outs=[hn_all_h[0].ap().opt()]))

    GE.wait_ge(ST.h, st_hn)
    c_aghn[1].inc(GE.collective_compute(
        "AllGather", ALU.bypass, replica_groups=[list(range(NC))],
        ins=[hn_in_h[1].ap().opt()], outs=[hn_all_h[1].ap().opt()]))

    wstage = sbuf("wstage", [128, DM], F32, p0)

    def load_cast(dst_ap, src_ap, width):
        SY.wait_ge(SC.h, SC.n)
        nL = L_.inc(SY.dma_start(out=wstage[:, 0:width], in_=src_ap), 16)
        SE.wait_ge(L_.h, nL)
        return SC.inc(SE.copy(dst_ap, wstage[:, 0:width]))

    for kk in range(16):
        load_cast(w_in_bf[:, 1024 * kk:1024 * (kk + 1)],
                  w_in_sl.rearrange("(a p) c -> a p c", p=128)[kk], 1024)
    for kd in range(4):
        load_cast(w_out_bf[:, DM * kd:DM * (kd + 1)],
                  wout_in.rearrange("(a p) c -> a p c", p=128)[kd], DM)
    for kd in range(4):
        load_cast(w_x_bf[:, 160 * kd:160 * (kd + 1)],
                  wx_in.rearrange("(a p) c -> a p c", p=128)[kd], 160)
    load_cast(w_dt_bf[:, :], wdt_in, DOWN)

    barrier()
    p0.close()

    pm1 = contextlib.ExitStack()
    rhs_sb = sbuf("rhs_sb", [128, 16 * 512], BF16, pm1)
    u_ext = [sbuf(f"u_ext{m}", [128, 515], BF16, pm1) for m in range(4)]
    halo = [sbuf(f"halo{m}", [128, 3], BF16, pm1) for m in range(4)]
    cacc = [sbuf("cacc0", [128, 512], F32, pm1), sbuf("cacc1", [128, 512], F32, pm1)]
    u_c = [sbuf(f"u_c{m}", [128, CT], BF16, pm1) for m in range(4)]
    siluz = [sbuf(f"siluz{m}", [128, CT], BF16, pm1) for m in range(4)]
    dtev = sbuf("dtev", [128, 512], F32, pm1)

    st_spill = 0
    sc_bank = [SC.n, SC.n]
    for k in range(NCHUNK):
        if k % 4 == 0:
            vhz = V.n
            for m in range(4):
                vhz = V.inc(VE.memset(halo[m][:, :], 0.0))
        for n in range(4):
            SY.wait_ge(PE.h, PE.n)
            SY.wait_ge(c_aghn[k // 4].h, 1)
            nL = L_.n
            for kk in range(16):
                for half in range(2):
                    r = 2 * n + half
                    nL = L_.inc(SY.dma_start(
                        out=rhs_sb[:, 512 * kk + 256 * half:512 * kk + 256 * (half + 1)],
                        in_=hn_all_h[k // 4].rearrange(
                            "(r a p) t -> r a p t", r=NC, p=128)[r][kk][
                            :, STR * (k % 4):STR * (k % 4 + 1)]), 16)
            TE.wait_ge(L_.h, nL)
            for m in range(8):
                pa = pA[m % 2]
                TE.wait_ge(SC.h, sc_bank[m % 2])
                for kk in range(16):
                    mm = TE.matmul(
                        pa[:, :],
                        w_in_bf[:, 1024 * kk + 128 * m:1024 * kk + 128 * (m + 1)],
                        rhs_sb[:, 512 * kk:512 * (kk + 1)],
                        start=(kk == 0), stop=(kk == 15))
                pe = PE.inc(mm)
                SE.wait_ge(PE.h, pe)
                if m < 4:
                    SE.wait_ge(V.h, V.n)
                    if n == 0:
                        SE.wait_ge(ST.h, st_spill)
                    sc1 = SC.inc(SE.copy(u_ext[m][:, 3:515], pa[:, :]))
                    sc_bank[m % 2] = sc1
                    VE.wait_ge(SC.h, sc1)
                    VE.tensor_copy(u_ext[m][:, 0:3], halo[m][:, :])
                    VE.tensor_scalar(cacc[0][:, :], u_ext[m][:, 0:512],
                                     convw_sb[:, 4 * m:4 * m + 1], None,
                                     ALU.mult, ALU.bypass)
                    for j in range(1, 4):
                        srcb, dstb = (cacc[0], cacc[1]) if j % 2 == 1 else (cacc[1], cacc[0])
                        VE.scalar_tensor_tensor(dstb[:, :], u_ext[m][:, j:j + 512],
                                                convw_sb[:, 4 * m + j:4 * m + j + 1],
                                                srcb[:, :], ALU.mult, ALU.add)
                    vcv = V.inc(VE.tensor_copy(halo[m][:, :], u_ext[m][:, 512:515]))
                    SE.wait_ge(V.h, vcv)
                    SC.inc(SE.activation(u_c[m][:, 512 * n:512 * (n + 1)], cacc[1][:, :],
                                         AF.Silu, bias=convb_sb[:, m:m + 1]))
                else:
                    if n == 0:
                        SE.wait_ge(ST.h, st_spill)
                    sc_bank[m % 2] = SC.inc(SE.activation(
                        siluz[m - 4][:, 512 * n:512 * (n + 1)], pa[:, :], AF.Silu))
            TE.wait_ge(SC.h, SC.n)
            for kd in range(4):
                TE.matmul(pB[0][:, :], w_x_bf[:, 160 * kd:160 * kd + 128],
                          u_c[kd][:, 512 * n:512 * (n + 1)],
                          start=(kd == 0), stop=(kd == 3))
                mm2 = TE.matmul(pB[1][0:32, :], w_x_bf[:, 160 * kd + 128:160 * (kd + 1)],
                                u_c[kd][:, 512 * n:512 * (n + 1)],
                                start=(kd == 0), stop=(kd == 3))
            pe = PE.inc(mm2)
            SE.wait_ge(PE.h, pe)
            SC.inc(SE.copy(dtev[:, :], pB[0][:, :]))
            st_dt = ST.inc(SE.dma_start(out=ar_in[k][0:128, 512 * n:512 * (n + 1)],
                                        in_=dtev[:, :]), 16)
            SE.wait_ge(ST.h, st_dt)
            SC.inc(SE.copy(dtev[0:32, :], pB[1][0:32, :]))
            st1 = ST.inc(SE.dma_start(out=ar_in[k][128:160, 512 * n:512 * (n + 1)],
                                      in_=dtev[0:32, :]), 16)
            SE.wait_ge(ST.h, st1)
        if k == 0:
            tapw("uc0", u_c[0][:, :])
            tapw("sz0", siluz[0][:, :])
        SE.wait_ge(SC.h, SC.n)
        for m in range(4):
            ST.inc(SE.dma_start(
                out=uc_spill.rearrange("(k m p) t -> k m p t", k=NCHUNK, p=128)[k][m],
                in_=u_c[m][:, :]), 16)
            st_spill = ST.inc(SE.dma_start(
                out=sz_spill.rearrange("(k m p) t -> k m p t", k=NCHUNK, p=128)[k][m],
                in_=siluz[m][:, :]), 16)
        GE.wait_ge(ST.h, st_spill)
        c_ar[k].inc(GE.collective_compute(
            "AllReduce", ALU.add, replica_groups=[list(range(NC))],
            ins=[ar_in[k].ap().opt()], outs=[ar_out[k].ap().opt()]))

    pq = contextlib.ExitStack()
    wstage2 = sbuf("wstage2", [128, DM], F32, pq)
    qs = {
        "v": sbuf("q_v", [128, DM], F32, pq),
        "r05": sbuf("q_r05", [128, DM], F32, pq),
        "r1": sbuf("q_r1", [128, DM], F32, pq),
        "r2c": sbuf("q_r2c", [128, DM], F32, pq),
        "q": sbuf("q_q", [128, DM], F32, pq),
        "f8": sbuf("q_f8", [128, DM], FP8, pq),
        "m2": sbuf("q_m2", [128, DM], U8, pq),
        "m4": sbuf("q_m4", [128, DM], U8, pq),
        "am": sbuf("q_am", [128, 64], F32, pq),
        "ie": sbuf("q_ie", [128, 64], I32, pq),
        "ir": sbuf("q_ir", [128, 64], I32, pq),
        "is": sbuf("q_is", [128, 64], I32, pq),
    }
    qbf = sbuf("qbf", [128, DM], BF16, pq)
    qev = sbuf("qev", [128, 1024], BF16, pq)

    def q_t_store(src_tile_ap, n_kf, dst, col0):
        last = ST.n
        for b in range(n_kf // 8):
            TE.wait_ge(SC.h, SC.n)
            for j in range(8):
                kf = 8 * b + j
                mmt = TE.transpose(pTB[:, 128 * j:128 * (j + 1)],
                                   qbf[:, 128 * kf:128 * (kf + 1)], ident_bf[:, :])
            pe = PE.inc(mmt)
            SE.wait_ge(PE.h, pe)
            SE.wait_ge(ST.h, last)
            SC.inc(SE.copy(qev[:, :], pTB[:, :]))
            SE.copy(junkB[:, 0:4], junkA[:, 0:4])
            last = ST.inc(SE.dma_start(
                out=dst.rearrange("(a p) c -> a p c", p=128)[8 * b:8 * b + 8][
                    :, :, col0:col0 + 128].transpose([1, 0, 2]),
                in_=qev[:, :]), 16)
        return last

    for wt_dram, dst in ((wg_in, wgq_in), (wu_in, wuq_in)):
        for it in range(8):
            SY.wait_ge(V.h, V.n)
            nL = L_.inc(SY.dma_start(
                out=wstage2[:, :], in_=wt_dram.rearrange("(a p) c -> a p c", p=128)[it]), 16)
            VE.wait_ge(L_.h, nL)
            VE.wait_ge(PE.h, PE.n)
            vqd = quant_w(wstage2, qbf, qs, DM)
            TE.wait_ge(V.h, vqd)
            stq = q_t_store(0, 16, dst, 128 * it)
        GE.wait_ge(ST.h, stq)
        sem = c_agwg if wt_dram is wg_in else c_agwu
        sem.inc(GE.collective_compute(
            "AllGather", ALU.bypass, replica_groups=[list(range(NC))],
            ins=[(wgq_in if wt_dram is wg_in else wuq_in).ap().opt()],
            outs=[(wgq_all if wt_dram is wg_in else wuq_all).ap().opt()]))

    for it in range(2):
        for sub in range(4):
            SY.wait_ge(V.h, V.n)
            nL = L_.inc(SY.dma_start(
                out=wstage2[:, :],
                in_=wd_in.rearrange("(a p) (s c) -> a s p c", p=128, c=DM)[it][sub]), 16)
            VE.wait_ge(L_.h, nL)
            VE.wait_ge(PE.h, PE.n)
            vqd = quant_w(wstage2, qbf, qs, DM)
            TE.wait_ge(V.h, vqd)
            last = ST.n
            for b in range(2):
                TE.wait_ge(SC.h, SC.n)
                for j in range(8):
                    kf = 8 * b + j
                    mmt = TE.transpose(pTB[:, 128 * j:128 * (j + 1)],
                                       qbf[:, 128 * kf:128 * (kf + 1)], ident_bf[:, :])
                pe = PE.inc(mmt)
                SE.wait_ge(PE.h, pe)
                SE.wait_ge(ST.h, last)
                SC.inc(SE.copy(qev[:, :], pTB[:, :]))
                SE.copy(junkB[:, 0:4], junkA[:, 0:4])
                last = ST.inc(SE.dma_start(
                    out=wdq_in.rearrange("(a p) c -> a p c", p=128)[
                        sub * 16 + 8 * b:sub * 16 + 8 * b + 8][
                        :, :, 128 * it:128 * (it + 1)].transpose([1, 0, 2]),
                    in_=qev[:, :]), 16)
    GE.wait_ge(ST.h, last)
    c_agwd.inc(GE.collective_compute(
        "AllGather", ALU.bypass, replica_groups=[list(range(NC))],
        ins=[wdq_in.ap().opt()], outs=[wdq_all.ap().opt()]))

    pq.close()

    barrier()
    pm1.close()
    pwi.close()

    pm3 = contextlib.ExitStack()
    HC = CT // 2
    u_c2b = [[sbuf(f"u_c2{p}_{m}", [128, HC], BF16, pm3) for m in range(4)]
             for p in range(2)]
    siluz2b = [[sbuf(f"sz2{p}_{m}", [128, HC], BF16, pm3) for m in range(4)]
               for p in range(2)]
    dtevB = sbuf("dtevB", [128, 512], F32, pm3)
    dt_bf = sbuf("dt_bf", [128, CT], BF16, pm3)
    bc_sb = sbuf("bc_sb", [32, CT], F32, pm3)
    bc_bf = sbuf("bc_bf", [32, CT], BF16, pm3)
    Bbc = [sbuf("BbcA", [128, 16 * TT], BF16, pm3),
           sbuf("BbcB", [128, 16 * TT], BF16, pm3)]
    Cbc = [sbuf("CbcA", [128, 16 * TT], BF16, pm3),
           sbuf("CbcB", [128, 16 * TT], BF16, pm3)]
    delta_sb = [sbuf(f"delta{m}", [128, HC], F32, pm3) for m in range(4)]
    sp_a = sbuf("sp_a", [128, 512], F32, pm3)
    sp_b = sbuf("sp_b", [128, 512], F32, pm3)
    sp_c = sbuf("sp_c", [128, 512], F32, pm3)
    dA2 = [sbuf("dAA", [128, 16 * TT], F32, pm3), sbuf("dAB", [128, 16 * TT], F32, pm3)]
    x2b = [sbuf("x2bA", [128, 16 * TT], BF16, pm3), sbuf("x2bB", [128, 16 * TT], BF16, pm3)]
    du_sb = sbuf("du_sb", [128, TT], F32, pm3)
    h_sb = sbuf("h_sb", [128, 16 * TT], BF16, pm3)
    y23b = [[sbuf(f"y23_{p}_{m}", [128, TT], BF16, pm3) for m in range(4)]
            for p in range(2)]
    carry = sbuf("carry", [128, 64], F32, pm3)
    oo_sb = sbuf("oo_sb", [128, 1024], F32, pm3)
    bc_dram = [dram(f"bc_dram{kq}", [32, CT], BF16) for kq in range(NCHUNK)]

    def uc_load(H):
        kq, hq = H // 2, H % 2
        SY.wait_ge(V.h, V.n)
        SY.wait_ge(G.h, G.n)
        n = L_.n
        for m in range(4):
            n = L_.inc(SY.dma_start(
                out=u_c2b[H % 2][m][:, :],
                in_=uc_spill.rearrange("(k m p) t -> k m p t", k=NCHUNK, p=128)[kq][m][
                    :, HC * hq:HC * (hq + 1)]), 16)
            n = L_.inc(SY.dma_start(
                out=siluz2b[H % 2][m][:, :],
                in_=sz_spill.rearrange("(k m p) t -> k m p t", k=NCHUNK, p=128)[kq][m][
                    :, HC * hq:HC * (hq + 1)]), 16)
        return n

    uc_ready = [0, 0]
    uc_ready[0] = uc_load(0)
    def bc_bcast_load(kq, tauq, bparq):
        SY.wait_ge(G.h, bc_guard[bparq][0])
        SY.wait_ge(V.h, bc_guard[bparq][1])
        SY.wait_ge(ST.h, st_bc_all[kq])
        t0q = TT * tauq
        n = L_.inc(SY.dma_start(
            out=Bbc[bparq][:, :].rearrange("p (s t) -> p s t", s=16),
            in_=bc_dram[kq].ap()[0:16, t0q:t0q + TT]
            .unsqueeze(0).broadcast_to([128, 16, TT])), 16)
        n = L_.inc(SY.dma_start(
            out=Cbc[bparq][:, :].rearrange("p (s t) -> p s t", s=16),
            in_=bc_dram[kq].ap()[16:32, t0q:t0q + TT]
            .unsqueeze(0).broadcast_to([128, 16, TT])), 16)
        return n

    st_bc_all = [0] * NCHUNK
    bc_guard = [[0, 0], [0, 0]]
    nB = [0] * NTAU
    sc_exp = [SC.n, SC.n]
    v_scans = [V.n, V.n]
    v_tree = [V.n, V.n]
    g_x = [G.n, G.n]
    for k in range(NCHUNK):
        if k % 4 == 0:
            VE.wait_ge(PE.h, PE.n)
            VE.memset(carry[:, :], 0.0)
            V.inc(spacer())
        SY.wait_ge(c_ar[k].h, 1)
        for hh in range(4):
            SY.wait_ge(SC.h, SC.n)
            nL = L_.inc(SY.dma_start(out=dtevB[:, :],
                                     in_=ar_out[k][0:128, 512 * hh:512 * (hh + 1)]), 16)
            SE.wait_ge(L_.h, nL)
            SE.wait_ge(PE.h, PE.n)
            SC.inc(SE.copy(dt_bf[:, 512 * hh:512 * (hh + 1)], dtevB[:, :]))
        SY.wait_ge(V.h, V.n)
        nL = L_.inc(SY.dma_start(out=bc_sb[:, :], in_=ar_out[k][128:160, :]), 16)
        VE.wait_ge(L_.h, nL)
        VE.tensor_copy(bc_bf[:, :], bc_sb[:, :])
        spacer()
        vbc = V.inc(spacer())
        SE.wait_ge(V.h, vbc)
        st_bc_all[k] = ST.inc(SE.dma_start(out=bc_dram[k].ap(), in_=bc_bf[:, :]), 16)
        if k > 0:
            nB[0] = bc_bcast_load(k, 0, 0)
        if k == 0:
            tapw("dtbf0", dt_bf[:, :])

        pe_wout = PE.n
        pe_wout2 = [PE.n, PE.n]
        for half in range(2):
            ht0 = HC * half
            VE.wait_ge(PE.h, pe_wout)
            VE.wait_ge(G.h, G.n)
            for m in range(4):
                for nn in range(2):
                    pa = pA[nn % 2]
                    TE.wait_ge(SC.h, SC.n)
                    pe = PE.inc(TE.matmul(pa[:, :], w_dt_bf[:, 128 * m:128 * (m + 1)],
                                          dt_bf[:, ht0 + 512 * nn:ht0 + 512 * (nn + 1)],
                                          start=True, stop=True))
                    SE.wait_ge(PE.h, pe)
                    SE.activation(sp_a[:, :], pa[:, :], AF.Relu, bias=dtb_sb[:, m:m + 1])
                    SE.activation(sp_b[:, :], pa[:, :], AF.Abs, bias=dtb_sb[:, m:m + 1])
                    SE.activation(sp_c[:, :], sp_b[:, :], AF.Exp, scale=-1.0)
                    scd = SC.inc(SE.activation(sp_b[:, :], sp_c[:, :], AF.Ln, bias=1.0))
                    VE.wait_ge(SC.h, scd)
                    VE.tensor_tensor(delta_sb[m][:, 512 * nn:512 * (nn + 1)],
                                     sp_a[:, :], sp_b[:, :], ALU.add)
            spacer()
            v_delta = V.inc(spacer())
            H = 2 * k + half
            u_c2 = u_c2b[H % 2]
            siluz2 = siluz2b[H % 2]
            if H + 1 < 2 * NCHUNK:
                uc_ready[(H + 1) % 2] = uc_load(H + 1)
            VE.wait_ge(L_.h, uc_ready[H % 2])
            GE.wait_ge(L_.h, uc_ready[H % 2])
            for tau4 in range(4):
                tau = half * 4 + tau4
                t0 = TT * tau
                tl0 = TT * tau4
                bpar = tau % 2
                if k == 0 and half == 0 and tau4 == 0:
                    nB[0] = bc_bcast_load(0, 0, 0)
                if tau + 1 < NTAU:
                    nB[tau + 1] = bc_bcast_load(k, tau + 1, (tau + 1) % 2)
                y23 = y23b[tau % 2]
                VE.wait_ge(PE.h, pe_wout2[tau % 2])
                for m in range(4):
                    par = m % 2
                    SE.wait_ge(V.h, v_delta)
                    SE.wait_ge(V.h, v_scans[par])
                    for s in range(16):
                        se_i = SE.activation(dA2[par][:, TT * s:TT * (s + 1)],
                                             delta_sb[m][:, tl0:tl0 + TT], AF.Exp,
                                             scale=negA_sb[:, 16 * m + s:16 * m + s + 1])
                    sc_exp[par] = SC.inc(se_i)
                    GE.wait_ge(V.h, v_delta)
                    GE.wait_ge(V.h, v_scans[par])
                    GE.wait_ge(V.h, v_tree[par])
                    GE.wait_ge(L_.h, nB[tau])
                    GE.tensor_tensor(du_sb[:, :], delta_sb[m][:, tl0:tl0 + TT],
                                     u_c2[m][:, tl0:tl0 + TT], ALU.mult)
                    gx = G.inc(GE.tensor_tensor(
                        x2b[par][:, :].rearrange("p (s t) -> p s t", s=16),
                        du_sb[:, :].unsqueeze(1).broadcast_to([128, 16, TT]),
                        Bbc[bpar][:, :].rearrange("p (s t) -> p s t", s=16), ALU.mult))
                    g_x[par] = gx
                    VE.wait_ge(SC.h, sc_exp[par])
                    VE.wait_ge(G.h, gx)
                    for s in range(16):
                        VE.tensor_tensor_scan(h_sb[:, TT * s:TT * (s + 1)],
                                              dA2[par][:, TT * s:TT * (s + 1)],
                                              x2b[par][:, TT * s:TT * (s + 1)],
                                              carry[:, 16 * m + s:16 * m + s + 1],
                                              ALU.mult, ALU.add)
                    spacer()
                    v_scans[par] = V.inc(spacer())
                    VE.tensor_tensor(x2b[par][:, :], h_sb[:, :], Cbc[bpar][:, :],
                                     ALU.mult)
                    dp_ = x2b[par]
                    VE.tensor_tensor(dp_[:, 0:8 * TT], dp_[:, 0:8 * TT],
                                     dp_[:, 8 * TT:16 * TT], ALU.add)
                    VE.tensor_tensor(dp_[:, 8 * TT:12 * TT], dp_[:, 0:4 * TT],
                                     dp_[:, 4 * TT:8 * TT], ALU.add)
                    VE.tensor_tensor(dp_[:, 12 * TT:14 * TT], dp_[:, 8 * TT:10 * TT],
                                     dp_[:, 10 * TT:12 * TT], ALU.add)
                    VE.tensor_tensor(dp_[:, 14 * TT:15 * TT], dp_[:, 12 * TT:13 * TT],
                                     dp_[:, 13 * TT:14 * TT], ALU.add)
                    VE.scalar_tensor_tensor(y23[m][:, :], u_c2[m][:, tl0:tl0 + TT],
                                            dp_sb[:, m:m + 1], dp_[:, 14 * TT:15 * TT],
                                            ALU.mult, ALU.add)
                    VE.tensor_tensor(y23[m][:, :], y23[m][:, :],
                                     siluz2[m][:, tl0:tl0 + TT], ALU.mult)
                    v_tree[par] = V.inc(VE.tensor_copy(
                        carry[:, 16 * m:16 * (m + 1)],
                        h_sb[:, :].rearrange("p (s t) -> p s t", s=16)[:, :, TT - 1]))
                bc_guard[bpar] = [G.n, V.n]
                spacer()
                v_y = V.inc(spacer())
                TE.wait_ge(V.h, v_y)
                for mh in range(2):
                    for fn in range(4):
                        pb = pB[fn % 2]
                        TE.wait_ge(SC.h, SC.n)
                        for kd in range(4):
                            mmo = TE.matmul(
                                pb[:, :], y23[kd][:, 128 * mh:128 * (mh + 1)],
                                w_out_bf[:, DM * kd + 512 * fn:DM * kd + 512 * (fn + 1)],
                                start=(kd == 0), stop=(kd == 3))
                        pe_wout = PE.inc(mmo)
                        pe_wout2[tau % 2] = pe_wout
                        SE.wait_ge(PE.h, pe_wout)
                        if fn % 2 == 0:
                            SE.wait_ge(ST.h, ST.n)
                        SC.inc(SE.copy(oo_sb[:, 512 * (fn % 2):512 * (fn % 2 + 1)],
                                       pb[:, :]))
                        ST.inc(SE.dma_start(
                            out=rs_in[k][t0 + 128 * mh:t0 + 128 * (mh + 1),
                                         512 * fn:512 * (fn + 1)],
                            in_=oo_sb[:, 512 * (fn % 2):512 * (fn % 2 + 1)]), 16)
        if k == 0:
            tapw("y230", y23[0][:, :])
        GE.wait_ge(ST.h, ST.n)
        c_rs[k].inc(GE.collective_compute(
            "ReduceScatter", ALU.add, replica_groups=[list(range(NC))],
            ins=[rs_in[k].ap().opt()], outs=[rs_out[k].ap().opt()]))

    barrier()
    pm3.close()
    pwo.close()

    pfc = contextlib.ExitStack()
    fbns = sbuf("fbns", [128, 24], F32, pfc)
    fmv = sbuf("fmv", [128, 4], F32, pfc)
    fcf = sbuf("fcf", [128, 4], F32, pfc)

    pf12 = contextlib.ExitStack()
    xqT = sbuf("xqT", [128, 16 * TOK], BF16, pf12)
    fqs = {
        "v": sbuf("f_v", [128, DM], F32, pf12),
        "r1": sbuf("f_r1", [128, DM], F32, pf12),
        "q": sbuf("f_q", [128, DM], F32, pf12),
        "f8": sbuf("f_f8", [128, DM], FP8, pf12),
        "m2": sbuf("f_m2", [128, DM], U8, pf12),
        "am": sbuf("f_am", [128, 64], F32, pf12),
        "ie": sbuf("f_ie", [128, 64], I32, pf12),
        "ir": sbuf("f_ir", [128, 64], I32, pf12),
        "is": sbuf("f_is", [128, 64], I32, pf12),
    }

    pf1 = contextlib.ExitStack()
    xt2 = sbuf("xt2", [128, DM], F32, pf1)
    at2 = sbuf("at2", [128, DM], F32, pf1)
    x2t = sbuf("x2t", [128, DM], F32, pf1)
    x3n = sbuf("x3n", [128, DM], F32, pf1)
    xq_bf = sbuf("xq_bf", [128, DM], BF16, pf1)

    if "rs0" in tap_bufs:
        GE.wait_ge(c_rs[0].h, 1)
        ST.inc(GE.dma_start(out=tap_bufs["rs0"], in_=rs_out[0].ap()), 16)
    for i in range(16):
        kchunk, half = i // 2, i % 2
        SY.wait_ge(c_rs[kchunk].h, 1)
        SY.wait_ge(V.h, V.n)
        SY.wait_ge(SC.h, SC.n)
        nL = L_.inc(SY.dma_start(out=xt2[:, :],
                                 in_=x_in.rearrange("(i p) f -> i p f", p=128)[i]), 16)
        nL = L_.inc(SY.dma_start(out=at2[:, :],
                                 in_=ln1_spill.rearrange("(i p) f -> i p f", p=128)[i]), 16)
        nL = L_.inc(SY.dma_start(out=x2t[:, :],
                                 in_=rs_out[kchunk].rearrange("(h p) f -> h p f", p=128)[half]),
                    16)
        VE.wait_ge(L_.h, nL)
        VE.wait_ge(ST.h, ST.n)
        VE.tensor_tensor(at2[:, :], at2[:, :], x2t[:, :], ALU.add)
        vw = ln_stats(at2, fbns, fmv)
        vw = ln_coefs(fmv, fcf, vw)
        VE.tensor_scalar(x2t[:, :], at2[:, :], fcf[:, 0:1], fcf[:, 1:2],
                         ALU.mult, ALU.add)
        VE.tensor_tensor(x2t[:, :], x2t[:, :], xt2[:, :], ALU.add)
        spacer()
        vx2 = V.inc(spacer())
        SE.wait_ge(V.h, vx2)
        ST.inc(SE.dma_start(out=x2_spill.rearrange("(i p) f -> i p f", p=128)[i],
                            in_=x2t[:, :]), 16)
        vw = ln_stats(x2t, fbns, fmv)
        vw = ln_coefs(fmv, fcf, vw)
        VE.tensor_scalar(x3n[:, :], x2t[:, :], fcf[:, 0:1], fcf[:, 1:2],
                         ALU.mult, ALU.add)
        VE.wait_ge(PE.h, PE.n)
        vq = quant_a(x3n, xq_bf[:, :], fqs, DM)
        if i == 0:
            tapw("xq0", xq_bf[:, :])
        TE.wait_ge(V.h, vq)
        for b in range(2):
            TE.wait_ge(SC.h, SC.n)
            for j in range(8):
                kf = 8 * b + j
                mmt = TE.transpose(pTB[:, 128 * j:128 * (j + 1)],
                                   xq_bf[:, 128 * kf:128 * (kf + 1)], ident_bf[:, :])
            pe = PE.inc(mmt)
            SE.wait_ge(PE.h, pe)
            SC.inc(SE.copy(
                xqT[:, :].rearrange("p (a t) -> p a t", t=2048)[
                    :, 8 * b:8 * b + 8, 128 * i:128 * (i + 1)],
                pTB[:, :].rearrange("p (a t) -> p a t", t=128)))

    tapw("xqTv", xqT[:, 0:2048])
    barrier()
    pf1.close()

    if "x2sp" in tap_bufs:
        GE.wait_ge(ST.h, ST.n)
        ST.inc(GE.dma_start(out=tap_bufs["x2sp"], in_=x2_spill.ap()), 16)
    pf2 = contextlib.ExitStack()
    wgt = [sbuf("wgtA", [128, 16 * 512], BF16, pf2),
           sbuf("wgtB", [128, 16 * 512], BF16, pf2)]
    wut = [sbuf("wutA", [128, 16 * 512], BF16, pf2),
           sbuf("wutB", [128, 16 * 512], BF16, pf2)]
    sg_sb = [sbuf("sg_sbA", [128, 512], F32, pf2),
             sbuf("sg_sbB", [128, 512], F32, pf2)]
    interB = [sbuf("interA", [128, 16 * 512], BF16, pf2),
              sbuf("interC", [128, 16 * 512], BF16, pf2)]
    iqev = [sbuf("iqevA", [128, 512], BF16, pf2), sbuf("iqevB", [128, 512], BF16, pf2)]

    SY.wait_ge(c_agwg.h, 1)
    SY.wait_ge(c_agwu.h, 1)
    sc_silu = [SC.n, SC.n]
    v_int = [V.n, V.n]

    def wgu_load(ic, wpar, pe_guard):
        r, icc = ic // 2, ic % 2
        SY.wait_ge(PE.h, pe_guard)
        n = L_.n
        for kf in range(16):
            n = L_.inc(SY.dma_start(
                out=wgt[wpar][:, 512 * kf:512 * (kf + 1)],
                in_=wgq_all.rearrange("(r a p) c -> r a p c", r=NC, p=128)[r][kf][
                    :, 512 * icc:512 * (icc + 1)]), 16)
            n = L_.inc(SY.dma_start(
                out=wut[wpar][:, 512 * kf:512 * (kf + 1)],
                in_=wuq_all.rearrange("(r a p) c -> r a p c", r=NC, p=128)[r][kf][
                    :, 512 * icc:512 * (icc + 1)]), 16)
        return n

    wgu_ready = [wgu_load(0, 0, 0), wgu_load(1, 1, 0)]
    pe_ic = [0, 0]
    st_iq = [ST.n, ST.n]

    def emit_tr(tic, vq_t):
        rr, ticc = tic // 2, tic % 2
        gI0 = rr * 1024 + ticc * 512
        TE.wait_ge(V.h, vq_t)
        for mt in range(16):
            par = mt % 2
            SE.wait_ge(ST.h, st_iq[par])
            TE.wait_ge(SC.h, SC.n)
            for sub in range(4):
                mmt = TE.transpose(
                    pTB[:, 128 * sub:128 * (sub + 1)],
                    interB[tic % 2][:, 512 * mt + 128 * sub:
                                    512 * mt + 128 * (sub + 1)],
                    ident_bf[:, :])
            pe = PE.inc(mmt)
            SE.wait_ge(PE.h, pe)
            SC.inc(SE.copy(iqev[par][:, :], pTB[:, 0:512]))
            SE.copy(junkB[:, 0:4], junkA[:, 0:4])
            st_iq[par] = ST.inc(SE.dma_start(
                out=ikt_spill.rearrange("(a p) t -> a p t", p=128)[
                    gI0 // 128:gI0 // 128 + 4][:, :, 128 * mt:128 * (mt + 1)]
                .transpose([1, 0, 2]),
                in_=iqev[par][:, :]), 16)

    vq_prev = None
    for ic in range(16):
        wpar = ic % 2
        inter = interB[ic % 2]
        VE.wait_ge(PE.h, PE.n)
        nL = wgu_ready[wpar]
        TE.wait_ge(L_.h, nL)
        for mt in range(16):
            fpar = mt % 2
            pg, pu = pA[fpar], pB[fpar]
            TE.wait_ge(SC.h, sc_silu[fpar])
            TE.wait_ge(V.h, v_int[fpar])
            for kf in range(16):
                TE.matmul(pg[:, :],
                          xqT[:, 2048 * kf + 128 * mt:2048 * kf + 128 * (mt + 1)],
                          wgt[wpar][:, 512 * kf:512 * (kf + 1)],
                          start=(kf == 0), stop=(kf == 15))
            for kf in range(16):
                mmu = TE.matmul(pu[:, :],
                                xqT[:, 2048 * kf + 128 * mt:2048 * kf + 128 * (mt + 1)],
                                wut[wpar][:, 512 * kf:512 * (kf + 1)],
                                start=(kf == 0), stop=(kf == 15))
            pe = PE.inc(mmu)
            SE.wait_ge(PE.h, pe)
            scS = SC.inc(SE.activation(sg_sb[fpar][:, :], pg[:, :], AF.Silu))
            sc_silu[fpar] = scS
            VE.wait_ge(SC.h, scS)
            v_int[fpar] = V.inc(VE.tensor_tensor(
                inter[:, 512 * mt:512 * (mt + 1)], sg_sb[fpar][:, :],
                pu[:, :], ALU.mult))
        if ic == 0:
            tapw("wgt0", wgt[0][:, :])
            tq = tapw("intpre", inter[:, :])
            if tq:
                VE.wait_ge(ST.h, tq)
        pe_ic[wpar] = PE.n
        if ic + 2 < 16:
            wgu_ready[wpar] = wgu_load(ic + 2, wpar, pe_ic[wpar])
        for sub in range(4):
            vq = quant_a(inter[:, 2048 * sub:2048 * (sub + 1)],
                         inter[:, 2048 * sub:2048 * (sub + 1)], fqs, 2048)
        if ic == 0:
            tapw("iq0", inter[:, :])
        if vq_prev is not None:
            emit_tr(ic - 1, vq_prev)
        vq_prev = vq
    emit_tr(15, vq_prev)

    barrier()
    pf2.close()
    pf12.close()

    pf3 = contextlib.ExitStack()
    ikt = [sbuf("iktA", [128, 64 * 128], BF16, pf3),
           sbuf("iktB", [128, 64 * 128], BF16, pf3)]
    wdt_s = [sbuf("wdt_sA", [128, 64 * 512], BF16, pf3),
             sbuf("wdt_sB", [128, 64 * 512], BF16, pf3)]
    dstage = [sbuf("dstgA", [128, 512], F32, pf3),
              sbuf("dstgB", [128, 512], F32, pf3)]
    dout_f = sbuf("dout_f", [128, DM], F32, pf3)
    ln4t = sbuf("ln4t", [128, DM], F32, pf3)
    xt3 = sbuf("xt3", [128, DM], F32, pf3)
    ot3 = sbuf("ot3a", [128, DM], F32, pf3)

    SY.wait_ge(c_agwd.h, 1)

    def wd_load(fb2, par, pe_guard):
        SY.wait_ge(PE.h, pe_guard)
        nW = L_.n
        for half in range(2):
            fb = 2 * fb2 + half
            for kg in range(16):
                nW = L_.inc(SY.dma_start(
                    out=wdt_s[par][:, 2048 * kg:2048 * (kg + 1)].rearrange(
                        "p (kk c) -> p kk c", c=512)[:, :, 256 * half:256 * (half + 1)],
                    in_=wdq_all.rearrange("(r kk p) c -> r p kk c", r=NC, p=128)[fb][
                        :, 4 * kg:4 * (kg + 1), :]), 16)
        return nW

    def ikt_load(mt, par, pe_guard):
        SY.wait_ge(PE.h, pe_guard)
        nL = L_.n
        for kg in range(16):
            nL = L_.inc(SY.dma_start(
                out=ikt[par][:, 512 * kg:512 * (kg + 1)].rearrange(
                    "p (kk t) -> p kk t", t=128),
                in_=ikt_spill.rearrange("(kk p) t -> p kk t", p=128)[
                    :, 4 * kg:4 * (kg + 1), 128 * mt:128 * (mt + 1)]), 16)
        return nL

    wd_ready = [wd_load(0, 0, 0), wd_load(1, 1, 0)]
    ikt_ready = [ikt_load(0, 0, 0), ikt_load(1, 1, 0)]
    st_do = [ST.n, ST.n]
    for fb2 in range(4):
        wpar = fb2 % 2
        for mt in range(16):
            ipar = mt % 2
            TE.wait_ge(L_.h, max(ikt_ready[ipar], wd_ready[wpar]))
            pb = pB[mt % 2]
            TE.wait_ge(SC.h, SC.n)
            TE.wait_ge(V.h, V.n)
            for kk in range(64):
                mmd = TE.matmul(pb[:, :], ikt[ipar][:, 128 * kk:128 * (kk + 1)],
                                wdt_s[wpar][:, 512 * kk:512 * (kk + 1)],
                                start=(kk == 0), stop=(kk == 63))
            mjunk = TE.matmul(pR[:, 0:128], onesbf[:, :], onesbf[:, :],
                              start=True, stop=True)
            pe = PE.inc(mjunk)
            nxt = mt + 2
            if nxt < 16:
                ikt_ready[ipar] = ikt_load(nxt, ipar, pe)
            elif fb2 < 3:
                ikt_ready[ipar] = ikt_load(nxt - 16, ipar, pe)
            if mt == 0 and fb2 in (1, 2):
                wd_ready[(fb2 + 1) % 2] = wd_load(fb2 + 1, (fb2 + 1) % 2, pe)
            VE.wait_ge(PE.h, pe)
            VE.wait_ge(ST.h, st_do[ipar])
            spacer()
            V.inc(VE.tensor_copy(dstage[ipar][:, :], pb[:, :]))
            spacer()
            spacer()
            vo = V.inc(spacer())
            SE.wait_ge(V.h, vo)
            st_do[ipar] = ST.inc(SE.dma_start(
                out=dout_spill.rearrange("(i p) f -> i p f", p=128)[mt][
                    :, 512 * fb2:512 * (fb2 + 1)],
                in_=dstage[ipar][:, :]), 16)

    SY.wait_ge(ST.h, ST.n)
    st_out = ST.n
    for mt in range(16):
        SY.wait_ge(V.h, V.n)
        nL2 = L_.inc(SY.dma_start(out=dout_f[:, :],
                                  in_=dout_spill.rearrange("(i p) f -> i p f", p=128)[mt]), 16)
        nL2 = L_.inc(SY.dma_start(out=xt3[:, :],
                                  in_=x2_spill.rearrange("(i p) f -> i p f", p=128)[mt]), 16)
        VE.wait_ge(L_.h, nL2)
        vw = ln_stats(dout_f, fbns, fmv)
        vw = ln_coefs(fmv, fcf, vw)
        VE.tensor_scalar(ln4t[:, :], dout_f[:, :], fcf[:, 0:1], fcf[:, 1:2],
                         ALU.mult, ALU.add)
        VE.wait_ge(ST.h, st_out)
        VE.tensor_tensor(ot3[:, :], xt3[:, :], ln4t[:, :], ALU.add)
        spacer()
        spacer()
        vo = V.inc(spacer())
        SE.wait_ge(V.h, vo)
        st_out = ST.inc(SE.dma_start(
            out=out_own.rearrange("(i p) f -> i p f", p=128)[mt],
            in_=ot3[:, :]), 16)

    SY.wait_ge(ST.h, ST.n)
    GE.wait_ge(ST.h, ST.n)

    pf3.close()
    pfc.close()
    es.close()
    return nc


def _slice_inputs(inputs, c):
    x = np.asarray(inputs["L0_attn_x"]).reshape(NT, DM)
    stripes = [x[(k * NC + c) * STR:(k * NC + c) * STR + STR] for k in range(NCHUNK)]
    x_own = np.ascontiguousarray(np.concatenate(stripes, axis=0))
    W_in = np.asarray(inputs["L0_attn_W_in"])
    w_in_sl = np.ascontiguousarray(np.concatenate(
        [W_in[:, c * DOWN:(c + 1) * DOWN], W_in[:, DI + c * DOWN:DI + (c + 1) * DOWN]],
        axis=1))
    cw = np.asarray(inputs["L0_attn_conv_w"])[c * DOWN:(c + 1) * DOWN, 0, :]
    convw = np.ascontiguousarray(cw.reshape(4, 128, 4).transpose(1, 0, 2).reshape(128, 16))
    convb = np.ascontiguousarray(
        np.asarray(inputs["L0_attn_conv_b"])[c * DOWN:(c + 1) * DOWN].reshape(4, 128).T)
    wx_sl = np.ascontiguousarray(np.asarray(inputs["L0_attn_W_x"])[c * DOWN:(c + 1) * DOWN])
    wdt_sl = np.ascontiguousarray(
        np.asarray(inputs["L0_attn_W_dt"])[:, c * DOWN:(c + 1) * DOWN])
    dtb = np.ascontiguousarray(
        np.asarray(inputs["L0_attn_dt_bias"])[c * DOWN:(c + 1) * DOWN].reshape(4, 128).T)
    alog = np.ascontiguousarray(
        np.asarray(inputs["L0_attn_A_log"])[c * DOWN:(c + 1) * DOWN].reshape(4, 128, 16)
        .transpose(1, 0, 2).reshape(128, 64))
    dp = np.ascontiguousarray(
        np.asarray(inputs["L0_attn_Dp"])[c * DOWN:(c + 1) * DOWN].reshape(4, 128).T)
    wout_sl = np.ascontiguousarray(np.asarray(inputs["L0_attn_W_out"])[c * DOWN:(c + 1) * DOWN])
    wg_sl = np.ascontiguousarray(np.asarray(inputs["L0_ffn_w_gate"])[c * 1024:(c + 1) * 1024])
    wu_sl = np.ascontiguousarray(np.asarray(inputs["L0_ffn_w_up"])[c * 1024:(c + 1) * 1024])
    wd_sl = np.ascontiguousarray(np.asarray(inputs["L0_ffn_w_down"])[c * 256:(c + 1) * 256])
    return dict(
        x_own=x_own.astype(np.float32),
        ident=np.eye(128, dtype=np.float32),
        rmsw=np.asarray(inputs["L0_attn_rms_w"]).reshape(1, DM).astype(np.float32),
        w_in_sl=w_in_sl.astype(np.float32), convw=convw.astype(np.float32),
        convb=convb.astype(np.float32), wx_sl=wx_sl.astype(np.float32),
        wdt_sl=wdt_sl.astype(np.float32), dtb=dtb.astype(np.float32),
        alog=alog.astype(np.float32), dp=dp.astype(np.float32),
        wout_sl=wout_sl.astype(np.float32), wg_sl=wg_sl.astype(np.float32),
        wu_sl=wu_sl.astype(np.float32), wd_sl=wd_sl.astype(np.float32),
    )


def kernel(**inputs):
    global _LAST_TAPS
    nc = build(taps=_DEBUG_TAPS)
    in_maps = [_slice_inputs(inputs, c) for c in range(NC)]
    res = run_bass_kernel_spmd(nc, in_maps, core_ids=list(range(NC)))
    _LAST_TAPS = {}
    for tname, _, _ in _DEBUG_TAPS:
        _LAST_TAPS[tname] = [r["tap_" + tname] for r in res.results]
    out = np.zeros((NT, DM), dtype=np.float32)
    for c in range(NC):
        o = res.results[c]["out_own"]
        for k in range(NCHUNK):
            out[(k * NC + c) * STR:(k * NC + c) * STR + STR] = o[k * STR:(k + 1) * STR]
    return out.reshape(B, L, DM)


# version 11 (final)
# speedup vs baseline: 1.1203x; 1.0085x over previous
import contextlib
import numpy as np
import concourse.bass as bass
import concourse.mybir as mybir
from concourse.bass_utils import run_bass_kernel_spmd

F32 = mybir.dt.float32
BF16 = mybir.dt.bfloat16
U8 = mybir.dt.uint8
I32 = mybir.dt.int32
FP8 = mybir.dt.float8e4
AF = mybir.ActivationFunctionType
ALU = mybir.AluOpType

NC = 8
B, L, DM = 2, 8192, 2048
NT = B * L
DI, DS, DR, KC = 4096, 16, 128, 4
IF = 8192
DOWN = DI // NC
TOK = NT // NC
STR = 256
NCHUNK = 8
CT = 2048
TT = 256
NTAU = CT // TT
EPS = 1e-5

S_8TH = 1.5 * 2.0 ** 20
S_HALF = 1.5 * 2.0 ** 22
S_ONE = 1.5 * 2.0 ** 23
S_TWO = 1.5 * 2.0 ** 24

_DEBUG_TAPS = ()
_LAST_TAPS = {}


class Sem:
    def __init__(self, nc, es, name):
        self.h = es.enter_context(nc.semaphore(name))
        self.n = 0

    def inc(self, instr, amt=1):
        instr.then_inc(self.h, amt)
        self.n += amt
        return self.n


def build(taps=()):
    nc = bass.Bass("TRN2", target_bir_lowering=False,
                   detect_race_conditions=False)
    es = contextlib.ExitStack()

    def din(name, shape, dt=F32):
        return nc.dram_tensor(name, shape, dt, kind="ExternalInput").ap()

    def dout(name, shape, dt=F32):
        return nc.dram_tensor(name, shape, dt, kind="ExternalOutput").ap()

    def dram(name, shape, dt=F32):
        return nc.dram_tensor(name, shape, dt)

    x_in = din("x_own", [TOK, DM])
    ident_in = din("ident", [128, 128])
    rmsw_in = din("rmsw", [1, DM])
    w_in_sl = din("w_in_sl", [DM, 1024])
    convw_in = din("convw", [128, 16])
    convb_in = din("convb", [128, 4])
    wx_in = din("wx_sl", [DOWN, 160])
    wdt_in = din("wdt_sl", [DR, DOWN])
    dtb_in = din("dtb", [128, 4])
    alog_in = din("alog", [128, 64])
    dp_in = din("dp", [128, 4])
    wout_in = din("wout_sl", [DOWN, DM])
    wg_in = din("wg_sl", [IF // NC, DM])
    wu_in = din("wu_sl", [IF // NC, DM])
    wd_in = din("wd_sl", [DM // NC, IF])

    out_own = dout("out_own", [TOK, DM])

    hn_in_h = [dram("hn_in_a", [DM, TOK // 2], BF16),
               dram("hn_in_b", [DM, TOK // 2], BF16)]
    hn_all_h = [dram("hn_all_a", [NC * DM, TOK // 2], BF16),
                dram("hn_all_b", [NC * DM, TOK // 2], BF16)]
    ln1_spill = dram("ln1_spill", [TOK, DM], F32)
    x2_spill = dram("x2_spill", [TOK, DM], F32)
    ar_in = [dram(f"ar_in{k}", [160, CT], F32) for k in range(NCHUNK)]
    ar_out = [dram(f"ar_out{k}", [160, CT], F32) for k in range(NCHUNK)]
    rs_in = [dram(f"rs_in{k}", [CT, DM], F32) for k in range(NCHUNK)]
    rs_out = [dram(f"rs_out{k}", [STR, DM], F32) for k in range(NCHUNK)]
    wgq_in = dram("wgq_in", [DM, 1024], BF16)
    wgq_all = dram("wgq_all", [NC * DM, 1024], BF16)
    wuq_in = dram("wuq_in", [DM, 1024], BF16)
    wuq_all = dram("wuq_all", [NC * DM, 1024], BF16)
    wdq_in = dram("wdq_in", [IF, DM // NC], BF16)
    wdq_all = dram("wdq_all", [NC * IF, DM // NC], BF16)
    ikt_spill = dram("ikt_spill", [IF, TOK], BF16)
    dout_spill = dram("dout_spill", [TOK, DM], F32)
    uc_spill = dram("uc_spill", [NCHUNK * 512, CT], BF16)
    sz_spill = dram("sz_spill", [NCHUNK * 512, CT], BF16)

    tap_bufs = {}
    for tname, tshape, tdt in taps:
        tap_bufs[tname] = dout("tap_" + tname, list(tshape), tdt)

    L_ = Sem(nc, es, "Lq")
    ST = Sem(nc, es, "STq")
    V = Sem(nc, es, "Vm")
    SC = Sem(nc, es, "SCm")
    PE = Sem(nc, es, "PEm")
    G = Sem(nc, es, "Gm")
    c_aghn = [Sem(nc, es, "c_aghn_a"), Sem(nc, es, "c_aghn_b")]
    c_agwg = Sem(nc, es, "c_agwg")
    c_agwu = Sem(nc, es, "c_agwu")
    c_agwd = Sem(nc, es, "c_agwd")
    c_ar = [Sem(nc, es, f"c_ar{k}") for k in range(NCHUNK)]
    c_rs = [Sem(nc, es, f"c_rs{k}") for k in range(NCHUNK)]

    def sbuf(name, shape, dt, stack=None):
        return (stack or es).enter_context(nc.sbuf_tensor(name, shape, dt))

    def psum(name, shape, dt=F32):
        return es.enter_context(nc.psum_tensor(name, shape, dt))

    w_x_bf = sbuf("w_x_bf", [128, 4 * 160], BF16)
    w_dt_bf = sbuf("w_dt_bf", [128, DOWN], BF16)
    A_sb = sbuf("A_sb", [128, 64], F32)
    negA_sb = sbuf("negA_sb", [128, 64], F32)
    convw_sb = sbuf("convw_sb", [128, 16], F32)
    convb_sb = sbuf("convb_sb", [128, 4], F32)
    dtb_sb = sbuf("dtb_sb", [128, 4], F32)
    dp_sb = sbuf("dp_sb", [128, 4], F32)
    onesbf = sbuf("onesbf", [1, 128], BF16)
    ones32 = sbuf("ones32", [1, 128], F32)
    ident_bf = sbuf("ident_bf", [128, 128], BF16)
    junkA = sbuf("junkA", [128, 8], F32)
    junkB = sbuf("junkB", [128, 8], F32)
    consts = sbuf("consts", [128, 4], F32)

    pA = [psum("pA0", [128, 512]), psum("pA1", [128, 512])]
    pB = [psum("pB0", [128, 512]), psum("pB1", [128, 512])]
    pT = [psum("pT0", [128, 128], BF16), psum("pT1", [128, 128], BF16)]
    pTB = psum("pTB", [128, 1024], BF16)
    pR = psum("pR", [128, 512])

    block = es.enter_context(nc.Block())
    SY, VE, SE, TE, GE = nc.sync, nc.vector, nc.scalar, nc.tensor, nc.gpsimd

    def spacer():
        return VE.tensor_copy(junkA[:, :], junkB[:, :])

    def barrier():
        vals = [(V.h, V.n), (SC.h, SC.n), (PE.h, PE.n), (G.h, G.n),
                (L_.h, L_.n), (ST.h, ST.n)]
        for e in (SY, VE, SE, TE, GE):
            for h, n in vals:
                if n > 0:
                    e.wait_ge(h, n)

    def ln_stats(src, bns, mv):
        for j in range(4):
            VE.bn_stats(bns[:, 6 * j:6 * j + 6], src[:, 512 * j:512 * (j + 1)])
        spacer()
        VE.bn_aggr(mv[:, 0:2], bns[:, 0:24])
        spacer()
        return V.inc(spacer())

    def ln_coefs(mv, cf, vwait):
        SE.wait_ge(V.h, vwait)
        SE.activation(cf[:, 2:3], mv[:, 1:2], AF.Ln, bias=float(EPS))
        sc = SC.inc(SE.activation(cf[:, 0:1], cf[:, 2:3], AF.Exp, scale=-0.5))
        VE.wait_ge(SC.h, sc)
        VE.scalar_tensor_tensor(cf[:, 1:2], mv[:, 0:1], -1.0, cf[:, 0:1],
                                ALU.mult, ALU.mult)
        spacer()
        return V.inc(spacer())

    def _block_scales(s, src_am_ap, nb):
        VE.tensor_scalar(s["ie"][:, 0:nb], src_am_ap.bitcast(I32), 23, None,
                         ALU.logical_shift_right, ALU.bypass)
        spacer()
        VE.tensor_scalar(s["ie"][:, 0:nb], s["ie"][:, 0:nb], 129, None,
                         ALU.subtract, ALU.bypass)
        spacer()
        VE.tensor_scalar(s["ie"][:, 0:nb], s["ie"][:, 0:nb], -127, 127,
                         ALU.max, ALU.min)
        spacer()
        VE.tensor_scalar(s["ir"][:, 0:nb], s["ie"][:, 0:nb], -1, 127,
                         ALU.mult, ALU.add)
        VE.tensor_scalar(s["is"][:, 0:nb], s["ie"][:, 0:nb], 127, None,
                         ALU.add, ALU.bypass)
        spacer()
        VE.tensor_scalar(s["ir"][:, 0:nb], s["ir"][:, 0:nb], 23, None,
                         ALU.logical_shift_left, ALU.bypass)
        VE.tensor_scalar(s["is"][:, 0:nb], s["is"][:, 0:nb], 23, None,
                         ALU.logical_shift_left, ALU.bypass)
        spacer()

    def quant_w(wt, qbf, s, Fw):
        nb = Fw // 32
        VE.tensor_reduce(s["am"][:, 0:nb],
                         wt[:, 0:Fw].rearrange("p (b e) -> p b e", e=32),
                         axis=mybir.AxisListType.X, op=ALU.max,
                         apply_absolute_value=True)
        spacer()
        _block_scales(s, s["am"][:, 0:nb], nb)
        VE.tensor_tensor(s["v"][:, 0:Fw].rearrange("p (b e) -> p b e", e=32),
                         wt[:, 0:Fw].rearrange("p (b e) -> p b e", e=32),
                         s["ir"][:, 0:nb].bitcast(F32).unsqueeze(2)
                         .broadcast_to([128, nb, 32]), ALU.mult)
        VE.tensor_scalar(s["q"][:, 0:Fw].bitcast(I32), s["v"][:, 0:Fw].bitcast(I32),
                         0x7FFFFFFF, None, ALU.bitwise_and, ALU.bypass)
        VE.tensor_scalar(s["r05"][:, 0:Fw], s["v"][:, 0:Fw], S_HALF, S_HALF,
                         ALU.add, ALU.subtract)
        VE.tensor_scalar(s["r1"][:, 0:Fw], s["v"][:, 0:Fw], S_ONE, S_ONE,
                         ALU.add, ALU.subtract)
        VE.tensor_scalar(s["r2c"][:, 0:Fw], s["v"][:, 0:Fw], S_TWO, S_TWO,
                         ALU.add, ALU.subtract)
        VE.tensor_scalar(s["m2"][:, 0:Fw], s["q"][:, 0:Fw].bitcast(I32), 0x40000000,
                         None, ALU.is_lt, ALU.bypass)
        VE.tensor_scalar(s["m4"][:, 0:Fw], s["q"][:, 0:Fw].bitcast(I32), 0x40800000,
                         None, ALU.is_lt, ALU.bypass)
        VE.tensor_copy(s["q"][:, 0:Fw], s["r2c"][:, 0:Fw])
        VE.copy_predicated(s["q"][:, 0:Fw], s["m4"][:, 0:Fw], s["r1"][:, 0:Fw])
        VE.copy_predicated(s["q"][:, 0:Fw], s["m2"][:, 0:Fw], s["r05"][:, 0:Fw])
        VE.tensor_scalar(s["q"][:, 0:Fw], s["q"][:, 0:Fw], 6.0, -6.0, ALU.min, ALU.max)
        VE.tensor_tensor(qbf[:, 0:Fw].rearrange("p (b e) -> p b e", e=32),
                         s["q"][:, 0:Fw].rearrange("p (b e) -> p b e", e=32),
                         s["is"][:, 0:nb].bitcast(F32).unsqueeze(2)
                         .broadcast_to([128, nb, 32]), ALU.mult)
        return V.inc(spacer())

    def quant_a(at, qbf_ap, s, Fw):
        nb = Fw // 32
        VE.tensor_reduce(s["am"][:, 0:nb],
                         at[:, 0:Fw].rearrange("p (b e) -> p b e", e=32),
                         axis=mybir.AxisListType.X, op=ALU.max,
                         apply_absolute_value=True)
        spacer()
        _block_scales(s, s["am"][:, 0:nb], nb)
        VE.tensor_tensor(s["v"][:, 0:Fw].rearrange("p (b e) -> p b e", e=32),
                         at[:, 0:Fw].rearrange("p (b e) -> p b e", e=32),
                         s["ir"][:, 0:nb].bitcast(F32).unsqueeze(2)
                         .broadcast_to([128, nb, 32]), ALU.mult)
        VE.tensor_copy(s["f8"][:, 0:Fw], s["v"][:, 0:Fw])
        VE.tensor_scalar(s["q"][:, 0:Fw].bitcast(I32), s["v"][:, 0:Fw].bitcast(I32),
                         0x7FFFFFFF, None, ALU.bitwise_and, ALU.bypass)
        VE.tensor_scalar(s["r1"][:, 0:Fw], s["v"][:, 0:Fw], S_8TH, S_8TH,
                         ALU.add, ALU.subtract)
        VE.tensor_scalar(s["m2"][:, 0:Fw], s["q"][:, 0:Fw].bitcast(I32), 0x3F800000,
                         None, ALU.is_lt, ALU.bypass)
        VE.tensor_copy(s["q"][:, 0:Fw], s["f8"][:, 0:Fw])
        VE.copy_predicated(s["q"][:, 0:Fw], s["m2"][:, 0:Fw], s["r1"][:, 0:Fw])
        VE.tensor_scalar(s["q"][:, 0:Fw], s["q"][:, 0:Fw], 7.5, -7.5, ALU.min, ALU.max)
        VE.tensor_tensor(qbf_ap.rearrange("p (b e) -> p b e", e=32),
                         s["q"][:, 0:Fw].rearrange("p (b e) -> p b e", e=32),
                         s["is"][:, 0:nb].bitcast(F32).unsqueeze(2)
                         .broadcast_to([128, nb, 32]), ALU.mult)
        return V.inc(spacer())


    def transpose8(src_ap_fn, n_tiles, evac_fn):
        for b0 in range(0, n_tiles, 8):
            nb_ = min(8, n_tiles - b0)
            TE.wait_ge(SC.h, SC.n)
            TE.wait_ge(V.h, V.n)
            for j in range(nb_):
                mmt = TE.transpose(pTB[:, 128 * j:128 * (j + 1)], src_ap_fn(b0 + j),
                                   ident_bf[:, :])
            pe = PE.inc(mmt)
            SE.wait_ge(PE.h, pe)
            evac_fn(b0 // 8, pTB[:, 0:128 * nb_])

    def tapw(name, src_ap):
        if name in tap_bufs:
            SE.wait_ge(V.h, V.n)
            SE.wait_ge(SC.h, SC.n)
            return ST.inc(SE.dma_start(out=tap_bufs[name], in_=src_ap), 16)
        return None

    pwo = contextlib.ExitStack()
    pwi = contextlib.ExitStack()
    w_out_bf = sbuf("w_out_bf", [128, 4 * DM], BF16, pwo)
    w_in_bf = sbuf("w_in_bf", [128, 16 * 1024], BF16, pwi)

    p0 = contextlib.ExitStack()
    ident_f = sbuf("ident_f", [128, 128], F32, p0)
    rmsw_row = sbuf("rmsw_row", [1, DM], F32, p0)
    rmsw_bc = sbuf("rmsw_bc", [128, DM], F32, p0)

    nL = L_.inc(SY.dma_start(out=ident_f[:, :], in_=ident_in), 16)
    nL = L_.inc(SY.dma_start(out=convw_sb[:, :], in_=convw_in), 16)
    nL = L_.inc(SY.dma_start(out=convb_sb[:, :], in_=convb_in), 16)
    nL = L_.inc(SY.dma_start(out=dtb_sb[:, :], in_=dtb_in), 16)
    nL = L_.inc(SY.dma_start(out=dp_sb[:, :], in_=dp_in), 16)
    nL = L_.inc(SY.dma_start(out=rmsw_row[:, :], in_=rmsw_in), 16)
    nc.const_aps.register(F32, float(EPS), consts[:, 0:1])
    nc.const_aps.register(F32, 1.0, consts[:, 1:2])
    VE.wait_ge(L_.h, nL)
    VE.memset(consts[:, 0:1], float(EPS))
    VE.memset(consts[:, 1:2], 1.0)
    VE.memset(onesbf[:, :], 1.0)
    VE.memset(ones32[:, :], 1.0)
    VE.memset(junkB[:, :], 0.0)
    vE = V.inc(VE.tensor_copy(ident_bf[:, :], ident_f[:, :]))
    SY.wait_ge(V.h, vE)
    nLA = L_.inc(SY.dma_start(out=ident_f[:, 0:64], in_=alog_in), 16)
    SE.wait_ge(L_.h, nLA)
    SC.inc(SE.activation(A_sb[:, :], ident_f[:, 0:64], AF.Exp))
    VE.wait_ge(SC.h, SC.n)
    V.inc(VE.tensor_scalar(negA_sb[:, :], A_sb[:, :], -1.0, None, ALU.mult, ALU.bypass))

    TE.wait_ge(V.h, vE)
    for j in range(4):
        pe = PE.inc(TE.matmul(pR[:, :], ones32[:, :], rmsw_row[:, 512 * j:512 * (j + 1)],
                              start=True, stop=True))
        SE.wait_ge(PE.h, pe)
        scR = SC.inc(SE.copy(rmsw_bc[:, 512 * j:512 * (j + 1)], pR[:, :]))
        TE.wait_ge(SC.h, scR)

    xt = sbuf("xt", [128, DM], F32, p0)
    ln1t = sbuf("ln1t", [128, DM], F32, p0)
    hnt = sbuf("hnt", [128, DM], BF16, p0)
    hnev = sbuf("hnev", [128, 1024], BF16, p0)
    bns = sbuf("bns", [128, 24], F32, p0)
    mv = sbuf("mv", [128, 4], F32, p0)
    cf = sbuf("cf", [128, 4], F32, p0)

    st_hn = 0
    for i in range(16):
        SY.wait_ge(V.h, V.n)
        SY.wait_ge(SC.h, SC.n)
        nL = L_.inc(SY.dma_start(out=xt[:, :],
                                 in_=x_in.rearrange("(i p) f -> i p f", p=128)[i]), 16)
        VE.wait_ge(L_.h, nL)
        vw = ln_stats(xt, bns, mv)
        vw = ln_coefs(mv, cf, vw)
        VE.wait_ge(ST.h, ST.n)
        VE.tensor_scalar(ln1t[:, :], xt[:, :], cf[:, 0:1], cf[:, 1:2],
                         ALU.mult, ALU.add)
        spacer()
        vc1 = V.inc(spacer())
        vw2 = ln_stats(ln1t, bns, mv)
        SE.wait_ge(V.h, vw2)
        SE.activation(cf[:, 2:3], mv[:, 1:2], AF.Ln, bias=float(EPS))
        scg = SC.inc(SE.activation(cf[:, 2:3], cf[:, 2:3], AF.Exp, scale=-0.5))
        VE.wait_ge(SC.h, scg)
        VE.wait_ge(PE.h, PE.n)
        VE.scalar_tensor_tensor(hnt[:, :], ln1t[:, :], cf[:, 2:3],
                                rmsw_bc[:, :], ALU.mult, ALU.mult)
        spacer()
        vh = V.inc(spacer())
        SE.wait_ge(V.h, vc1)
        st_l = ST.inc(SE.dma_start(out=ln1_spill.rearrange("(i p) f -> i p f", p=128)[i],
                                   in_=ln1t[:, :]), 16)
        if i == 0:
            tapw("ln1", ln1t[:, :])
            tapw("hnt", hnt[:, :])
        TE.wait_ge(V.h, vh)
        for b in range(2):
            TE.wait_ge(SC.h, SC.n)
            for j in range(8):
                mmt = TE.transpose(pTB[:, 128 * j:128 * (j + 1)],
                                   hnt[:, 128 * (8 * b + j):128 * (8 * b + j + 1)],
                                   ident_bf[:, :])
            pe = PE.inc(mmt)
            SE.wait_ge(PE.h, pe)
            SE.wait_ge(ST.h, st_hn)
            SC.inc(SE.copy(hnev[:, :], pTB[:, :]))
            SE.copy(junkB[:, 0:4], junkA[:, 0:4])
            st_hn = ST.inc(SE.dma_start(
                out=hn_in_h[i // 8].rearrange("(a p) t -> a p t", p=128)[
                    8 * b:8 * b + 8][
                    :, :, 128 * (i % 8):128 * (i % 8 + 1)].transpose([1, 0, 2]),
                in_=hnev[:, :]), 16)
        if i == 7:
            GE.wait_ge(ST.h, st_hn)
            c_aghn[0].inc(GE.collective_compute(
                "AllGather", ALU.bypass, replica_groups=[list(range(NC))],
                ins=[hn_in_h[0].ap().opt()], outs=[hn_all_h[0].ap().opt()]))

    GE.wait_ge(ST.h, st_hn)
    c_aghn[1].inc(GE.collective_compute(
        "AllGather", ALU.bypass, replica_groups=[list(range(NC))],
        ins=[hn_in_h[1].ap().opt()], outs=[hn_all_h[1].ap().opt()]))

    wstage = sbuf("wstage", [128, DM], F32, p0)

    def load_cast(dst_ap, src_ap, width):
        SY.wait_ge(SC.h, SC.n)
        nL = L_.inc(SY.dma_start(out=wstage[:, 0:width], in_=src_ap), 16)
        SE.wait_ge(L_.h, nL)
        return SC.inc(SE.copy(dst_ap, wstage[:, 0:width]))

    for kk in range(16):
        load_cast(w_in_bf[:, 1024 * kk:1024 * (kk + 1)],
                  w_in_sl.rearrange("(a p) c -> a p c", p=128)[kk], 1024)
    for kd in range(4):
        load_cast(w_out_bf[:, DM * kd:DM * (kd + 1)],
                  wout_in.rearrange("(a p) c -> a p c", p=128)[kd], DM)
    for kd in range(4):
        load_cast(w_x_bf[:, 160 * kd:160 * (kd + 1)],
                  wx_in.rearrange("(a p) c -> a p c", p=128)[kd], 160)
    load_cast(w_dt_bf[:, :], wdt_in, DOWN)

    barrier()
    p0.close()

    pm1 = contextlib.ExitStack()
    rhs_sb = sbuf("rhs_sb", [128, 16 * 512], BF16, pm1)
    u_ext = [sbuf(f"u_ext{m}", [128, 515], BF16, pm1) for m in range(4)]
    halo = [sbuf(f"halo{m}", [128, 3], BF16, pm1) for m in range(4)]
    cacc = [sbuf("cacc0", [128, 512], F32, pm1), sbuf("cacc1", [128, 512], F32, pm1)]
    u_c = [sbuf(f"u_c{m}", [128, CT], BF16, pm1) for m in range(4)]
    siluz = [sbuf(f"siluz{m}", [128, CT], BF16, pm1) for m in range(4)]
    dtev = sbuf("dtev", [128, 512], F32, pm1)

    st_spill = 0
    sc_bank = [SC.n, SC.n]
    for k in range(NCHUNK):
        if k % 4 == 0:
            vhz = V.n
            for m in range(4):
                vhz = V.inc(VE.memset(halo[m][:, :], 0.0))
        for n in range(4):
            SY.wait_ge(PE.h, PE.n)
            SY.wait_ge(c_aghn[k // 4].h, 1)
            nL = L_.n
            for kk in range(16):
                for half in range(2):
                    r = 2 * n + half
                    nL = L_.inc(SY.dma_start(
                        out=rhs_sb[:, 512 * kk + 256 * half:512 * kk + 256 * (half + 1)],
                        in_=hn_all_h[k // 4].rearrange(
                            "(r a p) t -> r a p t", r=NC, p=128)[r][kk][
                            :, STR * (k % 4):STR * (k % 4 + 1)]), 16)
            TE.wait_ge(L_.h, nL)
            for m in range(8):
                pa = pA[m % 2]
                TE.wait_ge(SC.h, sc_bank[m % 2])
                for kk in range(16):
                    mm = TE.matmul(
                        pa[:, :],
                        w_in_bf[:, 1024 * kk + 128 * m:1024 * kk + 128 * (m + 1)],
                        rhs_sb[:, 512 * kk:512 * (kk + 1)],
                        start=(kk == 0), stop=(kk == 15))
                pe = PE.inc(mm)
                SE.wait_ge(PE.h, pe)
                if m < 4:
                    SE.wait_ge(V.h, V.n)
                    if n == 0:
                        SE.wait_ge(ST.h, st_spill)
                    sc1 = SC.inc(SE.copy(u_ext[m][:, 3:515], pa[:, :]))
                    sc_bank[m % 2] = sc1
                    VE.wait_ge(SC.h, sc1)
                    VE.tensor_copy(u_ext[m][:, 0:3], halo[m][:, :])
                    VE.tensor_scalar(cacc[0][:, :], u_ext[m][:, 0:512],
                                     convw_sb[:, 4 * m:4 * m + 1], None,
                                     ALU.mult, ALU.bypass)
                    for j in range(1, 4):
                        srcb, dstb = (cacc[0], cacc[1]) if j % 2 == 1 else (cacc[1], cacc[0])
                        VE.scalar_tensor_tensor(dstb[:, :], u_ext[m][:, j:j + 512],
                                                convw_sb[:, 4 * m + j:4 * m + j + 1],
                                                srcb[:, :], ALU.mult, ALU.add)
                    vcv = V.inc(VE.tensor_copy(halo[m][:, :], u_ext[m][:, 512:515]))
                    SE.wait_ge(V.h, vcv)
                    SC.inc(SE.activation(u_c[m][:, 512 * n:512 * (n + 1)], cacc[1][:, :],
                                         AF.Silu, bias=convb_sb[:, m:m + 1]))
                else:
                    if n == 0:
                        SE.wait_ge(ST.h, st_spill)
                    sc_bank[m % 2] = SC.inc(SE.activation(
                        siluz[m - 4][:, 512 * n:512 * (n + 1)], pa[:, :], AF.Silu))
            TE.wait_ge(SC.h, SC.n)
            for kd in range(4):
                TE.matmul(pB[0][:, :], w_x_bf[:, 160 * kd:160 * kd + 128],
                          u_c[kd][:, 512 * n:512 * (n + 1)],
                          start=(kd == 0), stop=(kd == 3))
                mm2 = TE.matmul(pB[1][0:32, :], w_x_bf[:, 160 * kd + 128:160 * (kd + 1)],
                                u_c[kd][:, 512 * n:512 * (n + 1)],
                                start=(kd == 0), stop=(kd == 3))
            pe = PE.inc(mm2)
            SE.wait_ge(PE.h, pe)
            SC.inc(SE.copy(dtev[:, :], pB[0][:, :]))
            st_dt = ST.inc(SE.dma_start(out=ar_in[k][0:128, 512 * n:512 * (n + 1)],
                                        in_=dtev[:, :]), 16)
            SE.wait_ge(ST.h, st_dt)
            SC.inc(SE.copy(dtev[0:32, :], pB[1][0:32, :]))
            st1 = ST.inc(SE.dma_start(out=ar_in[k][128:160, 512 * n:512 * (n + 1)],
                                      in_=dtev[0:32, :]), 16)
            SE.wait_ge(ST.h, st1)
        if k == 0:
            tapw("uc0", u_c[0][:, :])
            tapw("sz0", siluz[0][:, :])
        SE.wait_ge(SC.h, SC.n)
        for m in range(4):
            ST.inc(SE.dma_start(
                out=uc_spill.rearrange("(k m p) t -> k m p t", k=NCHUNK, p=128)[k][m],
                in_=u_c[m][:, :]), 16)
            st_spill = ST.inc(SE.dma_start(
                out=sz_spill.rearrange("(k m p) t -> k m p t", k=NCHUNK, p=128)[k][m],
                in_=siluz[m][:, :]), 16)
        GE.wait_ge(ST.h, st_spill)
        c_ar[k].inc(GE.collective_compute(
            "AllReduce", ALU.add, replica_groups=[list(range(NC))],
            ins=[ar_in[k].ap().opt()], outs=[ar_out[k].ap().opt()]))

    pq = contextlib.ExitStack()
    wstage2 = sbuf("wstage2", [128, DM], F32, pq)
    qs = {
        "v": sbuf("q_v", [128, DM], F32, pq),
        "r05": sbuf("q_r05", [128, DM], F32, pq),
        "r1": sbuf("q_r1", [128, DM], F32, pq),
        "r2c": sbuf("q_r2c", [128, DM], F32, pq),
        "q": sbuf("q_q", [128, DM], F32, pq),
        "f8": sbuf("q_f8", [128, DM], FP8, pq),
        "m2": sbuf("q_m2", [128, DM], U8, pq),
        "m4": sbuf("q_m4", [128, DM], U8, pq),
        "am": sbuf("q_am", [128, 64], F32, pq),
        "ie": sbuf("q_ie", [128, 64], I32, pq),
        "ir": sbuf("q_ir", [128, 64], I32, pq),
        "is": sbuf("q_is", [128, 64], I32, pq),
    }
    qbf = sbuf("qbf", [128, DM], BF16, pq)
    qev = sbuf("qev", [128, 1024], BF16, pq)

    def q_t_store(src_tile_ap, n_kf, dst, col0):
        last = ST.n
        for b in range(n_kf // 8):
            TE.wait_ge(SC.h, SC.n)
            for j in range(8):
                kf = 8 * b + j
                mmt = TE.transpose(pTB[:, 128 * j:128 * (j + 1)],
                                   qbf[:, 128 * kf:128 * (kf + 1)], ident_bf[:, :])
            pe = PE.inc(mmt)
            SE.wait_ge(PE.h, pe)
            SE.wait_ge(ST.h, last)
            SC.inc(SE.copy(qev[:, :], pTB[:, :]))
            SE.copy(junkB[:, 0:4], junkA[:, 0:4])
            last = ST.inc(SE.dma_start(
                out=dst.rearrange("(a p) c -> a p c", p=128)[8 * b:8 * b + 8][
                    :, :, col0:col0 + 128].transpose([1, 0, 2]),
                in_=qev[:, :]), 16)
        return last

    for wt_dram, dst in ((wg_in, wgq_in), (wu_in, wuq_in)):
        for it in range(8):
            SY.wait_ge(V.h, V.n)
            nL = L_.inc(SY.dma_start(
                out=wstage2[:, :], in_=wt_dram.rearrange("(a p) c -> a p c", p=128)[it]), 16)
            VE.wait_ge(L_.h, nL)
            VE.wait_ge(PE.h, PE.n)
            vqd = quant_w(wstage2, qbf, qs, DM)
            TE.wait_ge(V.h, vqd)
            stq = q_t_store(0, 16, dst, 128 * it)
        GE.wait_ge(ST.h, stq)
        sem = c_agwg if wt_dram is wg_in else c_agwu
        sem.inc(GE.collective_compute(
            "AllGather", ALU.bypass, replica_groups=[list(range(NC))],
            ins=[(wgq_in if wt_dram is wg_in else wuq_in).ap().opt()],
            outs=[(wgq_all if wt_dram is wg_in else wuq_all).ap().opt()]))

    for it in range(2):
        for sub in range(4):
            SY.wait_ge(V.h, V.n)
            nL = L_.inc(SY.dma_start(
                out=wstage2[:, :],
                in_=wd_in.rearrange("(a p) (s c) -> a s p c", p=128, c=DM)[it][sub]), 16)
            VE.wait_ge(L_.h, nL)
            VE.wait_ge(PE.h, PE.n)
            vqd = quant_w(wstage2, qbf, qs, DM)
            TE.wait_ge(V.h, vqd)
            last = ST.n
            for b in range(2):
                TE.wait_ge(SC.h, SC.n)
                for j in range(8):
                    kf = 8 * b + j
                    mmt = TE.transpose(pTB[:, 128 * j:128 * (j + 1)],
                                       qbf[:, 128 * kf:128 * (kf + 1)], ident_bf[:, :])
                pe = PE.inc(mmt)
                SE.wait_ge(PE.h, pe)
                SE.wait_ge(ST.h, last)
                SC.inc(SE.copy(qev[:, :], pTB[:, :]))
                SE.copy(junkB[:, 0:4], junkA[:, 0:4])
                last = ST.inc(SE.dma_start(
                    out=wdq_in.rearrange("(a p) c -> a p c", p=128)[
                        sub * 16 + 8 * b:sub * 16 + 8 * b + 8][
                        :, :, 128 * it:128 * (it + 1)].transpose([1, 0, 2]),
                    in_=qev[:, :]), 16)
    GE.wait_ge(ST.h, last)
    c_agwd.inc(GE.collective_compute(
        "AllGather", ALU.bypass, replica_groups=[list(range(NC))],
        ins=[wdq_in.ap().opt()], outs=[wdq_all.ap().opt()]))

    pq.close()

    barrier()
    pm1.close()
    pwi.close()

    pm3 = contextlib.ExitStack()
    HC = CT // 2
    u_c2b = [[sbuf(f"u_c2{p}_{m}", [128, HC], BF16, pm3) for m in range(4)]
             for p in range(2)]
    siluz2b = [[sbuf(f"sz2{p}_{m}", [128, HC], BF16, pm3) for m in range(4)]
               for p in range(2)]
    dtevB = sbuf("dtevB", [128, 512], F32, pm3)
    dt_bf = sbuf("dt_bf", [128, CT], BF16, pm3)
    bc_sb = sbuf("bc_sb", [32, CT], F32, pm3)
    bc_bf = sbuf("bc_bf", [32, CT], BF16, pm3)
    Bbc = [sbuf("BbcA", [128, 16 * TT], BF16, pm3),
           sbuf("BbcB", [128, 16 * TT], BF16, pm3)]
    Cbc = [sbuf("CbcA", [128, 16 * TT], BF16, pm3),
           sbuf("CbcB", [128, 16 * TT], BF16, pm3)]
    delta_sb = [sbuf(f"delta{m}", [128, HC], F32, pm3) for m in range(4)]
    sp_a = sbuf("sp_a", [128, 512], F32, pm3)
    sp_b = sbuf("sp_b", [128, 512], F32, pm3)
    sp_c = sbuf("sp_c", [128, 512], F32, pm3)
    dA2 = [sbuf("dAA", [128, 16 * TT], F32, pm3), sbuf("dAB", [128, 16 * TT], F32, pm3)]
    x2b = [sbuf("x2bA", [128, 16 * TT], BF16, pm3), sbuf("x2bB", [128, 16 * TT], BF16, pm3)]
    du_sb = sbuf("du_sb", [128, TT], F32, pm3)
    h_sb = sbuf("h_sb", [128, 16 * TT], BF16, pm3)
    y23b = [[sbuf(f"y23_{p}_{m}", [128, TT], BF16, pm3) for m in range(4)]
            for p in range(2)]
    carry = sbuf("carry", [128, 64], F32, pm3)
    oo_sb = sbuf("oo_sb", [128, 1024], F32, pm3)
    bc_dram = [dram(f"bc_dram{kq}", [32, CT], BF16) for kq in range(NCHUNK)]

    def uc_load(H):
        kq, hq = H // 2, H % 2
        SY.wait_ge(V.h, V.n)
        SY.wait_ge(G.h, G.n)
        n = L_.n
        for m in range(4):
            n = L_.inc(SY.dma_start(
                out=u_c2b[H % 2][m][:, :],
                in_=uc_spill.rearrange("(k m p) t -> k m p t", k=NCHUNK, p=128)[kq][m][
                    :, HC * hq:HC * (hq + 1)]), 16)
            n = L_.inc(SY.dma_start(
                out=siluz2b[H % 2][m][:, :],
                in_=sz_spill.rearrange("(k m p) t -> k m p t", k=NCHUNK, p=128)[kq][m][
                    :, HC * hq:HC * (hq + 1)]), 16)
        return n

    uc_ready = [0, 0]
    uc_ready[0] = uc_load(0)
    def bc_bcast_load(kq, tauq, bparq):
        SY.wait_ge(G.h, bc_guard[bparq][0])
        SY.wait_ge(V.h, bc_guard[bparq][1])
        SY.wait_ge(ST.h, st_bc_all[kq])
        t0q = TT * tauq
        n = L_.inc(SY.dma_start(
            out=Bbc[bparq][:, :].rearrange("p (s t) -> p s t", s=16),
            in_=bc_dram[kq].ap()[0:16, t0q:t0q + TT]
            .unsqueeze(0).broadcast_to([128, 16, TT])), 16)
        n = L_.inc(SY.dma_start(
            out=Cbc[bparq][:, :].rearrange("p (s t) -> p s t", s=16),
            in_=bc_dram[kq].ap()[16:32, t0q:t0q + TT]
            .unsqueeze(0).broadcast_to([128, 16, TT])), 16)
        return n

    st_bc_all = [0] * NCHUNK
    bc_guard = [[0, 0], [0, 0]]
    nB = [0] * NTAU
    sc_exp = [SC.n, SC.n]
    v_scans = [V.n, V.n]
    v_tree = [V.n, V.n]
    g_x = [G.n, G.n]
    for k in range(NCHUNK):
        if k % 4 == 0:
            VE.wait_ge(PE.h, PE.n)
            VE.memset(carry[:, :], 0.0)
            V.inc(spacer())
        SY.wait_ge(c_ar[k].h, 1)
        for hh in range(4):
            SY.wait_ge(SC.h, SC.n)
            nL = L_.inc(SY.dma_start(out=dtevB[:, :],
                                     in_=ar_out[k][0:128, 512 * hh:512 * (hh + 1)]), 16)
            SE.wait_ge(L_.h, nL)
            SE.wait_ge(PE.h, PE.n)
            SC.inc(SE.copy(dt_bf[:, 512 * hh:512 * (hh + 1)], dtevB[:, :]))
        SY.wait_ge(V.h, V.n)
        nL = L_.inc(SY.dma_start(out=bc_sb[:, :], in_=ar_out[k][128:160, :]), 16)
        VE.wait_ge(L_.h, nL)
        VE.tensor_copy(bc_bf[:, :], bc_sb[:, :])
        spacer()
        vbc = V.inc(spacer())
        SE.wait_ge(V.h, vbc)
        st_bc_all[k] = ST.inc(SE.dma_start(out=bc_dram[k].ap(), in_=bc_bf[:, :]), 16)
        if k > 0:
            nB[0] = bc_bcast_load(k, 0, 0)
        if k == 0:
            tapw("dtbf0", dt_bf[:, :])

        pe_wout = PE.n
        pe_wout2 = [PE.n, PE.n]
        for half in range(2):
            ht0 = HC * half
            VE.wait_ge(PE.h, pe_wout)
            VE.wait_ge(G.h, G.n)
            for m in range(4):
                for nn in range(2):
                    pa = pA[nn % 2]
                    TE.wait_ge(SC.h, SC.n)
                    pe = PE.inc(TE.matmul(pa[:, :], w_dt_bf[:, 128 * m:128 * (m + 1)],
                                          dt_bf[:, ht0 + 512 * nn:ht0 + 512 * (nn + 1)],
                                          start=True, stop=True))
                    SE.wait_ge(PE.h, pe)
                    SE.activation(sp_a[:, :], pa[:, :], AF.Relu, bias=dtb_sb[:, m:m + 1])
                    SE.activation(sp_b[:, :], pa[:, :], AF.Abs, bias=dtb_sb[:, m:m + 1])
                    SE.activation(sp_c[:, :], sp_b[:, :], AF.Exp, scale=-1.0)
                    scd = SC.inc(SE.activation(sp_b[:, :], sp_c[:, :], AF.Ln, bias=1.0))
                    VE.wait_ge(SC.h, scd)
                    VE.tensor_tensor(delta_sb[m][:, 512 * nn:512 * (nn + 1)],
                                     sp_a[:, :], sp_b[:, :], ALU.add)
            spacer()
            v_delta = V.inc(spacer())
            H = 2 * k + half
            u_c2 = u_c2b[H % 2]
            siluz2 = siluz2b[H % 2]
            if H + 1 < 2 * NCHUNK:
                uc_ready[(H + 1) % 2] = uc_load(H + 1)
            VE.wait_ge(L_.h, uc_ready[H % 2])
            GE.wait_ge(L_.h, uc_ready[H % 2])
            for tau4 in range(4):
                tau = half * 4 + tau4
                t0 = TT * tau
                tl0 = TT * tau4
                bpar = tau % 2
                if k == 0 and half == 0 and tau4 == 0:
                    nB[0] = bc_bcast_load(0, 0, 0)
                if tau + 1 < NTAU:
                    nB[tau + 1] = bc_bcast_load(k, tau + 1, (tau + 1) % 2)
                y23 = y23b[tau % 2]
                VE.wait_ge(PE.h, pe_wout2[tau % 2])
                for m in range(4):
                    par = m % 2
                    SE.wait_ge(V.h, v_delta)
                    SE.wait_ge(V.h, v_scans[par])
                    for s in range(16):
                        se_i = SE.activation(dA2[par][:, TT * s:TT * (s + 1)],
                                             delta_sb[m][:, tl0:tl0 + TT], AF.Exp,
                                             scale=negA_sb[:, 16 * m + s:16 * m + s + 1])
                    sc_exp[par] = SC.inc(se_i)
                    GE.wait_ge(V.h, v_delta)
                    GE.wait_ge(V.h, v_scans[par])
                    GE.wait_ge(V.h, v_tree[par])
                    GE.wait_ge(L_.h, nB[tau])
                    GE.tensor_tensor(du_sb[:, :], delta_sb[m][:, tl0:tl0 + TT],
                                     u_c2[m][:, tl0:tl0 + TT], ALU.mult)
                    gx = G.inc(GE.tensor_tensor(
                        x2b[par][:, :].rearrange("p (s t) -> p s t", s=16),
                        du_sb[:, :].unsqueeze(1).broadcast_to([128, 16, TT]),
                        Bbc[bpar][:, :].rearrange("p (s t) -> p s t", s=16), ALU.mult))
                    g_x[par] = gx
                    VE.wait_ge(SC.h, sc_exp[par])
                    VE.wait_ge(G.h, gx)
                    for s in range(16):
                        VE.tensor_tensor_scan(h_sb[:, TT * s:TT * (s + 1)],
                                              dA2[par][:, TT * s:TT * (s + 1)],
                                              x2b[par][:, TT * s:TT * (s + 1)],
                                              carry[:, 16 * m + s:16 * m + s + 1],
                                              ALU.mult, ALU.add)
                    spacer()
                    v_scans[par] = V.inc(spacer())
                    VE.tensor_tensor(x2b[par][:, :], h_sb[:, :], Cbc[bpar][:, :],
                                     ALU.mult)
                    dp_ = x2b[par]
                    VE.tensor_tensor(dp_[:, 0:8 * TT], dp_[:, 0:8 * TT],
                                     dp_[:, 8 * TT:16 * TT], ALU.add)
                    VE.tensor_tensor(dp_[:, 8 * TT:12 * TT], dp_[:, 0:4 * TT],
                                     dp_[:, 4 * TT:8 * TT], ALU.add)
                    VE.tensor_tensor(dp_[:, 12 * TT:14 * TT], dp_[:, 8 * TT:10 * TT],
                                     dp_[:, 10 * TT:12 * TT], ALU.add)
                    VE.tensor_tensor(dp_[:, 14 * TT:15 * TT], dp_[:, 12 * TT:13 * TT],
                                     dp_[:, 13 * TT:14 * TT], ALU.add)
                    VE.scalar_tensor_tensor(y23[m][:, :], u_c2[m][:, tl0:tl0 + TT],
                                            dp_sb[:, m:m + 1], dp_[:, 14 * TT:15 * TT],
                                            ALU.mult, ALU.add)
                    VE.tensor_tensor(y23[m][:, :], y23[m][:, :],
                                     siluz2[m][:, tl0:tl0 + TT], ALU.mult)
                    v_tree[par] = V.inc(VE.tensor_copy(
                        carry[:, 16 * m:16 * (m + 1)],
                        h_sb[:, :].rearrange("p (s t) -> p s t", s=16)[:, :, TT - 1]))
                bc_guard[bpar] = [G.n, V.n]
                spacer()
                v_y = V.inc(spacer())
                TE.wait_ge(V.h, v_y)
                for mh in range(2):
                    for fn in range(4):
                        pb = pB[fn % 2]
                        TE.wait_ge(SC.h, SC.n)
                        for kd in range(4):
                            mmo = TE.matmul(
                                pb[:, :], y23[kd][:, 128 * mh:128 * (mh + 1)],
                                w_out_bf[:, DM * kd + 512 * fn:DM * kd + 512 * (fn + 1)],
                                start=(kd == 0), stop=(kd == 3))
                        pe_wout = PE.inc(mmo)
                        pe_wout2[tau % 2] = pe_wout
                        SE.wait_ge(PE.h, pe_wout)
                        if fn % 2 == 0:
                            SE.wait_ge(ST.h, ST.n)
                        SC.inc(SE.copy(oo_sb[:, 512 * (fn % 2):512 * (fn % 2 + 1)],
                                       pb[:, :]))
                        ST.inc(SE.dma_start(
                            out=rs_in[k][t0 + 128 * mh:t0 + 128 * (mh + 1),
                                         512 * fn:512 * (fn + 1)],
                            in_=oo_sb[:, 512 * (fn % 2):512 * (fn % 2 + 1)]), 16)
        if k == 0:
            tapw("y230", y23[0][:, :])
        GE.wait_ge(ST.h, ST.n)
        c_rs[k].inc(GE.collective_compute(
            "ReduceScatter", ALU.add, replica_groups=[list(range(NC))],
            ins=[rs_in[k].ap().opt()], outs=[rs_out[k].ap().opt()]))

    barrier()
    pm3.close()
    pwo.close()

    pfc = contextlib.ExitStack()
    fbns = sbuf("fbns", [128, 24], F32, pfc)
    fmv = sbuf("fmv", [128, 4], F32, pfc)
    fcf = sbuf("fcf", [128, 4], F32, pfc)

    pf12 = contextlib.ExitStack()
    xqT = sbuf("xqT", [128, 16 * TOK], BF16, pf12)
    fqs = {
        "v": sbuf("f_v", [128, DM], F32, pf12),
        "r1": sbuf("f_r1", [128, DM], F32, pf12),
        "q": sbuf("f_q", [128, DM], F32, pf12),
        "f8": sbuf("f_f8", [128, DM], FP8, pf12),
        "m2": sbuf("f_m2", [128, DM], U8, pf12),
        "am": sbuf("f_am", [128, 64], F32, pf12),
        "ie": sbuf("f_ie", [128, 64], I32, pf12),
        "ir": sbuf("f_ir", [128, 64], I32, pf12),
        "is": sbuf("f_is", [128, 64], I32, pf12),
    }

    pf1 = contextlib.ExitStack()
    xt2 = sbuf("xt2", [128, DM], F32, pf1)
    at2 = sbuf("at2", [128, DM], F32, pf1)
    x2t = sbuf("x2t", [128, DM], F32, pf1)
    x3n = sbuf("x3n", [128, DM], F32, pf1)
    xq_bf = sbuf("xq_bf", [128, DM], BF16, pf1)

    if "rs0" in tap_bufs:
        GE.wait_ge(c_rs[0].h, 1)
        ST.inc(GE.dma_start(out=tap_bufs["rs0"], in_=rs_out[0].ap()), 16)
    for i in range(16):
        kchunk, half = i // 2, i % 2
        SY.wait_ge(c_rs[kchunk].h, 1)
        SY.wait_ge(V.h, V.n)
        SY.wait_ge(SC.h, SC.n)
        nL = L_.inc(SY.dma_start(out=xt2[:, :],
                                 in_=x_in.rearrange("(i p) f -> i p f", p=128)[i]), 16)
        nL = L_.inc(SY.dma_start(out=at2[:, :],
                                 in_=ln1_spill.rearrange("(i p) f -> i p f", p=128)[i]), 16)
        nL = L_.inc(SY.dma_start(out=x2t[:, :],
                                 in_=rs_out[kchunk].rearrange("(h p) f -> h p f", p=128)[half]),
                    16)
        VE.wait_ge(L_.h, nL)
        VE.wait_ge(ST.h, ST.n)
        VE.tensor_tensor(at2[:, :], at2[:, :], x2t[:, :], ALU.add)
        vw = ln_stats(at2, fbns, fmv)
        vw = ln_coefs(fmv, fcf, vw)
        VE.tensor_scalar(x2t[:, :], at2[:, :], fcf[:, 0:1], fcf[:, 1:2],
                         ALU.mult, ALU.add)
        VE.tensor_tensor(x2t[:, :], x2t[:, :], xt2[:, :], ALU.add)
        spacer()
        vx2 = V.inc(spacer())
        SE.wait_ge(V.h, vx2)
        ST.inc(SE.dma_start(out=x2_spill.rearrange("(i p) f -> i p f", p=128)[i],
                            in_=x2t[:, :]), 16)
        vw = ln_stats(x2t, fbns, fmv)
        vw = ln_coefs(fmv, fcf, vw)
        VE.tensor_scalar(x3n[:, :], x2t[:, :], fcf[:, 0:1], fcf[:, 1:2],
                         ALU.mult, ALU.add)
        VE.wait_ge(PE.h, PE.n)
        vq = quant_a(x3n, xq_bf[:, :], fqs, DM)
        if i == 0:
            tapw("xq0", xq_bf[:, :])
        TE.wait_ge(V.h, vq)
        for b in range(2):
            TE.wait_ge(SC.h, SC.n)
            for j in range(8):
                kf = 8 * b + j
                mmt = TE.transpose(pTB[:, 128 * j:128 * (j + 1)],
                                   xq_bf[:, 128 * kf:128 * (kf + 1)], ident_bf[:, :])
            pe = PE.inc(mmt)
            SE.wait_ge(PE.h, pe)
            SC.inc(SE.copy(
                xqT[:, :].rearrange("p (a t) -> p a t", t=2048)[
                    :, 8 * b:8 * b + 8, 128 * i:128 * (i + 1)],
                pTB[:, :].rearrange("p (a t) -> p a t", t=128)))

    tapw("xqTv", xqT[:, 0:2048])
    barrier()
    pf1.close()

    if "x2sp" in tap_bufs:
        GE.wait_ge(ST.h, ST.n)
        ST.inc(GE.dma_start(out=tap_bufs["x2sp"], in_=x2_spill.ap()), 16)
    pf2 = contextlib.ExitStack()
    wgt = [sbuf("wgtA", [128, 16 * 512], BF16, pf2),
           sbuf("wgtB", [128, 16 * 512], BF16, pf2)]
    wut = [sbuf("wutA", [128, 16 * 512], BF16, pf2),
           sbuf("wutB", [128, 16 * 512], BF16, pf2)]
    sg_sb = [sbuf("sg_sbA", [128, 512], F32, pf2),
             sbuf("sg_sbB", [128, 512], F32, pf2)]
    interB = [sbuf("interA", [128, 16 * 512], BF16, pf2),
              sbuf("interC", [128, 16 * 512], BF16, pf2)]
    iqev = [sbuf("iqevA", [128, 512], BF16, pf2), sbuf("iqevB", [128, 512], BF16, pf2)]

    SY.wait_ge(c_agwg.h, 1)
    SY.wait_ge(c_agwu.h, 1)
    sc_silu = [SC.n, SC.n]
    v_int = [V.n, V.n]

    def wgu_load(ic, wpar, pe_guard):
        r, icc = ic // 2, ic % 2
        SY.wait_ge(PE.h, pe_guard)
        n = L_.n
        for kf in range(16):
            n = L_.inc(SY.dma_start(
                out=wgt[wpar][:, 512 * kf:512 * (kf + 1)],
                in_=wgq_all.rearrange("(r a p) c -> r a p c", r=NC, p=128)[r][kf][
                    :, 512 * icc:512 * (icc + 1)]), 16)
            n = L_.inc(SY.dma_start(
                out=wut[wpar][:, 512 * kf:512 * (kf + 1)],
                in_=wuq_all.rearrange("(r a p) c -> r a p c", r=NC, p=128)[r][kf][
                    :, 512 * icc:512 * (icc + 1)]), 16)
        return n

    wgu_ready = [wgu_load(0, 0, 0), wgu_load(1, 1, 0)]
    pe_ic = [0, 0]
    st_iq = [ST.n, ST.n]

    def emit_tr(tic, vq_t):
        rr, ticc = tic // 2, tic % 2
        gI0 = rr * 1024 + ticc * 512
        TE.wait_ge(V.h, vq_t)
        for mt in range(16):
            par = mt % 2
            SE.wait_ge(ST.h, st_iq[par])
            TE.wait_ge(SC.h, SC.n)
            for sub in range(4):
                mmt = TE.transpose(
                    pTB[:, 128 * sub:128 * (sub + 1)],
                    interB[tic % 2][:, 512 * mt + 128 * sub:
                                    512 * mt + 128 * (sub + 1)],
                    ident_bf[:, :])
            pe = PE.inc(mmt)
            SE.wait_ge(PE.h, pe)
            SC.inc(SE.copy(iqev[par][:, :], pTB[:, 0:512]))
            SE.copy(junkB[:, 0:4], junkA[:, 0:4])
            st_iq[par] = ST.inc(SE.dma_start(
                out=ikt_spill.rearrange("(a p) t -> a p t", p=128)[
                    gI0 // 128:gI0 // 128 + 4][:, :, 128 * mt:128 * (mt + 1)]
                .transpose([1, 0, 2]),
                in_=iqev[par][:, :]), 16)

    vq_prev = None
    for ic in range(16):
        wpar = ic % 2
        inter = interB[ic % 2]
        VE.wait_ge(PE.h, PE.n)
        nL = wgu_ready[wpar]
        TE.wait_ge(L_.h, nL)
        for mt in range(16):
            fpar = mt % 2
            pg, pu = pA[fpar], pB[fpar]
            TE.wait_ge(SC.h, sc_silu[fpar])
            TE.wait_ge(V.h, v_int[fpar])
            for kf in range(16):
                TE.matmul(pg[:, :],
                          xqT[:, 2048 * kf + 128 * mt:2048 * kf + 128 * (mt + 1)],
                          wgt[wpar][:, 512 * kf:512 * (kf + 1)],
                          start=(kf == 0), stop=(kf == 15))
            for kf in range(16):
                mmu = TE.matmul(pu[:, :],
                                xqT[:, 2048 * kf + 128 * mt:2048 * kf + 128 * (mt + 1)],
                                wut[wpar][:, 512 * kf:512 * (kf + 1)],
                                start=(kf == 0), stop=(kf == 15))
            pe = PE.inc(mmu)
            SE.wait_ge(PE.h, pe)
            scS = SC.inc(SE.activation(sg_sb[fpar][:, :], pg[:, :], AF.Silu))
            sc_silu[fpar] = scS
            VE.wait_ge(SC.h, scS)
            v_int[fpar] = V.inc(VE.tensor_tensor(
                inter[:, 512 * mt:512 * (mt + 1)], sg_sb[fpar][:, :],
                pu[:, :], ALU.mult))
        if ic == 0:
            tapw("wgt0", wgt[0][:, :])
            tq = tapw("intpre", inter[:, :])
            if tq:
                VE.wait_ge(ST.h, tq)
        pe_ic[wpar] = PE.n
        if ic + 2 < 16:
            wgu_ready[wpar] = wgu_load(ic + 2, wpar, pe_ic[wpar])
        for sub in range(4):
            vq = quant_a(inter[:, 2048 * sub:2048 * (sub + 1)],
                         inter[:, 2048 * sub:2048 * (sub + 1)], fqs, 2048)
        if ic == 0:
            tapw("iq0", inter[:, :])
        if vq_prev is not None:
            emit_tr(ic - 1, vq_prev)
        vq_prev = vq
    emit_tr(15, vq_prev)

    barrier()
    pf2.close()
    pf12.close()

    pf3 = contextlib.ExitStack()
    ikt = [sbuf("iktA", [128, 64 * 128], BF16, pf3),
           sbuf("iktB", [128, 64 * 128], BF16, pf3)]
    wdt_s = [sbuf("wdt_sA", [128, 64 * 512], BF16, pf3),
             sbuf("wdt_sB", [128, 64 * 512], BF16, pf3)]
    dstage = [sbuf("dstgA", [128, 512], F32, pf3),
              sbuf("dstgB", [128, 512], F32, pf3)]
    dout_f = sbuf("dout_f", [128, DM], F32, pf3)
    ln4t = sbuf("ln4t", [128, DM], F32, pf3)
    xt3 = sbuf("xt3", [128, DM], F32, pf3)
    ot3 = sbuf("ot3a", [128, DM], F32, pf3)

    SY.wait_ge(c_agwd.h, 1)

    def wd_load(fb2, par, pe_guard):
        SY.wait_ge(PE.h, pe_guard)
        nW = L_.n
        for half in range(2):
            fb = 2 * fb2 + half
            for kg in range(16):
                nW = L_.inc(SY.dma_start(
                    out=wdt_s[par][:, 2048 * kg:2048 * (kg + 1)].rearrange(
                        "p (kk c) -> p kk c", c=512)[:, :, 256 * half:256 * (half + 1)],
                    in_=wdq_all.rearrange("(r kk p) c -> r p kk c", r=NC, p=128)[fb][
                        :, 4 * kg:4 * (kg + 1), :]), 16)
        return nW

    def ikt_load(mt, par, pe_guard):
        SY.wait_ge(PE.h, pe_guard)
        nL = L_.n
        for kg in range(16):
            nL = L_.inc(SY.dma_start(
                out=ikt[par][:, 512 * kg:512 * (kg + 1)].rearrange(
                    "p (kk t) -> p kk t", t=128),
                in_=ikt_spill.rearrange("(kk p) t -> p kk t", p=128)[
                    :, 4 * kg:4 * (kg + 1), 128 * mt:128 * (mt + 1)]), 16)
        return nL

    wd_ready = [wd_load(0, 0, 0), wd_load(1, 1, 0)]
    ikt_ready = [ikt_load(0, 0, 0), ikt_load(1, 1, 0)]
    st_do = [ST.n, ST.n]
    st_out = ST.n
    v_evac = [V.n, V.n]
    for fb2 in range(4):
        wpar = fb2 % 2
        if fb2 == 3:
            SY.wait_ge(ST.h, ST.n)
        for mt in range(16):
            ipar = mt % 2
            if fb2 == 3:
                SY.wait_ge(V.h, V.n)
                L_.inc(SY.dma_start(
                    out=dout_f[:, 0:1536],
                    in_=dout_spill.rearrange("(i p) f -> i p f", p=128)[mt][
                        :, 0:1536]), 16)
                nL4 = L_.inc(SY.dma_start(
                    out=xt3[:, :],
                    in_=x2_spill.rearrange("(i p) f -> i p f", p=128)[mt]), 16)
            TE.wait_ge(L_.h, max(ikt_ready[ipar], wd_ready[wpar]))
            pb = pB[mt % 2]
            TE.wait_ge(SC.h, SC.n)
            TE.wait_ge(V.h, v_evac[mt % 2] if fb2 == 3 else V.n)
            for kk in range(64):
                mmd = TE.matmul(pb[:, :], ikt[ipar][:, 128 * kk:128 * (kk + 1)],
                                wdt_s[wpar][:, 512 * kk:512 * (kk + 1)],
                                start=(kk == 0), stop=(kk == 63))
            mjunk = TE.matmul(pR[:, 0:128], onesbf[:, :], onesbf[:, :],
                              start=True, stop=True)
            pe = PE.inc(mjunk)
            nxt = mt + 2
            if nxt < 16:
                ikt_ready[ipar] = ikt_load(nxt, ipar, pe)
            elif fb2 < 3:
                ikt_ready[ipar] = ikt_load(nxt - 16, ipar, pe)
            if mt == 0 and fb2 in (1, 2):
                wd_ready[(fb2 + 1) % 2] = wd_load(fb2 + 1, (fb2 + 1) % 2, pe)
            VE.wait_ge(PE.h, pe)
            if fb2 < 3:
                VE.wait_ge(ST.h, st_do[ipar])
                spacer()
                v_evac[mt % 2] = V.inc(
                    VE.tensor_copy(dstage[ipar][:, :], pb[:, :]))
                spacer()
                spacer()
                vo = V.inc(spacer())
                SE.wait_ge(V.h, vo)
                st_do[ipar] = ST.inc(SE.dma_start(
                    out=dout_spill.rearrange("(i p) f -> i p f", p=128)[mt][
                        :, 512 * fb2:512 * (fb2 + 1)],
                    in_=dstage[ipar][:, :]), 16)
            else:
                VE.wait_ge(L_.h, nL4)
                spacer()
                v_evac[mt % 2] = V.inc(
                    VE.tensor_copy(dout_f[:, 1536:2048], pb[:, :]))
                spacer()
                vw = ln_stats(dout_f, fbns, fmv)
                vw = ln_coefs(fmv, fcf, vw)
                VE.tensor_scalar(ln4t[:, :], dout_f[:, :], fcf[:, 0:1],
                                 fcf[:, 1:2], ALU.mult, ALU.add)
                VE.wait_ge(ST.h, st_out)
                VE.tensor_tensor(ot3[:, :], xt3[:, :], ln4t[:, :], ALU.add)
                spacer()
                spacer()
                vo = V.inc(spacer())
                SE.wait_ge(V.h, vo)
                st_out = ST.inc(SE.dma_start(
                    out=out_own.rearrange("(i p) f -> i p f", p=128)[mt],
                    in_=ot3[:, :]), 16)

    SY.wait_ge(ST.h, ST.n)
    GE.wait_ge(ST.h, ST.n)

    pf3.close()
    pfc.close()
    es.close()
    return nc


def _slice_inputs(inputs, c):
    x = np.asarray(inputs["L0_attn_x"]).reshape(NT, DM)
    stripes = [x[(k * NC + c) * STR:(k * NC + c) * STR + STR] for k in range(NCHUNK)]
    x_own = np.ascontiguousarray(np.concatenate(stripes, axis=0))
    W_in = np.asarray(inputs["L0_attn_W_in"])
    w_in_sl = np.ascontiguousarray(np.concatenate(
        [W_in[:, c * DOWN:(c + 1) * DOWN], W_in[:, DI + c * DOWN:DI + (c + 1) * DOWN]],
        axis=1))
    cw = np.asarray(inputs["L0_attn_conv_w"])[c * DOWN:(c + 1) * DOWN, 0, :]
    convw = np.ascontiguousarray(cw.reshape(4, 128, 4).transpose(1, 0, 2).reshape(128, 16))
    convb = np.ascontiguousarray(
        np.asarray(inputs["L0_attn_conv_b"])[c * DOWN:(c + 1) * DOWN].reshape(4, 128).T)
    wx_sl = np.ascontiguousarray(np.asarray(inputs["L0_attn_W_x"])[c * DOWN:(c + 1) * DOWN])
    wdt_sl = np.ascontiguousarray(
        np.asarray(inputs["L0_attn_W_dt"])[:, c * DOWN:(c + 1) * DOWN])
    dtb = np.ascontiguousarray(
        np.asarray(inputs["L0_attn_dt_bias"])[c * DOWN:(c + 1) * DOWN].reshape(4, 128).T)
    alog = np.ascontiguousarray(
        np.asarray(inputs["L0_attn_A_log"])[c * DOWN:(c + 1) * DOWN].reshape(4, 128, 16)
        .transpose(1, 0, 2).reshape(128, 64))
    dp = np.ascontiguousarray(
        np.asarray(inputs["L0_attn_Dp"])[c * DOWN:(c + 1) * DOWN].reshape(4, 128).T)
    wout_sl = np.ascontiguousarray(np.asarray(inputs["L0_attn_W_out"])[c * DOWN:(c + 1) * DOWN])
    wg_sl = np.ascontiguousarray(np.asarray(inputs["L0_ffn_w_gate"])[c * 1024:(c + 1) * 1024])
    wu_sl = np.ascontiguousarray(np.asarray(inputs["L0_ffn_w_up"])[c * 1024:(c + 1) * 1024])
    wd_sl = np.ascontiguousarray(np.asarray(inputs["L0_ffn_w_down"])[c * 256:(c + 1) * 256])
    return dict(
        x_own=x_own.astype(np.float32),
        ident=np.eye(128, dtype=np.float32),
        rmsw=np.asarray(inputs["L0_attn_rms_w"]).reshape(1, DM).astype(np.float32),
        w_in_sl=w_in_sl.astype(np.float32), convw=convw.astype(np.float32),
        convb=convb.astype(np.float32), wx_sl=wx_sl.astype(np.float32),
        wdt_sl=wdt_sl.astype(np.float32), dtb=dtb.astype(np.float32),
        alog=alog.astype(np.float32), dp=dp.astype(np.float32),
        wout_sl=wout_sl.astype(np.float32), wg_sl=wg_sl.astype(np.float32),
        wu_sl=wu_sl.astype(np.float32), wd_sl=wd_sl.astype(np.float32),
    )


def kernel(**inputs):
    global _LAST_TAPS
    nc = build(taps=_DEBUG_TAPS)
    in_maps = [_slice_inputs(inputs, c) for c in range(NC)]
    res = run_bass_kernel_spmd(nc, in_maps, core_ids=list(range(NC)))
    _LAST_TAPS = {}
    for tname, _, _ in _DEBUG_TAPS:
        _LAST_TAPS[tname] = [r["tap_" + tname] for r in res.results]
    out = np.zeros((NT, DM), dtype=np.float32)
    for c in range(NC):
        o = res.results[c]["out_own"]
        for k in range(NCHUNK):
            out[(k * NC + c) * STR:(k * NC + c) * STR + STR] = o[k * STR:(k + 1) * STR]
    return out.reshape(B, L, DM)
